# version 08
# baseline (speedup 1.0000x reference)
import numpy as np
import ml_dtypes
import contextlib
import concourse.bass as bass
import concourse.mybir as mybir
from concourse.bass_utils import run_bass_kernel_spmd

F32 = mybir.dt.float32
BF16 = mybir.dt.bfloat16
ACT = mybir.ActivationFunctionType
ALU = mybir.AluOpType
AX = mybir.AxisListType

T, D = 2048, 1024
NTILE, TT, C, NCH = 4, 512, 128, 4
NH, dk, dv, KC = 4, 128, 256, 8
H, HD, E, I = 8, 128, 16, 512
EPS = 1e-5
LNSCALE = float(np.log(128.0 ** -0.5))
ENGS = ("tensor", "vector", "scalar", "gpsimd", "sync")


class Sem:
    __slots__ = ("h", "n")

    def __init__(self, h):
        self.h = h
        self.n = 0


import os
SERIAL = bool(int(os.environ.get("KERNEL_SERIAL", "0")))


class Prog:
    def __init__(self, nc, es):
        self.nc = nc
        self.es = es
        self.ops = {e: [] for e in ENGS}
        self.esem = {}

    def sem(self, name):
        return Sem(self.es.enter_context(self.nc.semaphore(name)))

    def wait(self, eng, sem, val=None):
        v = sem.n if val is None else val
        assert v <= sem.n, f"wait on future increment ({v} > {sem.n})"
        if v <= 0 or SERIAL:
            return
        self.ops[eng].append(lambda e: e.wait_ge(sem.h, v))

    def op(self, eng, meth, *a, inc=(), **kw):
        inc = list(inc)
        if SERIAL:
            for s, k in inc:
                s.n += k
            for oe, os_ in self.esem.items():
                if os_.n > 0:
                    v = os_.n
                    self.ops[eng].append(
                        (lambda sem, vv: lambda e: e.wait_ge(sem.h, vv))(os_, v))
            own = self.esem[eng]
            inc = [(own, 16 if meth == "dma_start" else 1)]
        for s, k in inc:
            s.n += k

        def f(e):
            inst = getattr(e, meth)(*a, **kw)
            for s, k in inc:
                inst.then_inc(s.h, k)
        self.ops[eng].append(f)

    def run(self, block):
        for eng in ENGS:
            oplist = self.ops[eng]

            def make(lst):
                def body(e):
                    for f in lst:
                        f(e)
                return body
            getattr(block, eng)(make(oplist))


def build_nc(taps=()):
    nc = bass.Bass("TRN2", target_bir_lowering=False)
    dr = lambda n, s, d=F32, k="ExternalInput": nc.dram_tensor(n, s, d, kind=k).ap()
    x_d = dr("x", [T, D])
    wq_d = dr("wq", [D, 512])
    wk_d = dr("wk", [D, 512])
    wv_d = dr("wv", [D, D])
    wg_d = dr("wg", [D, D])
    wo_d = dr("wo", [D, D])
    wm_d = dr("wm", [D, D])
    wo2_d = dr("wo2", [D, D])
    wgk1_d = dr("wgk1", [D, 16])
    wgk2_d = dr("wgk2", [16, 512])
    w1_d = dr("w1", [E, HD, I])
    w2_d = dr("w2", [E, I, HD])
    emb_d = dr("emb", [HD, E])
    b2_d = dr("b2", [E, HD])
    bgk2l_d = dr("bgk2l", [128, 4])
    bml_d = dr("bml", [128, 8])
    b1l_d = dr("b1l", [128, E, 4])
    bob_d = dr("bob", [128, D])
    gwb_d = dr("gwb", [128, dv])
    ident_d = dr("ident", [128, 128], BF16)
    identf_d = dr("identf", [128, 128])
    trimask_d = dr("trimask", [128, 128], BF16)
    out_d = dr("out", [T, D], F32, "ExternalOutput")
    tap_shapes = {"q": ([128, 4, TT], BF16), "k": ([128, 4, TT], BF16),
                  "gksp": ([128, 4, TT], F32), "v": ([128, NCH, D], BF16),
                  "g": ([128, NCH, D], BF16), "h": ([128, H, TT], BF16),
                  "wu": ([128, NCH, H, 32], F32), "rsum": ([128, NCH, H], F32),
                  "S": ([128, NH, dv], F32), "orms": ([128, NCH, D], BF16),
                  "resid": ([128, NCH, D], BF16), "acc": ([128, NCH, D], F32),
                  "xnt": ([128, KC, TT], BF16), "sums": ([128, NCH, H], F32),
                  "lgd": ([128, NCH, 128], F32), "smc": ([128, 128], F32),
                  "xs": ([128, NCH, D], F32), "lnst": ([128, NCH, 8], F32)}
    tap_d = {tp: dr("tap_" + tp, *tap_shapes[tp], k="ExternalOutput") for tp in taps}

    es = contextlib.ExitStack()
    sb = lambda n, s, d=F32: es.enter_context(nc.sbuf_tensor(n, s, d))

    wq_s = sb("wq_s", [128, KC, 512], BF16)
    wk_s = sb("wk_s", [128, KC, 512], BF16)
    wgk1_s = sb("wgk1_s", [128, KC, 16], BF16)
    wgk2_s = sb("wgk2_s", [16, 4, 128], BF16)
    emb_s = sb("emb_s", [128, E], BF16)
    b2_s = sb("b2_s", [16, 128], BF16)
    b1l_s = sb("b1l_s", [128, E, 4])
    bml_s = sb("bml_s", [128, 8])
    bgk2r_s = sb("bgk2r_s", [128, 4])
    bgk2n_s = sb("bgk2n_s", [128, 4])
    bob_s = sb("bob_s", [128, D])
    gwb_s = sb("gwb_s", [128, dv])
    ident_s = sb("ident_s", [128, 128], BF16)
    identf_s = sb("identf_s", [128, 128])
    trimask_s = sb("trimask_s", [128, 128], BF16)
    ones_s = sb("ones_s", [128, 128])
    ones_s2 = sb("ones_s2", [128, 512])
    consts_s = sb("consts_s", [128, 3])
    wbigA = sb("wbigA", [128, KC, D], BF16)
    wbigB = sb("wbigB", [128, KC, D], BF16)
    w1b = sb("w1b", [128, 2, I], BF16)
    w2b = sb("w2b", [128, 2, 4, HD], BF16)
    x_s = sb("x_s", [128, NCH, D])
    xn_tm = sb("xn_tm", [128, D], BF16)
    xnT = sb("xnT", [128, KC, TT], BF16)
    qT = sb("qT", [128, NH, TT], BF16)
    kT = sb("kT", [128, NH, TT], BF16)
    gksp = sb("gksp", [128, NCH, NH, 128])
    rstd_s = sb("rstd_s", [128, 16])
    v_s = sb("v_s", [128, NCH, D], BF16)
    g_s = sb("g_s", [128, NCH, D], BF16)
    hT = sb("hT", [128, H, TT], BF16)
    gk1b = sb("gk1b", [16, TT], BF16)
    e1_s = sb("e1_s", [128, 512])
    sm_a = sb("sm_a", [128, 128])
    sm_b = sb("sm_b", [128, 128])
    sm_c = sb("sm_c", [128, 128])
    wu_s = sb("wu_s", [128, NCH, H, 32])
    wuT_s = sb("wuT_s", [16, H, 128], BF16)
    rsum_s = sb("rsum_s", [128, NCH, H])
    sums_s = sb("sums_s", [128, NCH, H])
    stats = sb("stats", [128, 32])
    S_s = sb("S_s", [128, NH, dv])
    S_bf = sb("S_bf", [128, NH, dv], BF16)
    A4 = sb("A4", [128, NH, 128])
    lam4 = sb("lam4", [128, NH, 128])
    kh4 = sb("kh4", [128, NH, 128], BF16)
    Pb4 = sb("Pb4", [128, NH, 128], BF16)
    khT_s = sb("khT_s", [128, NCH, NH, 128], BF16)
    o_rms = sb("o_rms", [128, NCH, D], BF16)
    resid = sb("resid", [128, NCH, D], BF16)
    acc = sb("acc", [128, NCH, D])
    comb_bf = v_s
    hmidS = sb("hmidS", [128, 4, H, 256], BF16)
    outst = sb("outst", [128, 2, 512])

    ph = [es.enter_context(nc.psum_tensor(f"ph{i}", [128, 1024], F32)) for i in range(2)]
    pw = [es.enter_context(nc.psum_tensor(f"pw{i}", [128, 512], F32)) for i in range(3)]
    ptp = es.enter_context(nc.psum_tensor("ptp", [128, 1024], BF16))

    p = Prog(nc, es)
    s_dx = p.sem("s_dx")
    s_di = p.sem("s_di")
    s_winit = p.sem("s_winit")
    s_wA = p.sem("s_wA")
    s_wB = p.sem("s_wB")
    s_w12 = [p.sem("s_w12a"), p.sem("s_w12b")]
    s_doA = p.sem("s_doA")
    s_doB = p.sem("s_doB")
    s_dt = p.sem("s_dt")
    s_pe = p.sem("s_pe")
    s_ve = p.sem("s_ve")
    s_se = p.sem("s_se")
    if SERIAL:
        p.esem = {e: p.sem("ser_" + e) for e in ENGS}

    guards = {}

    def guard_wait(eng, key):
        g = guards.get(key)
        if g is not None:
            p.wait(eng, g[0], g[1])

    def guard_set(key, sem):
        guards[key] = (sem, sem.n)

    for dst, src in [(b1l_s, b1l_d), (bml_s, bml_d), (bgk2r_s, bgk2l_d),
                     (bob_s, bob_d), (gwb_s, gwb_d), (ident_s, ident_d),
                     (identf_s, identf_d), (trimask_s, trimask_d)]:
        p.op("sync", "dma_start", out=dst[:], in_=src, inc=[(s_di, 16)])
    p.op("gpsimd", "dma_start", out=wq_s[:],
         in_=wq_d.rearrange("(kk p) n -> p kk n", p=128), inc=[(s_winit, 16)])
    p.op("gpsimd", "dma_start", out=wk_s[:],
         in_=wk_d.rearrange("(kk p) n -> p kk n", p=128), inc=[(s_winit, 16)])
    p.op("gpsimd", "dma_start", out=wgk1_s[:],
         in_=wgk1_d.rearrange("(kk p) n -> p kk n", p=128), inc=[(s_winit, 16)])
    p.op("gpsimd", "dma_start", out=wgk2_s[:],
         in_=wgk2_d.rearrange("k (m n) -> k m n", m=4), inc=[(s_winit, 16)])
    p.op("gpsimd", "dma_start", out=emb_s[:], in_=emb_d, inc=[(s_winit, 16)])
    p.op("gpsimd", "dma_start", out=b2_s[:], in_=b2_d, inc=[(s_winit, 16)])
    p.wait("vector", s_di)
    p.wait("tensor", s_di)
    p.wait("tensor", s_winit)
    p.wait("scalar", s_di)
    p.op("vector", "memset", wu_s[:], 0.0)
    p.op("vector", "memset", S_s[:], 0.0)
    p.op("vector", "memset", S_bf[:], 0.0)
    p.op("vector", "memset", ones_s[:], 1.0)
    p.op("vector", "memset", ones_s2[:], 1.0)
    p.op("vector", "memset", consts_s[:, 0:1], EPS)
    p.op("vector", "memset", consts_s[:, 1:2], 1.0)
    p.op("vector", "memset", consts_s[:, 2:3], LNSCALE)
    p.op("vector", "memset", stats[:, 8:9], 0.0)
    p.op("vector", "tensor_scalar_mul", out=bgk2n_s[:], in0=bgk2r_s[:],
         scalar1=-1.0, inc=[(s_ve, 1)])

    wsem = {"wbigA": s_wA, "wbigB": s_wB}

    def wload(dst_s, src_d, key):
        guard_wait("gpsimd", key)
        sem = wsem[key]
        p.op("gpsimd", "dma_start", out=dst_s[:],
             in_=src_d.rearrange("(kk p) n -> p kk n", p=128), inc=[(sem, 16)])
        return (sem, sem.n)

    def w12load(e):
        b = e % 2
        guard_wait("gpsimd", f"w1b{b}")
        p.op("gpsimd", "dma_start", out=w1b[:, b, :], in_=w1_d[e], inc=[(s_w12[b], 16)])
        p.op("gpsimd", "dma_start", out=w2b[:, b, :, :],
             in_=w2_d[e].rearrange("(i p) n -> p i n", p=128), inc=[(s_w12[b], 16)])

    def transpose_slab(src2d, dstT, c, nblk=8):
        guard_wait("tensor", "ptp")
        for i in range(nblk):
            p.op("tensor", "transpose", ptp[:, i * 128:(i + 1) * 128],
                 src2d[:, i * 128:(i + 1) * 128], ident_s[:],
                 inc=[(s_pe, 1)] if i == nblk - 1 else ())
        p.wait("vector", s_pe)
        p.op("vector", "tensor_copy",
             out=dstT[:, 0:nblk, c * 128:(c + 1) * 128],
             in_=ptp.rearrange("p (i n) -> p i n", i=8)[:, 0:nblk, :],
             inc=[(s_ve, 1)])
        guard_set("ptp", s_ve)

    sbf_se = {h: 0 for h in range(NH)}
    gate_ve = {}
    gate_se = {}
    gbank = [0]

    def xload(t):
        for c in range(NCH):
            guard_wait("sync", f"x_s{c}")
            p.op("sync", "dma_start", out=x_s[:, c, :],
                 in_=x_d[t * TT + c * 128: t * TT + (c + 1) * 128, :],
                 inc=[(s_dx, 16)])
        return s_dx.n

    xval = xload(0)

    for t in range(NTILE):
        wv_ready = wload(wbigA, wv_d, "wbigA")
        wg_ready = wload(wbigB, wg_d, "wbigB")
        for c in range(NCH):
            p.wait("scalar", s_dx, xval)
            p.wait("scalar", s_ve)
            p.op("scalar", "activation", out=xn_tm[:], in_=x_s[:, c, :],
                 func=ACT.Square, accum_out=stats[:, 0:1], inc=[(s_se, 1)])
            p.wait("vector", s_dx, xval)
            p.op("vector", "reduce_sum", out=stats[:, 1:2], in_=x_s[:, c, :], axis=AX.X,
                 inc=[(s_ve, 1)])
            p.wait("vector", s_ve)
            p.op("vector", "tensor_scalar_mul", out=stats[:, 2:3], in0=stats[:, 1:2],
                 scalar1=1.0 / D, inc=[(s_ve, 1)])
            p.wait("vector", s_ve)
            p.op("vector", "tensor_mul", out=stats[:, 3:4], in0=stats[:, 2:3],
                 in1=stats[:, 2:3], inc=[(s_ve, 1)])
            p.wait("vector", s_ve)
            p.wait("vector", s_se)
            p.op("vector", "tensor_scalar", out=stats[:, 4:5], in0=stats[:, 0:1],
                 scalar1=1.0 / D, scalar2=stats[:, 3:4], op0=ALU.mult,
                 op1=ALU.subtract, inc=[(s_ve, 1)])
            p.wait("scalar", s_ve)
            p.op("scalar", "activation", out=stats[:, 5:6], in_=stats[:, 4:5],
                 func=ACT.Ln, bias=consts_s[:, 0:1], inc=[(s_se, 1)])
            p.wait("scalar", s_se)
            p.op("scalar", "activation", out=stats[:, 6:7], in_=stats[:, 5:6],
                 func=ACT.Exp, scale=-0.5, inc=[(s_se, 1)])
            p.wait("vector", s_se)
            p.wait("vector", s_pe)
            p.op("vector", "tensor_scalar", out=xn_tm[:], in0=x_s[:, c, :],
                 scalar1=stats[:, 2:3], scalar2=stats[:, 6:7],
                 op0=ALU.subtract, op1=ALU.mult, inc=[(s_ve, 1)])
            p.wait("tensor", s_ve)
            transpose_slab(xn_tm, xnT, c)
        xn_done = s_ve.n

        def proj_fm(wsb, dstT, mrange, bias_l=None, evac_eng="vector"):
            for mi, m in enumerate(mrange):
                bk = mi % 2
                bank = pw[bk]
                guard_wait("tensor", f"pw{bk}")
                for kk in range(KC):
                    p.op("tensor", "matmul", bank[:],
                         wsb[:, kk, m * 128:(m + 1) * 128], xnT[:, kk, :],
                         start=(kk == 0), stop=(kk == KC - 1),
                         inc=[(s_pe, 1)] if kk == KC - 1 else ())
                if evac_eng == "vector":
                    p.wait("vector", s_pe)
                    p.op("vector", "tensor_copy", out=dstT[:, m, :], in_=bank[:],
                         inc=[(s_ve, 1)])
                    guard_set(f"pw{bk}", s_ve)
                else:
                    p.wait("scalar", s_pe)
                    p.op("scalar", "activation", out=dstT[:, m, :], in_=bank[:],
                         func=ACT.Identity, bias=bias_l[:, m:m + 1], inc=[(s_se, 1)])
                    guard_set(f"pw{bk}", s_se)

        p.wait("tensor", s_ve, xn_done)
        proj_fm(wq_s, qT, range(NH))
        proj_fm(wk_s, kT, range(NH))

        guard_wait("tensor", "pw2")
        for kk in range(KC):
            p.op("tensor", "matmul", pw[2][0:16, 0:TT], wgk1_s[:, kk, :],
                 xnT[:, kk, :], start=(kk == 0), stop=(kk == KC - 1),
                 inc=[(s_pe, 1)] if kk == KC - 1 else ())
        p.wait("vector", s_pe)
        p.op("vector", "tensor_copy", out=gk1b[:], in_=pw[2][0:16, 0:TT],
             inc=[(s_ve, 1)])
        guard_set("pw2", s_ve)
        gk1_ve = s_ve.n
        for m in range(4):
            bk = m % 2
            bank = pw[bk]
            guard_wait("tensor", f"pw{bk}")
            p.wait("tensor", s_ve, gk1_ve)
            p.op("tensor", "matmul", bank[:], wgk2_s[:, m, :], gk1b[:],
                 start=True, stop=True, inc=[(s_pe, 1)])
            p.wait("scalar", s_pe)
            p.op("scalar", "activation", out=e1_s[:], in_=bank[:],
                 func=ACT.Exp, scale=-1.0, bias=bgk2n_s[:, m:m + 1], inc=[(s_se, 1)])
            p.wait("scalar", s_se)
            p.op("scalar", "activation", out=gksp[:, :, m, :], in_=e1_s[:],
                 func=ACT.Ln, bias=consts_s[:, 1:2], inc=[(s_se, 1)])
            guard_set(f"pw{bk}", s_se)

        def proj_tm(wsb, dst, wready):
            p.wait("tensor", wready[0], wready[1])
            for ci in range(NCH):
                for half in range(2):
                    bk = (ci * 2 + half) % 2
                    bank = pw[bk]
                    guard_wait("tensor", f"pw{bk}")
                    for kk in range(KC):
                        p.op("tensor", "matmul", bank[:],
                             xnT[:, kk, ci * 128:(ci + 1) * 128],
                             wsb[:, kk, half * 512:(half + 1) * 512],
                             start=(kk == 0), stop=(kk == KC - 1),
                             inc=[(s_pe, 1)] if kk == KC - 1 else ())
                    p.wait("scalar", s_pe)
                    p.op("scalar", "copy",
                         out=dst[:, ci, half * 512:(half + 1) * 512],
                         in_=bank[:], inc=[(s_se, 1)])
                    guard_set(f"pw{bk}", s_se)

        gk_done_se = s_se.n
        for c in range(NCH):
            cs = slice(c * 128, (c + 1) * 128)
            A2 = A4.rearrange("p h n -> p (h n)")
            p.wait("vector", s_se, gk_done_se)
            p.op("vector", "tensor_tensor_scan", out=A2, data0=ones_s2[:],
                 data1=gksp[:, c].rearrange("p h n -> p (h n)"),
                 initial=0.0, op0=ALU.mult, op1=ALU.add, inc=[(s_ve, 1)])
            p.wait("vector", s_ve)
            p.op("vector", "tensor_copy", out=stats[:, 9:12],
                 in_=A4[:, 0:3, 127], inc=[(s_ve, 1)])
            p.wait("vector", s_ve)
            pcb = stats[:, 8:12].rearrange("p (h o) -> p h o", h=NH).broadcast_to([128, NH, 128])
            p.op("vector", "tensor_tensor", out=A4[:], in0=A4[:], in1=pcb,
                 op=ALU.subtract, inc=[(s_ve, 1)])
            A_ve = s_ve.n
            p.wait("scalar", s_ve, A_ve)
            p.op("scalar", "activation", out=stats[:, 12:16], in_=A4[:, :, 127],
                 func=ACT.Exp, scale=-1.0 / 16.0, inc=[(s_se, 1)])
            p.op("scalar", "activation", out=lam4.rearrange("p h n -> p (h n)"),
                 in_=A2, func=ACT.Exp, scale=1.0 / 16.0, inc=[(s_se, 1)])
            p.wait("vector", s_se)
            p.op("vector", "tensor_mul", out=kT[:, :, cs], in0=kT[:, :, cs],
                 in1=lam4[:], inc=[(s_ve, 1)])
            kt_ve = s_ve.n
            p.wait("vector", s_ve)
            dsb = stats[:, 12:16].rearrange("p (h o) -> p h o", h=NH).broadcast_to([128, NH, 128])
            p.op("vector", "tensor_tensor", out=kh4[:], in0=kT[:, :, cs], in1=dsb,
                 op=ALU.mult, inc=[(s_ve, 1)])
            kh_ve = s_ve.n
            p.wait("scalar", s_ve, kt_ve)
            p.op("scalar", "activation", out=lam4.rearrange("p h n -> p (h n)"),
                 in_=A2, func=ACT.Exp, scale=-1.0 / 16.0, bias=consts_s[:, 2:3],
                 inc=[(s_se, 1)])
            p.wait("vector", s_se)
            p.op("vector", "tensor_mul", out=qT[:, :, cs], in0=qT[:, :, cs],
                 in1=lam4[:], inc=[(s_ve, 1)])
            gate_ve[c] = s_ve.n
            gate_se[c] = s_se.n
            for h in range(NH):
                guard_wait("tensor", "ptp")
                p.wait("tensor", s_ve, kh_ve)
                p.op("tensor", "transpose", ptp[:, 0:128], kh4[:, h, :], ident_s[:],
                     inc=[(s_pe, 1)])
                p.wait("vector", s_pe)
                p.op("vector", "tensor_copy", out=khT_s[:, c, h, :],
                     in_=ptp[:, 0:128], inc=[(s_ve, 1)])
                guard_set("ptp", s_ve)


        proj_tm(wbigA, v_s, wv_ready)
        guards["wbigA"] = (s_pe, s_pe.n)
        wm_ready = wload(wbigA, wm_d, "wbigA")
        proj_tm(wbigB, g_s, wg_ready)
        guards["wbigB"] = (s_pe, s_pe.n)
        wo_ready = wload(wbigB, wo_d, "wbigB")
        p.wait("tensor", wm_ready[0], wm_ready[1])
        proj_fm(wbigA, hT, range(H), bias_l=bml_s, evac_eng="scalar")
        guards["wbigA"] = (s_pe, s_pe.n)

        h_done_se = s_se.n
        for c in range(NCH):
            guard_wait("tensor", "pw2")
            p.wait("tensor", s_se, h_done_se)
            for j in range(H):
                p.op("tensor", "matmul", pw[2][:, j * 16:(j + 1) * 16],
                     hT[:, j, c * 128:(c + 1) * 128], emb_s[:],
                     start=True, stop=True,
                     inc=[(s_pe, 1)] if j == H - 1 else ())
            lg = pw[2][:, 0:128].rearrange("p (j e) -> p j e", j=H)
            p.wait("vector", s_pe)
            p.op("vector", "tensor_reduce", out=sm_a[:, 0:8], in_=lg, axis=AX.X,
                 op=ALU.max, negate=True, inc=[(s_ve, 1)])
            p.wait("vector", s_ve)
            nmx = sm_a[:, 0:8].rearrange("p (j o) -> p j o", j=H).broadcast_to([128, H, 16])
            p.wait("vector", s_se)
            p.op("vector", "tensor_tensor",
                 out=sm_b[:].rearrange("p (j e) -> p j e", j=H), in0=lg, in1=nmx,
                 op=ALU.add, inc=[(s_ve, 1)])
            guard_set("pw2", s_ve)
            p.wait("scalar", s_ve)
            p.op("scalar", "activation", out=sm_c[:], in_=sm_b[:], func=ACT.Exp,
                 inc=[(s_se, 1)])
            p.wait("vector", s_se)
            w3 = sm_c[:].rearrange("p (j e) -> p j e", j=H)
            p.op("vector", "tensor_reduce", out=sums_s[:, c, :], in_=w3, axis=AX.X,
                 op=ALU.add, inc=[(s_ve, 1)])
            p.wait("vector", s_ve)
            p.op("vector", "reciprocal", out=rsum_s[:, c, :], in_=sums_s[:, c, :])
            p.op("vector", "tensor_reduce", out=sm_b[:, 0:8], in_=w3, axis=AX.X,
                 op=ALU.max, inc=[(s_ve, 1)])
            p.wait("vector", s_ve)
            m1b = sm_b[:, 0:8].rearrange("p (j o) -> p j o", j=H).broadcast_to([128, H, 16])
            p.op("vector", "tensor_tensor",
                 out=sm_a[:].rearrange("p (j e) -> p j e", j=H), in0=w3, in1=m1b,
                 op=ALU.is_lt, inc=[(s_ve, 1)])
            p.wait("vector", s_ve)
            p.op("vector", "tensor_tensor",
                 out=sm_a[:].rearrange("p (j e) -> p j e", j=H), in0=w3,
                 in1=sm_a[:].rearrange("p (j e) -> p j e", j=H), op=ALU.mult,
                 inc=[(s_ve, 1)])
            p.wait("vector", s_ve)
            p.op("vector", "tensor_reduce", out=sm_b[:, 0:8],
                 in_=sm_a[:].rearrange("p (j e) -> p j e", j=H), axis=AX.X,
                 op=ALU.max, inc=[(s_ve, 1)])
            p.wait("vector", s_ve)
            m2b = sm_b[:, 0:8].rearrange("p (j o) -> p j o", j=H).broadcast_to([128, H, 16])
            p.op("vector", "tensor_tensor",
                 out=sm_a[:].rearrange("p (j e) -> p j e", j=H),
                 in0=w3, in1=m2b, op=ALU.is_ge, inc=[(s_ve, 1)])
            p.wait("vector", s_ve)
            p.op("vector", "tensor_tensor",
                 out=sm_b[:].rearrange("p (j e) -> p j e", j=H), in0=w3,
                 in1=sm_a[:].rearrange("p (j e) -> p j e", j=H),
                 op=ALU.mult, inc=[(s_ve, 1)])
            p.wait("vector", s_ve)
            rsb = rsum_s[:, c, :].rearrange("p (j o) -> p j o", j=H).broadcast_to([128, H, 16])
            p.op("vector", "tensor_tensor",
                 out=wu_s[:, c, :, 0:16],
                 in0=sm_b[:].rearrange("p (j e) -> p j e", j=H), in1=rsb,
                 op=ALU.mult, inc=[(s_ve, 1)])
            wu_ve = s_ve.n
            for jj in range(2):
                guard_wait("tensor", "pw2")
                p.wait("tensor", s_ve, wu_ve)
                for j4 in range(4):
                    j = jj * 4 + j4
                    p.op("tensor", "transpose", pw[2][0:16, j4 * 128:(j4 + 1) * 128],
                         wu_s[:, c, j, 0:16], identf_s[:],
                         inc=[(s_pe, 1)] if j4 == 3 else ())
                p.wait("vector", s_pe)
                p.op("vector", "tensor_copy",
                     out=wuT_s[:, jj * 4:(jj + 1) * 4, :],
                     in_=pw[2][0:16, 0:512].rearrange("p (j n) -> p j n", j=4),
                     inc=[(s_ve, 1)])
                guard_set("pw2", s_ve)
            wuT_ve = s_ve.n
            for jj in range(2):
                guard_wait("tensor", "pw2")
                p.wait("tensor", s_ve, wuT_ve)
                for j4 in range(4):
                    j = jj * 4 + j4
                    p.op("tensor", "matmul", pw[2][:, j4 * 128:(j4 + 1) * 128],
                         wuT_s[:, j, :], b2_s[:], start=True, stop=True,
                         inc=[(s_pe, 1)] if j4 == 3 else ())
                p.wait("vector", s_pe)
                p.op("vector", "tensor_copy",
                     out=acc[:, c, jj * 512:(jj + 1) * 512],
                     in_=pw[2][:, 0:512], inc=[(s_ve, 1)])
                guard_set("pw2", s_ve)

        def head_gen(c, h):
            if True:
                ch = c * NH + h
                cs = slice(c * 128, (c + 1) * 128)
                vsl = v_s[:, c, h * dv:(h + 1) * dv]
                bk = gbank[0] % 3
                gbank[0] += 1
                bank = pw[bk]
                guard_wait("tensor", f"pw{bk}")
                p.wait("tensor", s_ve, gate_ve[c])
                p.wait("tensor", s_se, sbf_se[h])
                p.op("tensor", "matmul", bank[:, 256:384], kT[:, h, cs],
                     qT[:, h, cs], start=True, stop=True)
                p.op("tensor", "matmul", bank[:, 0:dv], qT[:, h, cs],
                     S_bf[:, h, :], start=True, stop=False, inc=[(s_pe, 1)])
                p.wait("vector", s_pe)
                p.op("vector", "tensor_mul", out=Pb4[:, h, :], in0=bank[:, 256:384],
                     in1=trimask_s[:], inc=[(s_ve, 1)])
                mask_ve = s_ve.n
                yield
                p.wait("tensor", s_ve, mask_ve)
                p.op("tensor", "matmul", bank[:, 0:dv], Pb4[:, h, :], vsl,
                     start=False, stop=True, inc=[(s_pe, 1)])
                o_pe = s_pe.n
                p.wait("scalar", s_pe, o_pe)
                p.op("scalar", "activation",
                     out=o_rms[:, c, h * dv:(h + 1) * dv], in_=bank[:, 0:dv],
                     func=ACT.Square, accum_out=stats[:, 16 + ch:17 + ch],
                     inc=[(s_se, 1)])
                yield
                p.wait("vector", s_se)
                p.op("vector", "tensor_copy", out=o_rms[:, c, h * dv:(h + 1) * dv],
                     in_=bank[:, 0:dv], inc=[(s_ve, 1)])
                oev_ve = s_ve.n
                yield
                p.wait("tensor", s_ve, oev_ve)
                p.op("tensor", "matmul", bank[:, 256:512], khT_s[:, c, h, :], vsl,
                     start=True, stop=True, inc=[(s_pe, 1)])
                smm_pe = s_pe.n
                p.wait("vector", s_pe, smm_pe)
                p.wait("vector", s_se, gate_se[c])
                p.op("vector", "scalar_tensor_tensor", out=S_s[:, h, :],
                     in0=S_s[:, h, :], scalar=stats[:, 12 + h:13 + h],
                     in1=bank[:, 256:512], op0=ALU.mult, op1=ALU.add,
                     inc=[(s_ve, 1)])
                guard_set(f"pw{bk}", s_ve)
                p.wait("scalar", s_ve)
                p.op("scalar", "activation", out=S_bf[:, h, :], in_=S_s[:, h, :],
                     func=ACT.Copy, inc=[(s_se, 1)])
                sbf_se[h] = s_se.n
                yield

        for c in range(NCH):
            for h in range(NH):
                for _ in head_gen(c, h):
                    pass

        p.wait("scalar", s_se)
        p.op("scalar", "activation", out=rstd_s[:], in_=stats[:, 16:32],
             func=ACT.Ln, scale=1.0 / dv, bias=consts_s[:, 0:1], inc=[(s_se, 1)])
        p.wait("scalar", s_se)
        p.op("scalar", "activation", out=stats[:, 16:32], in_=rstd_s[:],
             func=ACT.Exp, scale=-0.5, inc=[(s_se, 1)])
        p.wait("scalar", s_ve)
        for c in range(NCH):
            p.op("scalar", "activation", out=g_s[:, c, :], in_=g_s[:, c, :],
                 func=ACT.Silu, inc=[(s_se, 1)])
        p.wait("vector", s_se)
        for c in range(NCH):
            for h in range(NH):
                ch = c * NH + h
                p.op("vector", "scalar_tensor_tensor",
                     out=o_rms[:, c, h * dv:(h + 1) * dv],
                     in0=o_rms[:, c, h * dv:(h + 1) * dv],
                     scalar=stats[:, 16 + ch:17 + ch], in1=gwb_s[:],
                     op0=ALU.mult, op1=ALU.mult, inc=[(s_ve, 1)])
            p.wait("vector", s_ve)
            p.op("vector", "tensor_mul", out=o_rms[:, c, :], in0=o_rms[:, c, :],
                 in1=g_s[:, c, :], inc=[(s_ve, 1)])
        p.wait("tensor", s_ve)
        for c in range(NCH):
            transpose_slab(o_rms[:, c, :], xnT, c)
        oT_ve = s_ve.n
        p.wait("tensor", wo_ready[0], wo_ready[1])
        p.wait("tensor", s_ve, oT_ve)
        for c in range(NCH):
            for half in range(2):
                bk = (c * 2 + half) % 2
                bank = pw[bk]
                guard_wait("tensor", f"pw{bk}")
                for kk in range(KC):
                    p.op("tensor", "matmul", bank[:],
                         xnT[:, kk, c * 128:(c + 1) * 128],
                         wbigB[:, kk, half * 512:(half + 1) * 512],
                         start=(kk == 0), stop=(kk == KC - 1),
                         inc=[(s_pe, 1)] if kk == KC - 1 else ())
                p.wait("vector", s_pe)
                p.op("vector", "tensor_add", out=e1_s[:], in0=bank[:],
                     in1=x_s[:, c, half * 512:(half + 1) * 512], inc=[(s_ve, 1)])
                guard_set(f"pw{bk}", s_ve)
                p.wait("vector", s_ve)
                p.op("vector", "tensor_add",
                     out=resid[:, c, half * 512:(half + 1) * 512],
                     in0=e1_s[:], in1=bob_s[:, half * 512:(half + 1) * 512])
            guard_set(f"x_s{c}", s_ve)
        guards["wbigB"] = (s_pe, s_pe.n)
        wo2_ready = wload(wbigB, wo2_d, "wbigB")
        if t + 1 < NTILE:
            xval = xload(t + 1)

        w12load(0)
        w12load(1)
        for e in range(E):
            b = e % 2
            p.wait("tensor", s_w12[b])
            mm1_done = {}
            for h2 in range(2):
                for jh in range(2):
                    g = guards.get(f"hmidS{jh}")
                    for i in range(4):
                        pb = (h2 * 2 + jh + i) % 2
                        bank = ph[pb]
                        guard_wait("tensor", f"ph{pb}")
                        for j4 in range(4):
                            j = jh * 4 + j4
                            p.op("tensor", "matmul", bank[:, j4 * 256:(j4 + 1) * 256],
                                 w1b[:, b, i * 128:(i + 1) * 128],
                                 hT[:, j, h2 * 256:(h2 + 1) * 256],
                                 start=True, stop=True,
                                 inc=[(s_pe, 1)] if j4 == 3 else ())
                        if g is not None:
                            p.wait("scalar", g[0], g[1])
                            g = None
                        p.wait("scalar", s_pe)
                        p.op("scalar", "activation",
                             out=hmidS[:, i, jh * 4:(jh + 1) * 4, :],
                             in_=bank[:], func=ACT.Gelu,
                             bias=b1l_s[:, e, i:i + 1], inc=[(s_se, 1)])
                        guard_set(f"ph{pb}", s_se)
                    mm1_done[jh] = s_se.n
                for jh in range(2):
                    p.wait("tensor", s_se, mm1_done[jh])
                    for j4 in range(4):
                        j = jh * 4 + j4
                        bk = j % 2
                        bank = pw[bk]
                        guard_wait("tensor", f"pw{bk}")
                        for c2 in range(2):
                            for i in range(4):
                                p.op("tensor", "matmul",
                                     bank[:, c2 * 128:(c2 + 1) * 128],
                                     hmidS[:, i, j, c2 * 128:(c2 + 1) * 128],
                                     w2b[:, b, i, :], start=(i == 0), stop=(i == 3),
                                     inc=[(s_pe, 1)] if i == 3 else ())
                        p.wait("vector", s_pe)
                        for c2 in range(2):
                            cc = h2 * 2 + c2
                            p.op("vector", "scalar_tensor_tensor",
                                 out=acc[:, cc, j * 128:(j + 1) * 128],
                                 in0=bank[:, c2 * 128:(c2 + 1) * 128],
                                 scalar=wu_s[:, cc, j, e:e + 1],
                                 in1=acc[:, cc, j * 128:(j + 1) * 128],
                                 op0=ALU.mult, op1=ALU.add,
                                 inc=[(s_ve, 1)] if c2 == 1 else ())
                        guard_set(f"pw{bk}", s_ve)
                    guard_set(f"hmidS{jh}", s_ve)
            guard_set(f"w1b{b}", s_pe)
            if e + 2 < E:
                w12load(e + 2)

        p.wait("vector", s_ve)
        for c in range(NCH):
            p.op("vector", "tensor_copy", out=comb_bf[:, c, :], in_=acc[:, c, :],
                 inc=[(s_ve, 1)])
        p.wait("tensor", s_ve)
        for c in range(NCH):
            transpose_slab(comb_bf[:, c, :], xnT, c)
        combT_ve = s_ve.n
        p.wait("tensor", wo2_ready[0], wo2_ready[1])
        p.wait("tensor", s_ve, combT_ve)
        for c in range(NCH):
            for half in range(2):
                bk = (c * 2 + half) % 2
                bank = pw[bk]
                guard_wait("tensor", f"pw{bk}")
                for kk in range(KC):
                    p.op("tensor", "matmul", bank[:],
                         xnT[:, kk, c * 128:(c + 1) * 128],
                         wbigB[:, kk, half * 512:(half + 1) * 512],
                         start=(kk == 0), stop=(kk == KC - 1),
                         inc=[(s_pe, 1)] if kk == KC - 1 else ())
                ost = (c * 2 + half) % 2
                s_dox = [s_doA, s_doB][ost]
                guard_wait("vector", f"outst{ost}")
                p.wait("vector", s_pe)
                p.op("vector", "tensor_add", out=outst[:, ost, :], in0=bank[:],
                     in1=resid[:, c, half * 512:(half + 1) * 512], inc=[(s_ve, 1)])
                guard_set(f"pw{bk}", s_ve)
                p.wait("sync", s_ve)
                p.op("sync", "dma_start",
                     out=out_d[t * TT + c * 128: t * TT + (c + 1) * 128,
                               half * 512:(half + 1) * 512],
                     in_=outst[:, ost, :], inc=[(s_dox, 16)])
                guard_set(f"outst{ost}", s_dox)
        guards["wbigB"] = (s_pe, s_pe.n)

        if t == 0 and tap_d:
            srcs = {"q": qT[:], "k": kT[:], "gksp": gksp[:], "v": v_s[:],
                    "g": g_s[:], "h": hT[:], "wu": wu_s[:], "rsum": rsum_s[:],
                    "S": S_s[:], "orms": o_rms[:], "resid": resid[:],
                    "acc": acc[:], "xnt": xnT[:], "sums": sums_s[:],
                    "smc": sm_c[:], "xs": x_s[:],
                    "smc": sm_c[:]}
            p.wait("sync", s_ve)
            p.wait("sync", s_se)
            for name, dst in tap_d.items():
                p.op("sync", "dma_start", out=dst, in_=srcs[name],
                     inc=[(s_dt, 16)])
                p.wait("sync", s_dt)

    p.wait("sync", s_doA, s_doA.n)
    p.wait("sync", s_doB, s_doB.n)
    p.wait("sync", s_dt, s_dt.n)
    p.wait("gpsimd", s_wA, s_wA.n)
    p.wait("gpsimd", s_wB, s_wB.n)
    p.wait("gpsimd", s_w12[0], s_w12[0].n)
    p.wait("gpsimd", s_w12[1], s_w12[1].n)

    block = es.enter_context(nc.Block())
    p.run(block)
    es.close()
    return nc


_NC_CACHE = {}


def _get_nc(taps=()):
    key = tuple(sorted(taps))
    if key not in _NC_CACHE:
        _NC_CACHE[key] = build_nc(taps=key)
    return _NC_CACHE[key]


def make_core_inputs(inputs):
    f = np.ascontiguousarray
    shared = {
        "wq": f(np.asarray(inputs["L0_attn_Wq"], np.float32)),
        "wk": f(np.asarray(inputs["L0_attn_Wk"], np.float32)),
        "wv": f(np.asarray(inputs["L0_attn_Wv"], np.float32)),
        "wg": f(np.asarray(inputs["L0_attn_Wg"], np.float32)),
        "wo": f(np.asarray(inputs["L0_attn_Wo"], np.float32)),
        "wm": f(np.asarray(inputs["L0_ffn_Wm"], np.float32)),
        "wo2": f(np.asarray(inputs["L0_ffn_Wo"], np.float32)),
        "wgk1": f(np.asarray(inputs["L0_attn_Wgk1"], np.float32)),
        "wgk2": f(np.asarray(inputs["L0_attn_Wgk2"], np.float32)),
        "w1": f(np.asarray(inputs["L0_ffn_W1"], np.float32)),
        "w2": f(np.asarray(inputs["L0_ffn_W2"], np.float32)),
        "emb": f(np.asarray(inputs["L0_ffn_expert_emb"], np.float32)),
        "b2": f(np.asarray(inputs["L0_ffn_b2"], np.float32)),
        "bgk2l": f(np.asarray(inputs["L0_attn_bgk2"], np.float32).reshape(4, 128).T),
        "bml": f(np.asarray(inputs["L0_ffn_bm"], np.float32).reshape(8, 128).T),
        "b1l": f(np.asarray(inputs["L0_ffn_b1"], np.float32).reshape(E, 4, 128).transpose(2, 0, 1)),
        "bob": f(np.broadcast_to(np.asarray(inputs["L0_ffn_bo"], np.float32), (128, D))),
        "gwb": f(np.broadcast_to(np.asarray(inputs["L0_attn_gw"], np.float32), (128, dv))),
        "ident": np.eye(128, dtype=ml_dtypes.bfloat16),
        "identf": np.eye(128, dtype=np.float32),
        "trimask": f(np.triu(np.ones((128, 128))).astype(ml_dtypes.bfloat16)),
    }
    x = np.asarray(inputs["L0_attn_x"], np.float32)
    in_maps = []
    for core in range(8):
        m = dict(shared)
        m["x"] = f(x[core])
        in_maps.append(m)
    return in_maps


def kernel(**inputs):
    nc = _get_nc()
    in_maps = make_core_inputs(inputs)
    res = run_bass_kernel_spmd(nc, in_maps, core_ids=list(range(8)))
    return np.stack([r["out"] for r in res.results])


# version 09
# speedup vs baseline: 1.0457x; 1.0457x over previous
import numpy as np
import ml_dtypes
import contextlib
import concourse.bass as bass
import concourse.mybir as mybir
from concourse.bass_utils import run_bass_kernel_spmd

F32 = mybir.dt.float32
BF16 = mybir.dt.bfloat16
ACT = mybir.ActivationFunctionType
ALU = mybir.AluOpType
AX = mybir.AxisListType

T, D = 2048, 1024
NTILE, TT, C, NCH = 4, 512, 128, 4
NH, dk, dv, KC = 4, 128, 256, 8
H, HD, E, I = 8, 128, 16, 512
EPS = 1e-5
LNSCALE = float(np.log(128.0 ** -0.5))
ENGS = ("tensor", "vector", "scalar", "gpsimd", "sync")


class Sem:
    __slots__ = ("h", "n")

    def __init__(self, h):
        self.h = h
        self.n = 0


import os
SERIAL = bool(int(os.environ.get("KERNEL_SERIAL", "0")))


class Prog:
    def __init__(self, nc, es):
        self.nc = nc
        self.es = es
        self.ops = {e: [] for e in ENGS}
        self.esem = {}

    def sem(self, name):
        return Sem(self.es.enter_context(self.nc.semaphore(name)))

    def wait(self, eng, sem, val=None):
        v = sem.n if val is None else val
        assert v <= sem.n, f"wait on future increment ({v} > {sem.n})"
        if v <= 0 or SERIAL:
            return
        self.ops[eng].append(lambda e: e.wait_ge(sem.h, v))

    def op(self, eng, meth, *a, inc=(), **kw):
        inc = list(inc)
        if SERIAL:
            for s, k in inc:
                s.n += k
            for oe, os_ in self.esem.items():
                if os_.n > 0:
                    v = os_.n
                    self.ops[eng].append(
                        (lambda sem, vv: lambda e: e.wait_ge(sem.h, vv))(os_, v))
            own = self.esem[eng]
            inc = [(own, 16 if meth == "dma_start" else 1)]
        for s, k in inc:
            s.n += k

        def f(e):
            inst = getattr(e, meth)(*a, **kw)
            for s, k in inc:
                inst.then_inc(s.h, k)
        self.ops[eng].append(f)

    def run(self, block):
        for eng in ENGS:
            oplist = self.ops[eng]

            def make(lst):
                def body(e):
                    for f in lst:
                        f(e)
                return body
            getattr(block, eng)(make(oplist))


def build_nc(taps=()):
    nc = bass.Bass("TRN2", target_bir_lowering=False)
    dr = lambda n, s, d=F32, k="ExternalInput": nc.dram_tensor(n, s, d, kind=k).ap()
    x_d = dr("x", [T, D])
    wq_d = dr("wq", [D, 512])
    wk_d = dr("wk", [D, 512])
    wv_d = dr("wv", [D, D])
    wg_d = dr("wg", [D, D])
    wo_d = dr("wo", [D, D])
    wm_d = dr("wm", [D, D])
    wo2_d = dr("wo2", [D, D])
    wgk1_d = dr("wgk1", [D, 16])
    wgk2_d = dr("wgk2", [16, 512])
    w1_d = dr("w1", [E, HD, I])
    w2_d = dr("w2", [E, I, HD])
    emb_d = dr("emb", [HD, E])
    b2_d = dr("b2", [E, HD])
    bgk2l_d = dr("bgk2l", [128, 4])
    bml_d = dr("bml", [128, 8])
    b1l_d = dr("b1l", [128, E, 4])
    bob_d = dr("bob", [128, D])
    gwb_d = dr("gwb", [128, dv])
    ident_d = dr("ident", [128, 128], BF16)
    identf_d = dr("identf", [128, 128])
    trimask_d = dr("trimask", [128, 128], BF16)
    out_d = dr("out", [T, D], F32, "ExternalOutput")
    tap_shapes = {"q": ([128, 4, TT], BF16), "k": ([128, 4, TT], BF16),
                  "gksp": ([128, 4, TT], F32), "v": ([128, NCH, D], BF16),
                  "g": ([128, NCH, D], BF16), "h": ([128, H, TT], BF16),
                  "wu": ([128, NCH, H, 32], F32), "rsum": ([128, NCH, H], F32),
                  "S": ([128, NH, dv], F32), "orms": ([128, NCH, D], BF16),
                  "resid": ([128, NCH, D], BF16), "acc": ([128, NCH, D], F32),
                  "xnt": ([128, KC, TT], BF16), "sums": ([128, NCH, H], F32),
                  "lgd": ([128, NCH, 128], F32), "smc": ([128, 128], F32),
                  "xs": ([128, NCH, D], F32), "lnst": ([128, NCH, 8], F32)}
    tap_d = {tp: dr("tap_" + tp, *tap_shapes[tp], k="ExternalOutput") for tp in taps}

    es = contextlib.ExitStack()
    sb = lambda n, s, d=F32: es.enter_context(nc.sbuf_tensor(n, s, d))

    wq_s = sb("wq_s", [128, KC, 512], BF16)
    wk_s = sb("wk_s", [128, KC, 512], BF16)
    wgk1_s = sb("wgk1_s", [128, KC, 16], BF16)
    wgk2_s = sb("wgk2_s", [16, 4, 128], BF16)
    emb_s = sb("emb_s", [128, E], BF16)
    b2_s = sb("b2_s", [16, 128], BF16)
    b1l_s = sb("b1l_s", [128, E, 4])
    bml_s = sb("bml_s", [128, 8])
    bgk2r_s = sb("bgk2r_s", [128, 4])
    bgk2n_s = sb("bgk2n_s", [128, 4])
    bob_s = sb("bob_s", [128, D])
    gwb_s = sb("gwb_s", [128, dv])
    ident_s = sb("ident_s", [128, 128], BF16)
    identf_s = sb("identf_s", [128, 128])
    trimask_s = sb("trimask_s", [128, 128], BF16)
    ones_s = sb("ones_s", [128, 128])
    ones_s2 = sb("ones_s2", [128, 512])
    consts_s = sb("consts_s", [128, 3])
    wbigA = sb("wbigA", [128, KC, D], BF16)
    wbigB = sb("wbigB", [128, KC, D], BF16)
    w1b = sb("w1b", [128, 2, I], BF16)
    w2b = sb("w2b", [128, 2, 4, HD], BF16)
    x_s = sb("x_s", [128, NCH, D])
    xn_tm = sb("xn_tm", [128, D], BF16)
    xnT = sb("xnT", [128, KC, TT], BF16)
    qT = sb("qT", [128, NH, TT], BF16)
    kT = sb("kT", [128, NH, TT], BF16)
    gksp = sb("gksp", [128, NCH, NH, 128])
    rstd_s = sb("rstd_s", [128, 16])
    v_s = sb("v_s", [128, NCH, D], BF16)
    g_s = sb("g_s", [128, NCH, D], BF16)
    hT = sb("hT", [128, H, TT], BF16)
    gk1b = sb("gk1b", [16, TT], BF16)
    e1_s = sb("e1_s", [128, 512])
    sm_a = sb("sm_a", [128, 128])
    sm_b = sb("sm_b", [128, 128])
    sm_c = sb("sm_c", [128, 128])
    wu_s = sb("wu_s", [128, NCH, H, 32])
    wuT_s = sb("wuT_s", [16, H, 128], BF16)
    rsum_s = sb("rsum_s", [128, NCH, H])
    sums_s = sb("sums_s", [128, NCH, H])
    stats = sb("stats", [128, 32])
    S_s = sb("S_s", [128, NH, dv])
    S_bf = sb("S_bf", [128, NH, dv], BF16)
    A4 = sb("A4", [128, NH, 128])
    lam4 = sb("lam4", [128, NH, 128])
    Pb4 = sb("Pb4", [128, NH, 128], BF16)
    khT_s = sb("khT_s", [128, NCH, NH, 128], BF16)
    o_rms = sb("o_rms", [128, NCH, D], BF16)
    resid = sb("resid", [128, NCH, D], BF16)
    acc = sb("acc", [128, NCH, D])
    comb_bf = v_s
    hmidS = sb("hmidS", [128, 4, H, 256], BF16)
    kh4 = hmidS.rearrange("p a h n -> p (a h n)")[:, 0:NCH * NH * 128] \
        .rearrange("p (c h n) -> p c h n", c=NCH, h=NH)
    outst = sb("outst", [128, 2, 512])

    ph = [es.enter_context(nc.psum_tensor(f"ph{i}", [128, 1024], F32)) for i in range(2)]
    pw = [es.enter_context(nc.psum_tensor(f"pw{i}", [128, 512], F32)) for i in range(3)]
    ptp = es.enter_context(nc.psum_tensor("ptp", [128, 1024], BF16))

    p = Prog(nc, es)
    s_dx = p.sem("s_dx")
    s_di = p.sem("s_di")
    s_winit = p.sem("s_winit")
    s_wA = p.sem("s_wA")
    s_wB = p.sem("s_wB")
    s_w12 = [p.sem("s_w12a"), p.sem("s_w12b")]
    s_doA = p.sem("s_doA")
    s_doB = p.sem("s_doB")
    s_dt = p.sem("s_dt")
    s_pe = p.sem("s_pe")
    s_ve = p.sem("s_ve")
    s_se = p.sem("s_se")
    if SERIAL:
        p.esem = {e: p.sem("ser_" + e) for e in ENGS}

    guards = {}

    def guard_wait(eng, key):
        g = guards.get(key)
        if g is not None:
            p.wait(eng, g[0], g[1])

    def guard_set(key, sem):
        guards[key] = (sem, sem.n)

    for dst, src in [(b1l_s, b1l_d), (bml_s, bml_d), (bgk2r_s, bgk2l_d),
                     (bob_s, bob_d), (gwb_s, gwb_d), (ident_s, ident_d),
                     (identf_s, identf_d), (trimask_s, trimask_d)]:
        p.op("sync", "dma_start", out=dst[:], in_=src, inc=[(s_di, 16)])
    p.op("gpsimd", "dma_start", out=wq_s[:],
         in_=wq_d.rearrange("(kk p) n -> p kk n", p=128), inc=[(s_winit, 16)])
    p.op("gpsimd", "dma_start", out=wk_s[:],
         in_=wk_d.rearrange("(kk p) n -> p kk n", p=128), inc=[(s_winit, 16)])
    p.op("gpsimd", "dma_start", out=wgk1_s[:],
         in_=wgk1_d.rearrange("(kk p) n -> p kk n", p=128), inc=[(s_winit, 16)])
    p.op("gpsimd", "dma_start", out=wgk2_s[:],
         in_=wgk2_d.rearrange("k (m n) -> k m n", m=4), inc=[(s_winit, 16)])
    p.op("gpsimd", "dma_start", out=emb_s[:], in_=emb_d, inc=[(s_winit, 16)])
    p.op("gpsimd", "dma_start", out=b2_s[:], in_=b2_d, inc=[(s_winit, 16)])
    p.wait("vector", s_di)
    p.wait("tensor", s_di)
    p.wait("tensor", s_winit)
    p.wait("scalar", s_di)
    p.op("vector", "memset", wu_s[:], 0.0)
    p.op("vector", "memset", S_s[:], 0.0)
    p.op("vector", "memset", S_bf[:], 0.0)
    p.op("vector", "memset", ones_s[:], 1.0)
    p.op("vector", "memset", ones_s2[:], 1.0)
    p.op("vector", "memset", consts_s[:, 0:1], EPS)
    p.op("vector", "memset", consts_s[:, 1:2], 1.0)
    p.op("vector", "memset", consts_s[:, 2:3], LNSCALE)
    p.op("vector", "memset", stats[:, 8:9], 0.0)
    p.op("vector", "tensor_scalar_mul", out=bgk2n_s[:], in0=bgk2r_s[:],
         scalar1=-1.0, inc=[(s_ve, 1)])

    wsem = {"wbigA": s_wA, "wbigB": s_wB}

    def wload(dst_s, src_d, key):
        guard_wait("gpsimd", key)
        sem = wsem[key]
        p.op("gpsimd", "dma_start", out=dst_s[:],
             in_=src_d.rearrange("(kk p) n -> p kk n", p=128), inc=[(sem, 16)])
        return (sem, sem.n)

    def w12load(e):
        b = e % 2
        guard_wait("gpsimd", f"w1b{b}")
        p.op("gpsimd", "dma_start", out=w1b[:, b, :], in_=w1_d[e], inc=[(s_w12[b], 16)])
        p.op("gpsimd", "dma_start", out=w2b[:, b, :, :],
             in_=w2_d[e].rearrange("(i p) n -> p i n", p=128), inc=[(s_w12[b], 16)])

    def transpose_slab(src2d, dstT, c, nblk=8):
        guard_wait("tensor", "ptp")
        for i in range(nblk):
            p.op("tensor", "transpose", ptp[:, i * 128:(i + 1) * 128],
                 src2d[:, i * 128:(i + 1) * 128], ident_s[:],
                 inc=[(s_pe, 1)] if i == nblk - 1 else ())
        p.wait("vector", s_pe)
        p.op("vector", "tensor_copy",
             out=dstT[:, 0:nblk, c * 128:(c + 1) * 128],
             in_=ptp.rearrange("p (i n) -> p i n", i=8)[:, 0:nblk, :],
             inc=[(s_ve, 1)])
        guard_set("ptp", s_ve)

    sbf_se = {h: 0 for h in range(NH)}
    gate_ve = {}
    gate_se = {}
    kh_ve_c = {}
    gbank = [0]

    def xload(t):
        for c in range(NCH):
            guard_wait("sync", f"x_s{c}")
            p.op("sync", "dma_start", out=x_s[:, c, :],
                 in_=x_d[t * TT + c * 128: t * TT + (c + 1) * 128, :],
                 inc=[(s_dx, 16)])
        return s_dx.n

    xval = xload(0)

    for t in range(NTILE):
        wv_ready = wload(wbigA, wv_d, "wbigA")
        wg_ready = wload(wbigB, wg_d, "wbigB")
        for c in range(NCH):
            p.wait("scalar", s_dx, xval)
            p.wait("scalar", s_ve)
            p.op("scalar", "activation", out=xn_tm[:], in_=x_s[:, c, :],
                 func=ACT.Square, accum_out=stats[:, 0:1], inc=[(s_se, 1)])
            p.wait("vector", s_dx, xval)
            p.op("vector", "reduce_sum", out=stats[:, 1:2], in_=x_s[:, c, :], axis=AX.X,
                 inc=[(s_ve, 1)])
            p.wait("vector", s_ve)
            p.op("vector", "tensor_scalar_mul", out=stats[:, 2:3], in0=stats[:, 1:2],
                 scalar1=1.0 / D, inc=[(s_ve, 1)])
            p.wait("vector", s_ve)
            p.op("vector", "tensor_mul", out=stats[:, 3:4], in0=stats[:, 2:3],
                 in1=stats[:, 2:3], inc=[(s_ve, 1)])
            p.wait("vector", s_ve)
            p.wait("vector", s_se)
            p.op("vector", "tensor_scalar", out=stats[:, 4:5], in0=stats[:, 0:1],
                 scalar1=1.0 / D, scalar2=stats[:, 3:4], op0=ALU.mult,
                 op1=ALU.subtract, inc=[(s_ve, 1)])
            p.wait("scalar", s_ve)
            p.op("scalar", "activation", out=stats[:, 5:6], in_=stats[:, 4:5],
                 func=ACT.Ln, bias=consts_s[:, 0:1], inc=[(s_se, 1)])
            p.wait("scalar", s_se)
            p.op("scalar", "activation", out=stats[:, 6:7], in_=stats[:, 5:6],
                 func=ACT.Exp, scale=-0.5, inc=[(s_se, 1)])
            p.wait("vector", s_se)
            p.wait("vector", s_pe)
            p.op("vector", "tensor_scalar", out=xn_tm[:], in0=x_s[:, c, :],
                 scalar1=stats[:, 2:3], scalar2=stats[:, 6:7],
                 op0=ALU.subtract, op1=ALU.mult, inc=[(s_ve, 1)])
            p.wait("tensor", s_ve)
            transpose_slab(xn_tm, xnT, c)
        xn_done = s_ve.n

        def proj_fm(wsb, dstT, mrange, bias_l=None, evac_eng="vector"):
            for mi, m in enumerate(mrange):
                bk = mi % 2
                bank = pw[bk]
                guard_wait("tensor", f"pw{bk}")
                for kk in range(KC):
                    p.op("tensor", "matmul", bank[:],
                         wsb[:, kk, m * 128:(m + 1) * 128], xnT[:, kk, :],
                         start=(kk == 0), stop=(kk == KC - 1),
                         inc=[(s_pe, 1)] if kk == KC - 1 else ())
                if evac_eng == "vector":
                    p.wait("vector", s_pe)
                    p.op("vector", "tensor_copy", out=dstT[:, m, :], in_=bank[:],
                         inc=[(s_ve, 1)])
                    guard_set(f"pw{bk}", s_ve)
                else:
                    p.wait("scalar", s_pe)
                    p.op("scalar", "activation", out=dstT[:, m, :], in_=bank[:],
                         func=ACT.Identity, bias=bias_l[:, m:m + 1], inc=[(s_se, 1)])
                    guard_set(f"pw{bk}", s_se)

        p.wait("tensor", s_ve, xn_done)
        proj_fm(wq_s, qT, range(NH))
        proj_fm(wk_s, kT, range(NH))

        guard_wait("tensor", "pw2")
        for kk in range(KC):
            p.op("tensor", "matmul", pw[2][0:16, 0:TT], wgk1_s[:, kk, :],
                 xnT[:, kk, :], start=(kk == 0), stop=(kk == KC - 1),
                 inc=[(s_pe, 1)] if kk == KC - 1 else ())
        p.wait("vector", s_pe)
        p.op("vector", "tensor_copy", out=gk1b[:], in_=pw[2][0:16, 0:TT],
             inc=[(s_ve, 1)])
        guard_set("pw2", s_ve)
        gk1_ve = s_ve.n
        for m in range(4):
            bk = m % 2
            bank = pw[bk]
            guard_wait("tensor", f"pw{bk}")
            p.wait("tensor", s_ve, gk1_ve)
            p.op("tensor", "matmul", bank[:], wgk2_s[:, m, :], gk1b[:],
                 start=True, stop=True, inc=[(s_pe, 1)])
            p.wait("scalar", s_pe)
            p.op("scalar", "activation", out=e1_s[:], in_=bank[:],
                 func=ACT.Exp, scale=-1.0, bias=bgk2n_s[:, m:m + 1], inc=[(s_se, 1)])
            p.wait("scalar", s_se)
            p.op("scalar", "activation", out=gksp[:, :, m, :], in_=e1_s[:],
                 func=ACT.Ln, bias=consts_s[:, 1:2], inc=[(s_se, 1)])
            guard_set(f"pw{bk}", s_se)

        def proj_tm(wsb, dst, wready):
            p.wait("tensor", wready[0], wready[1])
            for ci in range(NCH):
                for half in range(2):
                    bk = (ci * 2 + half) % 2
                    bank = pw[bk]
                    guard_wait("tensor", f"pw{bk}")
                    for kk in range(KC):
                        p.op("tensor", "matmul", bank[:],
                             xnT[:, kk, ci * 128:(ci + 1) * 128],
                             wsb[:, kk, half * 512:(half + 1) * 512],
                             start=(kk == 0), stop=(kk == KC - 1),
                             inc=[(s_pe, 1)] if kk == KC - 1 else ())
                    p.wait("scalar", s_pe)
                    p.op("scalar", "copy",
                         out=dst[:, ci, half * 512:(half + 1) * 512],
                         in_=bank[:], inc=[(s_se, 1)])
                    guard_set(f"pw{bk}", s_se)

        gk_done_se = s_se.n
        for c in range(NCH):
            cs = slice(c * 128, (c + 1) * 128)
            A2 = A4.rearrange("p h n -> p (h n)")
            p.wait("vector", s_se, gk_done_se)
            p.op("vector", "tensor_tensor_scan", out=A2, data0=ones_s2[:],
                 data1=gksp[:, c].rearrange("p h n -> p (h n)"),
                 initial=0.0, op0=ALU.mult, op1=ALU.add, inc=[(s_ve, 1)])
            p.wait("vector", s_ve)
            p.op("vector", "tensor_copy", out=stats[:, 9:12],
                 in_=A4[:, 0:3, 127], inc=[(s_ve, 1)])
            p.wait("vector", s_ve)
            pcb = stats[:, 8:12].rearrange("p (h o) -> p h o", h=NH).broadcast_to([128, NH, 128])
            p.op("vector", "tensor_tensor", out=A4[:], in0=A4[:], in1=pcb,
                 op=ALU.subtract, inc=[(s_ve, 1)])
            A_ve = s_ve.n
            p.wait("scalar", s_ve, A_ve)
            p.op("scalar", "activation", out=stats[:, 12:16], in_=A4[:, :, 127],
                 func=ACT.Exp, scale=-1.0 / 16.0, inc=[(s_se, 1)])
            p.op("scalar", "activation", out=lam4.rearrange("p h n -> p (h n)"),
                 in_=A2, func=ACT.Exp, scale=1.0 / 16.0, inc=[(s_se, 1)])
            p.wait("vector", s_se)
            p.op("vector", "tensor_mul", out=kT[:, :, cs], in0=kT[:, :, cs],
                 in1=lam4[:], inc=[(s_ve, 1)])
            kt_ve = s_ve.n
            p.wait("vector", s_ve)
            dsb = stats[:, 12:16].rearrange("p (h o) -> p h o", h=NH).broadcast_to([128, NH, 128])
            if c == 0:
                guard_wait("vector", "hmidS0")
                guard_wait("vector", "hmidS1")
            p.op("vector", "tensor_tensor", out=kh4[:, c], in0=kT[:, :, cs], in1=dsb,
                 op=ALU.mult, inc=[(s_ve, 1)])
            kh_ve_c[c] = s_ve.n
            p.wait("scalar", s_ve, kt_ve)
            p.op("scalar", "activation", out=lam4.rearrange("p h n -> p (h n)"),
                 in_=A2, func=ACT.Exp, scale=-1.0 / 16.0, bias=consts_s[:, 2:3],
                 inc=[(s_se, 1)])
            p.wait("vector", s_se)
            p.op("vector", "tensor_mul", out=qT[:, :, cs], in0=qT[:, :, cs],
                 in1=lam4[:], inc=[(s_ve, 1)])
            gate_ve[c] = s_ve.n
            gate_se[c] = s_se.n

        proj_tm(wbigA, v_s, wv_ready)
        guards["wbigA"] = (s_pe, s_pe.n)
        wm_ready = wload(wbigA, wm_d, "wbigA")
        for c in range(NCH):
            guard_wait("tensor", "ptp")
            p.wait("tensor", s_ve, kh_ve_c[c])
            for h in range(NH):
                p.op("tensor", "transpose", ptp[:, h * 128:(h + 1) * 128],
                     kh4[:, c, h, :], ident_s[:],
                     inc=[(s_pe, 1)] if h == NH - 1 else ())
            p.wait("vector", s_pe)
            p.op("vector", "tensor_copy", out=khT_s[:, c, :, :],
                 in_=ptp.rearrange("p (i n) -> p i n", i=8)[:, 0:NH, :],
                 inc=[(s_ve, 1)])
            guard_set("ptp", s_ve)
        guards["hmidS0"] = guards["hmidS1"] = (s_ve, s_ve.n)
        proj_tm(wbigB, g_s, wg_ready)
        guards["wbigB"] = (s_pe, s_pe.n)
        wo_ready = wload(wbigB, wo_d, "wbigB")
        p.wait("tensor", wm_ready[0], wm_ready[1])
        proj_fm(wbigA, hT, range(H), bias_l=bml_s, evac_eng="scalar")
        guards["wbigA"] = (s_pe, s_pe.n)

        h_done_se = s_se.n
        for c in range(NCH):
            guard_wait("tensor", "pw2")
            p.wait("tensor", s_se, h_done_se)
            for j in range(H):
                p.op("tensor", "matmul", pw[2][:, j * 16:(j + 1) * 16],
                     hT[:, j, c * 128:(c + 1) * 128], emb_s[:],
                     start=True, stop=True,
                     inc=[(s_pe, 1)] if j == H - 1 else ())
            lg = pw[2][:, 0:128].rearrange("p (j e) -> p j e", j=H)
            p.wait("vector", s_pe)
            p.op("vector", "tensor_reduce", out=sm_a[:, 0:8], in_=lg, axis=AX.X,
                 op=ALU.max, negate=True, inc=[(s_ve, 1)])
            p.wait("vector", s_ve)
            nmx = sm_a[:, 0:8].rearrange("p (j o) -> p j o", j=H).broadcast_to([128, H, 16])
            p.wait("vector", s_se)
            p.op("vector", "tensor_tensor",
                 out=sm_b[:].rearrange("p (j e) -> p j e", j=H), in0=lg, in1=nmx,
                 op=ALU.add, inc=[(s_ve, 1)])
            guard_set("pw2", s_ve)
            p.wait("scalar", s_ve)
            p.op("scalar", "activation", out=sm_c[:], in_=sm_b[:], func=ACT.Exp,
                 inc=[(s_se, 1)])
            p.wait("vector", s_se)
            w3 = sm_c[:].rearrange("p (j e) -> p j e", j=H)
            p.op("vector", "tensor_reduce", out=sums_s[:, c, :], in_=w3, axis=AX.X,
                 op=ALU.add, inc=[(s_ve, 1)])
            p.wait("vector", s_ve)
            p.op("vector", "reciprocal", out=rsum_s[:, c, :], in_=sums_s[:, c, :])
            p.op("vector", "tensor_reduce", out=sm_b[:, 0:8], in_=w3, axis=AX.X,
                 op=ALU.max, inc=[(s_ve, 1)])
            p.wait("vector", s_ve)
            m1b = sm_b[:, 0:8].rearrange("p (j o) -> p j o", j=H).broadcast_to([128, H, 16])
            p.op("vector", "tensor_tensor",
                 out=sm_a[:].rearrange("p (j e) -> p j e", j=H), in0=w3, in1=m1b,
                 op=ALU.is_lt, inc=[(s_ve, 1)])
            p.wait("vector", s_ve)
            p.op("vector", "tensor_tensor",
                 out=sm_a[:].rearrange("p (j e) -> p j e", j=H), in0=w3,
                 in1=sm_a[:].rearrange("p (j e) -> p j e", j=H), op=ALU.mult,
                 inc=[(s_ve, 1)])
            p.wait("vector", s_ve)
            p.op("vector", "tensor_reduce", out=sm_b[:, 0:8],
                 in_=sm_a[:].rearrange("p (j e) -> p j e", j=H), axis=AX.X,
                 op=ALU.max, inc=[(s_ve, 1)])
            p.wait("vector", s_ve)
            m2b = sm_b[:, 0:8].rearrange("p (j o) -> p j o", j=H).broadcast_to([128, H, 16])
            p.op("vector", "tensor_tensor",
                 out=sm_a[:].rearrange("p (j e) -> p j e", j=H),
                 in0=w3, in1=m2b, op=ALU.is_ge, inc=[(s_ve, 1)])
            p.wait("vector", s_ve)
            p.op("vector", "tensor_tensor",
                 out=sm_b[:].rearrange("p (j e) -> p j e", j=H), in0=w3,
                 in1=sm_a[:].rearrange("p (j e) -> p j e", j=H),
                 op=ALU.mult, inc=[(s_ve, 1)])
            p.wait("vector", s_ve)
            rsb = rsum_s[:, c, :].rearrange("p (j o) -> p j o", j=H).broadcast_to([128, H, 16])
            p.op("vector", "tensor_tensor",
                 out=wu_s[:, c, :, 0:16],
                 in0=sm_b[:].rearrange("p (j e) -> p j e", j=H), in1=rsb,
                 op=ALU.mult, inc=[(s_ve, 1)])
            wu_ve = s_ve.n
            for jj in range(2):
                guard_wait("tensor", "pw2")
                p.wait("tensor", s_ve, wu_ve)
                for j4 in range(4):
                    j = jj * 4 + j4
                    p.op("tensor", "transpose", pw[2][0:16, j4 * 128:(j4 + 1) * 128],
                         wu_s[:, c, j, 0:16], identf_s[:],
                         inc=[(s_pe, 1)] if j4 == 3 else ())
                p.wait("vector", s_pe)
                p.op("vector", "tensor_copy",
                     out=wuT_s[:, jj * 4:(jj + 1) * 4, :],
                     in_=pw[2][0:16, 0:512].rearrange("p (j n) -> p j n", j=4),
                     inc=[(s_ve, 1)])
                guard_set("pw2", s_ve)
            wuT_ve = s_ve.n
            for jj in range(2):
                guard_wait("tensor", "pw2")
                p.wait("tensor", s_ve, wuT_ve)
                for j4 in range(4):
                    j = jj * 4 + j4
                    p.op("tensor", "matmul", pw[2][:, j4 * 128:(j4 + 1) * 128],
                         wuT_s[:, j, :], b2_s[:], start=True, stop=True,
                         inc=[(s_pe, 1)] if j4 == 3 else ())
                p.wait("vector", s_pe)
                p.op("vector", "tensor_copy",
                     out=acc[:, c, jj * 512:(jj + 1) * 512],
                     in_=pw[2][:, 0:512], inc=[(s_ve, 1)])
                guard_set("pw2", s_ve)

        def head_gen(c, h):
            if True:
                ch = c * NH + h
                cs = slice(c * 128, (c + 1) * 128)
                vsl = v_s[:, c, h * dv:(h + 1) * dv]
                bk = gbank[0] % 3
                gbank[0] += 1
                bank = pw[bk]
                guard_wait("tensor", f"pw{bk}")
                p.wait("tensor", s_ve, gate_ve[c])
                p.wait("tensor", s_se, sbf_se[h])
                p.op("tensor", "matmul", bank[:, 256:384], kT[:, h, cs],
                     qT[:, h, cs], start=True, stop=True)
                p.op("tensor", "matmul", bank[:, 0:dv], qT[:, h, cs],
                     S_bf[:, h, :], start=True, stop=False, inc=[(s_pe, 1)])
                p.wait("vector", s_pe)
                p.op("vector", "tensor_mul", out=Pb4[:, h, :], in0=bank[:, 256:384],
                     in1=trimask_s[:], inc=[(s_ve, 1)])
                mask_ve = s_ve.n
                yield
                p.wait("tensor", s_ve, mask_ve)
                p.op("tensor", "matmul", bank[:, 0:dv], Pb4[:, h, :], vsl,
                     start=False, stop=True, inc=[(s_pe, 1)])
                o_pe = s_pe.n
                p.wait("scalar", s_pe, o_pe)
                p.op("scalar", "activation",
                     out=o_rms[:, c, h * dv:(h + 1) * dv], in_=bank[:, 0:dv],
                     func=ACT.Square, accum_out=stats[:, 16 + ch:17 + ch],
                     inc=[(s_se, 1)])
                yield
                p.wait("vector", s_se)
                p.op("vector", "tensor_copy", out=o_rms[:, c, h * dv:(h + 1) * dv],
                     in_=bank[:, 0:dv], inc=[(s_ve, 1)])
                oev_ve = s_ve.n
                yield
                p.wait("tensor", s_ve, oev_ve)
                p.op("tensor", "matmul", bank[:, 256:512], khT_s[:, c, h, :], vsl,
                     start=True, stop=True, inc=[(s_pe, 1)])
                smm_pe = s_pe.n
                p.wait("vector", s_pe, smm_pe)
                p.wait("vector", s_se, gate_se[c])
                p.op("vector", "scalar_tensor_tensor", out=S_s[:, h, :],
                     in0=S_s[:, h, :], scalar=stats[:, 12 + h:13 + h],
                     in1=bank[:, 256:512], op0=ALU.mult, op1=ALU.add,
                     inc=[(s_ve, 1)])
                guard_set(f"pw{bk}", s_ve)
                p.wait("scalar", s_ve)
                p.op("scalar", "activation", out=S_bf[:, h, :], in_=S_s[:, h, :],
                     func=ACT.Copy, inc=[(s_se, 1)])
                sbf_se[h] = s_se.n
                yield

        for c in range(NCH):
            for h in range(NH):
                for _ in head_gen(c, h):
                    pass

        w12load(0)
        w12load(1)
        p.wait("scalar", s_se)
        p.op("scalar", "activation", out=rstd_s[:], in_=stats[:, 16:32],
             func=ACT.Ln, scale=1.0 / dv, bias=consts_s[:, 0:1], inc=[(s_se, 1)])
        p.wait("scalar", s_se)
        p.op("scalar", "activation", out=stats[:, 16:32], in_=rstd_s[:],
             func=ACT.Exp, scale=-0.5, inc=[(s_se, 1)])
        exp_se = s_se.n
        for c in range(NCH):
            p.op("scalar", "activation", out=g_s[:, c, :], in_=g_s[:, c, :],
                 func=ACT.Silu, inc=[(s_se, 1)])
        silu_se = s_se.n
        p.wait("vector", s_se, exp_se)
        for c in range(NCH):
            for h in range(NH):
                ch = c * NH + h
                p.op("vector", "scalar_tensor_tensor",
                     out=o_rms[:, c, h * dv:(h + 1) * dv],
                     in0=o_rms[:, c, h * dv:(h + 1) * dv],
                     scalar=stats[:, 16 + ch:17 + ch], in1=gwb_s[:],
                     op0=ALU.mult, op1=ALU.mult, inc=[(s_ve, 1)])
        p.wait("vector", s_ve)
        p.wait("vector", s_se, silu_se)
        for c in range(NCH):
            p.op("vector", "tensor_mul", out=o_rms[:, c, :], in0=o_rms[:, c, :],
                 in1=g_s[:, c, :], inc=[(s_ve, 1)])
        p.wait("tensor", s_ve)
        for c in range(NCH):
            transpose_slab(o_rms[:, c, :], xnT, c)
        oT_ve = s_ve.n
        p.wait("tensor", wo_ready[0], wo_ready[1])
        p.wait("tensor", s_ve, oT_ve)
        for c in range(NCH):
            for half in range(2):
                bk = (c * 2 + half) % 2
                bank = pw[bk]
                guard_wait("tensor", f"pw{bk}")
                for kk in range(KC):
                    p.op("tensor", "matmul", bank[:],
                         xnT[:, kk, c * 128:(c + 1) * 128],
                         wbigB[:, kk, half * 512:(half + 1) * 512],
                         start=(kk == 0), stop=(kk == KC - 1),
                         inc=[(s_pe, 1)] if kk == KC - 1 else ())
                p.wait("vector", s_pe)
                p.op("vector", "tensor_add", out=e1_s[:], in0=bank[:],
                     in1=x_s[:, c, half * 512:(half + 1) * 512], inc=[(s_ve, 1)])
                guard_set(f"pw{bk}", s_ve)
                p.wait("vector", s_ve)
                p.op("vector", "tensor_add",
                     out=resid[:, c, half * 512:(half + 1) * 512],
                     in0=e1_s[:], in1=bob_s[:, half * 512:(half + 1) * 512])
            guard_set(f"x_s{c}", s_ve)
        guards["wbigB"] = (s_pe, s_pe.n)
        wo2_ready = wload(wbigB, wo2_d, "wbigB")
        if t + 1 < NTILE:
            xval = xload(t + 1)

        for e in range(E):
            b = e % 2
            p.wait("tensor", s_w12[b])
            mm1_done = {}
            for h2 in range(2):
                for jh in range(2):
                    g = guards.get(f"hmidS{jh}")
                    for i in range(4):
                        pb = (h2 * 2 + jh + i) % 2
                        bank = ph[pb]
                        guard_wait("tensor", f"ph{pb}")
                        for j4 in range(4):
                            j = jh * 4 + j4
                            p.op("tensor", "matmul", bank[:, j4 * 256:(j4 + 1) * 256],
                                 w1b[:, b, i * 128:(i + 1) * 128],
                                 hT[:, j, h2 * 256:(h2 + 1) * 256],
                                 start=True, stop=True,
                                 inc=[(s_pe, 1)] if j4 == 3 else ())
                        if g is not None:
                            p.wait("scalar", g[0], g[1])
                            g = None
                        p.wait("scalar", s_pe)
                        p.op("scalar", "activation",
                             out=hmidS[:, i, jh * 4:(jh + 1) * 4, :],
                             in_=bank[:], func=ACT.Gelu,
                             bias=b1l_s[:, e, i:i + 1], inc=[(s_se, 1)])
                        guard_set(f"ph{pb}", s_se)
                    mm1_done[jh] = s_se.n
                for jh in range(2):
                    p.wait("tensor", s_se, mm1_done[jh])
                    for j4 in range(4):
                        j = jh * 4 + j4
                        bk = j % 2
                        bank = pw[bk]
                        guard_wait("tensor", f"pw{bk}")
                        for c2 in range(2):
                            for i in range(4):
                                p.op("tensor", "matmul",
                                     bank[:, c2 * 128:(c2 + 1) * 128],
                                     hmidS[:, i, j, c2 * 128:(c2 + 1) * 128],
                                     w2b[:, b, i, :], start=(i == 0), stop=(i == 3),
                                     inc=[(s_pe, 1)] if i == 3 else ())
                        p.wait("vector", s_pe)
                        for c2 in range(2):
                            cc = h2 * 2 + c2
                            p.op("vector", "scalar_tensor_tensor",
                                 out=acc[:, cc, j * 128:(j + 1) * 128],
                                 in0=bank[:, c2 * 128:(c2 + 1) * 128],
                                 scalar=wu_s[:, cc, j, e:e + 1],
                                 in1=acc[:, cc, j * 128:(j + 1) * 128],
                                 op0=ALU.mult, op1=ALU.add,
                                 inc=[(s_ve, 1)] if c2 == 1 else ())
                        guard_set(f"pw{bk}", s_ve)
                    guard_set(f"hmidS{jh}", s_ve)
            guard_set(f"w1b{b}", s_pe)
            if e + 2 < E:
                w12load(e + 2)

        p.wait("vector", s_ve)
        for c in range(NCH):
            p.op("vector", "tensor_copy", out=comb_bf[:, c, :], in_=acc[:, c, :],
                 inc=[(s_ve, 1)])
        p.wait("tensor", s_ve)
        for c in range(NCH):
            transpose_slab(comb_bf[:, c, :], xnT, c)
        combT_ve = s_ve.n
        p.wait("tensor", wo2_ready[0], wo2_ready[1])
        p.wait("tensor", s_ve, combT_ve)
        for c in range(NCH):
            for half in range(2):
                bk = (c * 2 + half) % 2
                bank = pw[bk]
                guard_wait("tensor", f"pw{bk}")
                for kk in range(KC):
                    p.op("tensor", "matmul", bank[:],
                         xnT[:, kk, c * 128:(c + 1) * 128],
                         wbigB[:, kk, half * 512:(half + 1) * 512],
                         start=(kk == 0), stop=(kk == KC - 1),
                         inc=[(s_pe, 1)] if kk == KC - 1 else ())
                ost = (c * 2 + half) % 2
                s_dox = [s_doA, s_doB][ost]
                guard_wait("vector", f"outst{ost}")
                p.wait("vector", s_pe)
                p.op("vector", "tensor_add", out=outst[:, ost, :], in0=bank[:],
                     in1=resid[:, c, half * 512:(half + 1) * 512], inc=[(s_ve, 1)])
                guard_set(f"pw{bk}", s_ve)
                p.wait("sync", s_ve)
                p.op("sync", "dma_start",
                     out=out_d[t * TT + c * 128: t * TT + (c + 1) * 128,
                               half * 512:(half + 1) * 512],
                     in_=outst[:, ost, :], inc=[(s_dox, 16)])
                guard_set(f"outst{ost}", s_dox)
        guards["wbigB"] = (s_pe, s_pe.n)

        if t == 0 and tap_d:
            srcs = {"q": qT[:], "k": kT[:], "gksp": gksp[:], "v": v_s[:],
                    "g": g_s[:], "h": hT[:], "wu": wu_s[:], "rsum": rsum_s[:],
                    "S": S_s[:], "orms": o_rms[:], "resid": resid[:],
                    "acc": acc[:], "xnt": xnT[:], "sums": sums_s[:],
                    "smc": sm_c[:], "xs": x_s[:],
                    "smc": sm_c[:]}
            p.wait("sync", s_ve)
            p.wait("sync", s_se)
            for name, dst in tap_d.items():
                p.op("sync", "dma_start", out=dst, in_=srcs[name],
                     inc=[(s_dt, 16)])
                p.wait("sync", s_dt)

    p.wait("sync", s_doA, s_doA.n)
    p.wait("sync", s_doB, s_doB.n)
    p.wait("sync", s_dt, s_dt.n)
    p.wait("gpsimd", s_wA, s_wA.n)
    p.wait("gpsimd", s_wB, s_wB.n)
    p.wait("gpsimd", s_w12[0], s_w12[0].n)
    p.wait("gpsimd", s_w12[1], s_w12[1].n)

    block = es.enter_context(nc.Block())
    p.run(block)
    es.close()
    return nc


_NC_CACHE = {}


def _get_nc(taps=()):
    key = tuple(sorted(taps))
    if key not in _NC_CACHE:
        _NC_CACHE[key] = build_nc(taps=key)
    return _NC_CACHE[key]


def make_core_inputs(inputs):
    f = np.ascontiguousarray
    shared = {
        "wq": f(np.asarray(inputs["L0_attn_Wq"], np.float32)),
        "wk": f(np.asarray(inputs["L0_attn_Wk"], np.float32)),
        "wv": f(np.asarray(inputs["L0_attn_Wv"], np.float32)),
        "wg": f(np.asarray(inputs["L0_attn_Wg"], np.float32)),
        "wo": f(np.asarray(inputs["L0_attn_Wo"], np.float32)),
        "wm": f(np.asarray(inputs["L0_ffn_Wm"], np.float32)),
        "wo2": f(np.asarray(inputs["L0_ffn_Wo"], np.float32)),
        "wgk1": f(np.asarray(inputs["L0_attn_Wgk1"], np.float32)),
        "wgk2": f(np.asarray(inputs["L0_attn_Wgk2"], np.float32)),
        "w1": f(np.asarray(inputs["L0_ffn_W1"], np.float32)),
        "w2": f(np.asarray(inputs["L0_ffn_W2"], np.float32)),
        "emb": f(np.asarray(inputs["L0_ffn_expert_emb"], np.float32)),
        "b2": f(np.asarray(inputs["L0_ffn_b2"], np.float32)),
        "bgk2l": f(np.asarray(inputs["L0_attn_bgk2"], np.float32).reshape(4, 128).T),
        "bml": f(np.asarray(inputs["L0_ffn_bm"], np.float32).reshape(8, 128).T),
        "b1l": f(np.asarray(inputs["L0_ffn_b1"], np.float32).reshape(E, 4, 128).transpose(2, 0, 1)),
        "bob": f(np.broadcast_to(np.asarray(inputs["L0_ffn_bo"], np.float32), (128, D))),
        "gwb": f(np.broadcast_to(np.asarray(inputs["L0_attn_gw"], np.float32), (128, dv))),
        "ident": np.eye(128, dtype=ml_dtypes.bfloat16),
        "identf": np.eye(128, dtype=np.float32),
        "trimask": f(np.triu(np.ones((128, 128))).astype(ml_dtypes.bfloat16)),
    }
    x = np.asarray(inputs["L0_attn_x"], np.float32)
    in_maps = []
    for core in range(8):
        m = dict(shared)
        m["x"] = f(x[core])
        in_maps.append(m)
    return in_maps


def kernel(**inputs):
    nc = _get_nc()
    in_maps = make_core_inputs(inputs)
    res = run_bass_kernel_spmd(nc, in_maps, core_ids=list(range(8)))
    return np.stack([r["out"] for r in res.results])


# version 10
# speedup vs baseline: 1.0566x; 1.0104x over previous
import numpy as np
import ml_dtypes
import contextlib
import concourse.bass as bass
import concourse.mybir as mybir
from concourse.bass_utils import run_bass_kernel_spmd

F32 = mybir.dt.float32
BF16 = mybir.dt.bfloat16
ACT = mybir.ActivationFunctionType
ALU = mybir.AluOpType
AX = mybir.AxisListType

T, D = 2048, 1024
NTILE, TT, C, NCH = 4, 512, 128, 4
NH, dk, dv, KC = 4, 128, 256, 8
H, HD, E, I = 8, 128, 16, 512
EPS = 1e-5
LNSCALE = float(np.log(128.0 ** -0.5))
ENGS = ("tensor", "vector", "scalar", "gpsimd", "sync")


class Sem:
    __slots__ = ("h", "n")

    def __init__(self, h):
        self.h = h
        self.n = 0


import os
SERIAL = bool(int(os.environ.get("KERNEL_SERIAL", "0")))


class Prog:
    def __init__(self, nc, es):
        self.nc = nc
        self.es = es
        self.ops = {e: [] for e in ENGS}
        self.esem = {}

    def sem(self, name):
        return Sem(self.es.enter_context(self.nc.semaphore(name)))

    def wait(self, eng, sem, val=None):
        v = sem.n if val is None else val
        assert v <= sem.n, f"wait on future increment ({v} > {sem.n})"
        if v <= 0 or SERIAL:
            return
        self.ops[eng].append(lambda e: e.wait_ge(sem.h, v))

    def op(self, eng, meth, *a, inc=(), **kw):
        inc = list(inc)
        if SERIAL:
            for s, k in inc:
                s.n += k
            for oe, os_ in self.esem.items():
                if os_.n > 0:
                    v = os_.n
                    self.ops[eng].append(
                        (lambda sem, vv: lambda e: e.wait_ge(sem.h, vv))(os_, v))
            own = self.esem[eng]
            inc = [(own, 16 if meth == "dma_start" else 1)]
        for s, k in inc:
            s.n += k

        def f(e):
            inst = getattr(e, meth)(*a, **kw)
            for s, k in inc:
                inst.then_inc(s.h, k)
        self.ops[eng].append(f)

    def run(self, block):
        for eng in ENGS:
            oplist = self.ops[eng]

            def make(lst):
                def body(e):
                    for f in lst:
                        f(e)
                return body
            getattr(block, eng)(make(oplist))


def build_nc(taps=()):
    nc = bass.Bass("TRN2", target_bir_lowering=False)
    dr = lambda n, s, d=F32, k="ExternalInput": nc.dram_tensor(n, s, d, kind=k).ap()
    x_d = dr("x", [T, D])
    wq_d = dr("wq", [D, 512])
    wk_d = dr("wk", [D, 512])
    wv_d = dr("wv", [D, D])
    wg_d = dr("wg", [D, D])
    wo_d = dr("wo", [D, D])
    wm_d = dr("wm", [D, D])
    wo2_d = dr("wo2", [D, D])
    wgk1_d = dr("wgk1", [D, 16])
    wgk2_d = dr("wgk2", [16, 512])
    w1_d = dr("w1", [E, HD, I])
    w2_d = dr("w2", [E, I, HD])
    emb_d = dr("emb", [HD, E])
    b2_d = dr("b2", [E, HD])
    bgk2l_d = dr("bgk2l", [128, 4])
    bml_d = dr("bml", [128, 8])
    b1l_d = dr("b1l", [128, E, 4])
    bob_d = dr("bob", [128, D])
    gwb_d = dr("gwb", [128, dv])
    ident_d = dr("ident", [128, 128], BF16)
    identf_d = dr("identf", [128, 128])
    trimask_d = dr("trimask", [128, 128], BF16)
    out_d = dr("out", [T, D], F32, "ExternalOutput")
    tap_shapes = {"q": ([128, 4, TT], BF16), "k": ([128, 4, TT], BF16),
                  "gksp": ([128, 4, TT], F32), "v": ([128, NCH, D], BF16),
                  "g": ([128, NCH, D], BF16), "h": ([128, H, TT], BF16),
                  "wu": ([128, NCH, H, 32], F32), "rsum": ([128, NCH, H], F32),
                  "S": ([128, NH, dv], F32), "orms": ([128, NCH, D], BF16),
                  "resid": ([128, NCH, D], BF16), "acc": ([128, NCH, D], F32),
                  "xnt": ([128, KC, TT], BF16), "sums": ([128, NCH, H], F32),
                  "lgd": ([128, NCH, 128], F32), "smc": ([128, 128], F32),
                  "xs": ([128, NCH, D], F32), "lnst": ([128, NCH, 8], F32)}
    tap_d = {tp: dr("tap_" + tp, *tap_shapes[tp], k="ExternalOutput") for tp in taps}

    es = contextlib.ExitStack()
    sb = lambda n, s, d=F32: es.enter_context(nc.sbuf_tensor(n, s, d))

    wq_s = sb("wq_s", [128, KC, 512], BF16)
    wk_s = sb("wk_s", [128, KC, 512], BF16)
    wgk1_s = sb("wgk1_s", [128, KC, 16], BF16)
    wgk2_s = sb("wgk2_s", [16, 4, 128], BF16)
    emb_s = sb("emb_s", [128, E], BF16)
    b2_s = sb("b2_s", [16, 128], BF16)
    b1l_s = sb("b1l_s", [128, E, 4])
    bml_s = sb("bml_s", [128, 8])
    bgk2r_s = sb("bgk2r_s", [128, 4])
    bgk2n_s = sb("bgk2n_s", [128, 4])
    bob_s = sb("bob_s", [128, D])
    gwb_s = sb("gwb_s", [128, dv])
    ident_s = sb("ident_s", [128, 128], BF16)
    identf_s = sb("identf_s", [128, 128])
    trimask_s = sb("trimask_s", [128, 128], BF16)
    ones_s = sb("ones_s", [128, 128])
    ones_s2 = sb("ones_s2", [128, 512])
    consts_s = sb("consts_s", [128, 3])
    wbigA = sb("wbigA", [128, KC, D], BF16)
    wbigB = sb("wbigB", [128, KC, D], BF16)
    w1b = sb("w1b", [128, 2, I], BF16)
    w2b = sb("w2b", [128, 2, 4, HD], BF16)
    x_s = sb("x_s", [128, NCH, D])
    xn_tm = sb("xn_tm", [128, D], BF16)
    xnT = sb("xnT", [128, KC, TT], BF16)
    qT = sb("qT", [128, NH, TT], BF16)
    kT = sb("kT", [128, NH, TT], BF16)
    gksp = sb("gksp", [128, NCH, NH, 128])
    rstd_s = sb("rstd_s", [128, 16])
    v_s = sb("v_s", [128, NCH, D], BF16)
    g_s = sb("g_s", [128, NCH, D], BF16)
    hT = sb("hT", [128, H, TT], BF16)
    gk1b = sb("gk1b", [16, TT], BF16)
    e1_s = sb("e1_s", [128, 512])
    sm_a = sb("sm_a", [128, 128])
    sm_b = sb("sm_b", [128, 128])
    sm_c = sb("sm_c", [128, 128])
    wu_s = sb("wu_s", [128, NCH, H, 32])
    wuT_s = sb("wuT_s", [16, H, 128], BF16)
    rsum_s = sb("rsum_s", [128, NCH, H])
    sums_s = sb("sums_s", [128, NCH, H])
    stats = sb("stats", [128, 32])
    S_s = sb("S_s", [128, NH, dv])
    S_bf = sb("S_bf", [128, NH, dv], BF16)
    A4 = sb("A4", [128, NH, 128])
    lam4 = sb("lam4", [128, NH, 128])
    Pb4 = sb("Pb4", [128, NH, 128], BF16)
    khT_s = sb("khT_s", [128, NCH, NH, 128], BF16)
    o_rms = sb("o_rms", [128, NCH, D], BF16)
    resid = sb("resid", [128, NCH, D], BF16)
    acc = sb("acc", [128, NCH, D])
    comb_bf = v_s
    hmidS = sb("hmidS", [128, 4, H, 256], BF16)
    kh4 = hmidS.rearrange("p a h n -> p (a h n)")[:, 0:NCH * NH * 128] \
        .rearrange("p (c h n) -> p c h n", c=NCH, h=NH)
    outst = sb("outst", [128, 2, 512])

    ph = [es.enter_context(nc.psum_tensor(f"ph{i}", [128, 1024], F32)) for i in range(2)]
    pw = [es.enter_context(nc.psum_tensor(f"pw{i}", [128, 512], F32)) for i in range(3)]
    ptp = es.enter_context(nc.psum_tensor("ptp", [128, 1024], BF16))

    p = Prog(nc, es)
    s_dx = p.sem("s_dx")
    s_di = p.sem("s_di")
    s_winit = p.sem("s_winit")
    s_wA = p.sem("s_wA")
    s_wB = p.sem("s_wB")
    s_w12 = [p.sem("s_w12a"), p.sem("s_w12b")]
    s_doA = p.sem("s_doA")
    s_doB = p.sem("s_doB")
    s_dt = p.sem("s_dt")
    s_pe = p.sem("s_pe")
    s_ve = p.sem("s_ve")
    s_se = p.sem("s_se")
    if SERIAL:
        p.esem = {e: p.sem("ser_" + e) for e in ENGS}

    guards = {}

    def guard_wait(eng, key):
        g = guards.get(key)
        if g is not None:
            p.wait(eng, g[0], g[1])

    def guard_set(key, sem):
        guards[key] = (sem, sem.n)

    for dst, src in [(b1l_s, b1l_d), (bml_s, bml_d), (bgk2r_s, bgk2l_d),
                     (bob_s, bob_d), (gwb_s, gwb_d), (ident_s, ident_d),
                     (identf_s, identf_d), (trimask_s, trimask_d)]:
        p.op("sync", "dma_start", out=dst[:], in_=src, inc=[(s_di, 16)])
    p.op("gpsimd", "dma_start", out=wq_s[:],
         in_=wq_d.rearrange("(kk p) n -> p kk n", p=128), inc=[(s_winit, 16)])
    p.op("gpsimd", "dma_start", out=wk_s[:],
         in_=wk_d.rearrange("(kk p) n -> p kk n", p=128), inc=[(s_winit, 16)])
    p.op("gpsimd", "dma_start", out=wgk1_s[:],
         in_=wgk1_d.rearrange("(kk p) n -> p kk n", p=128), inc=[(s_winit, 16)])
    p.op("gpsimd", "dma_start", out=wgk2_s[:],
         in_=wgk2_d.rearrange("k (m n) -> k m n", m=4), inc=[(s_winit, 16)])
    p.op("gpsimd", "dma_start", out=emb_s[:], in_=emb_d, inc=[(s_winit, 16)])
    p.op("gpsimd", "dma_start", out=b2_s[:], in_=b2_d, inc=[(s_winit, 16)])
    p.wait("vector", s_di)
    p.wait("tensor", s_di)
    p.wait("tensor", s_winit)
    p.wait("scalar", s_di)
    p.op("vector", "memset", wu_s[:], 0.0)
    p.op("vector", "memset", S_s[:], 0.0)
    p.op("vector", "memset", S_bf[:], 0.0)
    p.op("vector", "memset", ones_s[:], 1.0)
    p.op("vector", "memset", ones_s2[:], 1.0)
    p.op("vector", "memset", consts_s[:, 0:1], EPS)
    p.op("vector", "memset", consts_s[:, 1:2], 1.0)
    p.op("vector", "memset", consts_s[:, 2:3], LNSCALE)
    p.op("vector", "memset", stats[:, 8:9], 0.0)
    p.op("vector", "tensor_scalar_mul", out=bgk2n_s[:], in0=bgk2r_s[:],
         scalar1=-1.0, inc=[(s_ve, 1)])

    wsem = {"wbigA": s_wA, "wbigB": s_wB}

    def wload(dst_s, src_d, key):
        guard_wait("gpsimd", key)
        sem = wsem[key]
        p.op("gpsimd", "dma_start", out=dst_s[:],
             in_=src_d.rearrange("(kk p) n -> p kk n", p=128), inc=[(sem, 16)])
        return (sem, sem.n)

    def w12load(e):
        b = e % 2
        guard_wait("gpsimd", f"w1b{b}")
        p.op("gpsimd", "dma_start", out=w1b[:, b, :], in_=w1_d[e], inc=[(s_w12[b], 16)])
        p.op("gpsimd", "dma_start", out=w2b[:, b, :, :],
             in_=w2_d[e].rearrange("(i p) n -> p i n", p=128), inc=[(s_w12[b], 16)])

    def transpose_slab(src2d, dstT, c, nblk=8):
        guard_wait("tensor", "ptp")
        for i in range(nblk):
            p.op("tensor", "transpose", ptp[:, i * 128:(i + 1) * 128],
                 src2d[:, i * 128:(i + 1) * 128], ident_s[:],
                 inc=[(s_pe, 1)] if i == nblk - 1 else ())
        p.wait("vector", s_pe)
        p.op("vector", "tensor_copy",
             out=dstT[:, 0:nblk, c * 128:(c + 1) * 128],
             in_=ptp.rearrange("p (i n) -> p i n", i=8)[:, 0:nblk, :],
             inc=[(s_ve, 1)])
        guard_set("ptp", s_ve)

    sbf_se = {h: 0 for h in range(NH)}
    gate_ve = {}
    gate_se = {}
    kh_ve_c = {}
    gbank = [0]

    def xload(t):
        for c in range(NCH):
            guard_wait("sync", f"x_s{c}")
            p.op("sync", "dma_start", out=x_s[:, c, :],
                 in_=x_d[t * TT + c * 128: t * TT + (c + 1) * 128, :],
                 inc=[(s_dx, 16)])
        return s_dx.n

    xval = xload(0)
    wv_ready = wload(wbigA, wv_d, "wbigA")
    wg_ready = wload(wbigB, wg_d, "wbigB")
    ln_ve = {}

    def ln_chunk(c, xv):
        p.wait("scalar", s_dx, xv)
        p.wait("scalar", s_ve)
        p.op("scalar", "activation", out=xn_tm[:], in_=x_s[:, c, :],
             func=ACT.Square, accum_out=stats[:, 0:1], inc=[(s_se, 1)])
        p.wait("vector", s_dx, xv)
        p.op("vector", "reduce_sum", out=stats[:, 1:2], in_=x_s[:, c, :], axis=AX.X,
             inc=[(s_ve, 1)])
        p.wait("vector", s_ve)
        p.op("vector", "tensor_scalar_mul", out=stats[:, 2:3], in0=stats[:, 1:2],
             scalar1=1.0 / D, inc=[(s_ve, 1)])
        p.wait("vector", s_ve)
        p.op("vector", "tensor_mul", out=stats[:, 3:4], in0=stats[:, 2:3],
             in1=stats[:, 2:3], inc=[(s_ve, 1)])
        p.wait("vector", s_ve)
        p.wait("vector", s_se)
        p.op("vector", "tensor_scalar", out=stats[:, 4:5], in0=stats[:, 0:1],
             scalar1=1.0 / D, scalar2=stats[:, 3:4], op0=ALU.mult,
             op1=ALU.subtract, inc=[(s_ve, 1)])
        p.wait("scalar", s_ve)
        p.op("scalar", "activation", out=stats[:, 5:6], in_=stats[:, 4:5],
             func=ACT.Ln, bias=consts_s[:, 0:1], inc=[(s_se, 1)])
        p.wait("scalar", s_se)
        p.op("scalar", "activation", out=stats[:, 6:7], in_=stats[:, 5:6],
             func=ACT.Exp, scale=-0.5, inc=[(s_se, 1)])
        p.wait("vector", s_se)
        p.wait("vector", s_pe)
        p.op("vector", "tensor_scalar", out=o_rms[:, c, :], in0=x_s[:, c, :],
             scalar1=stats[:, 2:3], scalar2=stats[:, 6:7],
             op0=ALU.subtract, op1=ALU.mult, inc=[(s_ve, 1)])
        ln_ve[c] = s_ve.n

    for c in range(NCH):
        ln_chunk(c, xval)

    for t in range(NTILE):
        for c in range(NCH):
            p.wait("tensor", s_ve, ln_ve[c])
            transpose_slab(o_rms[:, c, :], xnT, c)
        xn_done = s_ve.n

        def proj_fm(wsb, dstT, mrange, bias_l=None, evac_eng="vector"):
            for mi, m in enumerate(mrange):
                bk = mi % 2
                bank = pw[bk]
                guard_wait("tensor", f"pw{bk}")
                for kk in range(KC):
                    p.op("tensor", "matmul", bank[:],
                         wsb[:, kk, m * 128:(m + 1) * 128], xnT[:, kk, :],
                         start=(kk == 0), stop=(kk == KC - 1),
                         inc=[(s_pe, 1)] if kk == KC - 1 else ())
                if evac_eng == "vector":
                    p.wait("vector", s_pe)
                    p.op("vector", "tensor_copy", out=dstT[:, m, :], in_=bank[:],
                         inc=[(s_ve, 1)])
                    guard_set(f"pw{bk}", s_ve)
                else:
                    p.wait("scalar", s_pe)
                    p.op("scalar", "activation", out=dstT[:, m, :], in_=bank[:],
                         func=ACT.Identity, bias=bias_l[:, m:m + 1], inc=[(s_se, 1)])
                    guard_set(f"pw{bk}", s_se)

        p.wait("tensor", s_ve, xn_done)
        proj_fm(wq_s, qT, range(NH))
        proj_fm(wk_s, kT, range(NH))

        guard_wait("tensor", "pw2")
        for kk in range(KC):
            p.op("tensor", "matmul", pw[2][0:16, 0:TT], wgk1_s[:, kk, :],
                 xnT[:, kk, :], start=(kk == 0), stop=(kk == KC - 1),
                 inc=[(s_pe, 1)] if kk == KC - 1 else ())
        p.wait("vector", s_pe)
        p.op("vector", "tensor_copy", out=gk1b[:], in_=pw[2][0:16, 0:TT],
             inc=[(s_ve, 1)])
        guard_set("pw2", s_ve)
        gk1_ve = s_ve.n
        for m in range(4):
            bk = m % 2
            bank = pw[bk]
            guard_wait("tensor", f"pw{bk}")
            p.wait("tensor", s_ve, gk1_ve)
            p.op("tensor", "matmul", bank[:], wgk2_s[:, m, :], gk1b[:],
                 start=True, stop=True, inc=[(s_pe, 1)])
            p.wait("scalar", s_pe)
            p.op("scalar", "activation", out=e1_s[:], in_=bank[:],
                 func=ACT.Exp, scale=-1.0, bias=bgk2n_s[:, m:m + 1], inc=[(s_se, 1)])
            p.wait("scalar", s_se)
            p.op("scalar", "activation", out=gksp[:, :, m, :], in_=e1_s[:],
                 func=ACT.Ln, bias=consts_s[:, 1:2], inc=[(s_se, 1)])
            guard_set(f"pw{bk}", s_se)

        def proj_tm(wsb, dst, wready):
            p.wait("tensor", wready[0], wready[1])
            for ci in range(NCH):
                for half in range(2):
                    bk = (ci * 2 + half) % 2
                    bank = pw[bk]
                    guard_wait("tensor", f"pw{bk}")
                    for kk in range(KC):
                        p.op("tensor", "matmul", bank[:],
                             xnT[:, kk, ci * 128:(ci + 1) * 128],
                             wsb[:, kk, half * 512:(half + 1) * 512],
                             start=(kk == 0), stop=(kk == KC - 1),
                             inc=[(s_pe, 1)] if kk == KC - 1 else ())
                    p.wait("scalar", s_pe)
                    p.op("scalar", "copy",
                         out=dst[:, ci, half * 512:(half + 1) * 512],
                         in_=bank[:], inc=[(s_se, 1)])
                    guard_set(f"pw{bk}", s_se)

        gk_done_se = s_se.n
        for c in range(NCH):
            cs = slice(c * 128, (c + 1) * 128)
            A2 = A4.rearrange("p h n -> p (h n)")
            p.wait("vector", s_se, gk_done_se)
            p.op("vector", "tensor_tensor_scan", out=A2, data0=ones_s2[:],
                 data1=gksp[:, c].rearrange("p h n -> p (h n)"),
                 initial=0.0, op0=ALU.mult, op1=ALU.add, inc=[(s_ve, 1)])
            p.wait("vector", s_ve)
            p.op("vector", "tensor_copy", out=stats[:, 9:12],
                 in_=A4[:, 0:3, 127], inc=[(s_ve, 1)])
            p.wait("vector", s_ve)
            pcb = stats[:, 8:12].rearrange("p (h o) -> p h o", h=NH).broadcast_to([128, NH, 128])
            p.op("vector", "tensor_tensor", out=A4[:], in0=A4[:], in1=pcb,
                 op=ALU.subtract, inc=[(s_ve, 1)])
            A_ve = s_ve.n
            p.wait("scalar", s_ve, A_ve)
            p.op("scalar", "activation", out=stats[:, 12:16], in_=A4[:, :, 127],
                 func=ACT.Exp, scale=-1.0 / 16.0, inc=[(s_se, 1)])
            p.op("scalar", "activation", out=lam4.rearrange("p h n -> p (h n)"),
                 in_=A2, func=ACT.Exp, scale=1.0 / 16.0, inc=[(s_se, 1)])
            p.wait("vector", s_se)
            p.op("vector", "tensor_mul", out=kT[:, :, cs], in0=kT[:, :, cs],
                 in1=lam4[:], inc=[(s_ve, 1)])
            kt_ve = s_ve.n
            p.wait("vector", s_ve)
            dsb = stats[:, 12:16].rearrange("p (h o) -> p h o", h=NH).broadcast_to([128, NH, 128])
            if c == 0:
                guard_wait("vector", "hmidS0")
                guard_wait("vector", "hmidS1")
            p.op("vector", "tensor_tensor", out=kh4[:, c], in0=kT[:, :, cs], in1=dsb,
                 op=ALU.mult, inc=[(s_ve, 1)])
            kh_ve_c[c] = s_ve.n
            p.wait("scalar", s_ve, kt_ve)
            p.op("scalar", "activation", out=lam4.rearrange("p h n -> p (h n)"),
                 in_=A2, func=ACT.Exp, scale=-1.0 / 16.0, bias=consts_s[:, 2:3],
                 inc=[(s_se, 1)])
            p.wait("vector", s_se)
            p.op("vector", "tensor_mul", out=qT[:, :, cs], in0=qT[:, :, cs],
                 in1=lam4[:], inc=[(s_ve, 1)])
            gate_ve[c] = s_ve.n
            gate_se[c] = s_se.n

        proj_tm(wbigA, v_s, wv_ready)
        guards["wbigA"] = (s_pe, s_pe.n)
        wm_ready = wload(wbigA, wm_d, "wbigA")
        for c in range(NCH):
            guard_wait("tensor", "ptp")
            p.wait("tensor", s_ve, kh_ve_c[c])
            for h in range(NH):
                p.op("tensor", "transpose", ptp[:, h * 128:(h + 1) * 128],
                     kh4[:, c, h, :], ident_s[:],
                     inc=[(s_pe, 1)] if h == NH - 1 else ())
            p.wait("vector", s_pe)
            p.op("vector", "tensor_copy", out=khT_s[:, c, :, :],
                 in_=ptp.rearrange("p (i n) -> p i n", i=8)[:, 0:NH, :],
                 inc=[(s_ve, 1)])
            guard_set("ptp", s_ve)
        guards["hmidS0"] = guards["hmidS1"] = (s_ve, s_ve.n)
        proj_tm(wbigB, g_s, wg_ready)
        guards["wbigB"] = (s_pe, s_pe.n)
        wo_ready = wload(wbigB, wo_d, "wbigB")
        p.wait("tensor", wm_ready[0], wm_ready[1])
        proj_fm(wbigA, hT, range(H), bias_l=bml_s, evac_eng="scalar")
        guards["wbigA"] = (s_pe, s_pe.n)
        if t + 1 < NTILE:
            wv_ready = wload(wbigA, wv_d, "wbigA")

        h_done_se = s_se.n
        for c in range(NCH):
            guard_wait("tensor", "pw2")
            p.wait("tensor", s_se, h_done_se)
            for j in range(H):
                p.op("tensor", "matmul", pw[2][:, j * 16:(j + 1) * 16],
                     hT[:, j, c * 128:(c + 1) * 128], emb_s[:],
                     start=True, stop=True,
                     inc=[(s_pe, 1)] if j == H - 1 else ())
            lg = pw[2][:, 0:128].rearrange("p (j e) -> p j e", j=H)
            p.wait("vector", s_pe)
            p.op("vector", "tensor_reduce", out=sm_a[:, 0:8], in_=lg, axis=AX.X,
                 op=ALU.max, negate=True, inc=[(s_ve, 1)])
            p.wait("vector", s_ve)
            nmx = sm_a[:, 0:8].rearrange("p (j o) -> p j o", j=H).broadcast_to([128, H, 16])
            p.wait("vector", s_se)
            p.op("vector", "tensor_tensor",
                 out=sm_b[:].rearrange("p (j e) -> p j e", j=H), in0=lg, in1=nmx,
                 op=ALU.add, inc=[(s_ve, 1)])
            guard_set("pw2", s_ve)
            p.wait("scalar", s_ve)
            p.op("scalar", "activation", out=sm_c[:], in_=sm_b[:], func=ACT.Exp,
                 inc=[(s_se, 1)])
            p.wait("vector", s_se)
            w3 = sm_c[:].rearrange("p (j e) -> p j e", j=H)
            p.op("vector", "tensor_reduce", out=sums_s[:, c, :], in_=w3, axis=AX.X,
                 op=ALU.add, inc=[(s_ve, 1)])
            p.wait("vector", s_ve)
            p.op("vector", "reciprocal", out=rsum_s[:, c, :], in_=sums_s[:, c, :])
            p.op("vector", "tensor_reduce", out=sm_b[:, 0:8], in_=w3, axis=AX.X,
                 op=ALU.max, inc=[(s_ve, 1)])
            p.wait("vector", s_ve)
            m1b = sm_b[:, 0:8].rearrange("p (j o) -> p j o", j=H).broadcast_to([128, H, 16])
            p.op("vector", "tensor_tensor",
                 out=sm_a[:].rearrange("p (j e) -> p j e", j=H), in0=w3, in1=m1b,
                 op=ALU.is_lt, inc=[(s_ve, 1)])
            p.wait("vector", s_ve)
            p.op("vector", "tensor_tensor",
                 out=sm_a[:].rearrange("p (j e) -> p j e", j=H), in0=w3,
                 in1=sm_a[:].rearrange("p (j e) -> p j e", j=H), op=ALU.mult,
                 inc=[(s_ve, 1)])
            p.wait("vector", s_ve)
            p.op("vector", "tensor_reduce", out=sm_b[:, 0:8],
                 in_=sm_a[:].rearrange("p (j e) -> p j e", j=H), axis=AX.X,
                 op=ALU.max, inc=[(s_ve, 1)])
            p.wait("vector", s_ve)
            m2b = sm_b[:, 0:8].rearrange("p (j o) -> p j o", j=H).broadcast_to([128, H, 16])
            p.op("vector", "tensor_tensor",
                 out=sm_a[:].rearrange("p (j e) -> p j e", j=H),
                 in0=w3, in1=m2b, op=ALU.is_ge, inc=[(s_ve, 1)])
            p.wait("vector", s_ve)
            p.op("vector", "tensor_tensor",
                 out=sm_b[:].rearrange("p (j e) -> p j e", j=H), in0=w3,
                 in1=sm_a[:].rearrange("p (j e) -> p j e", j=H),
                 op=ALU.mult, inc=[(s_ve, 1)])
            p.wait("vector", s_ve)
            rsb = rsum_s[:, c, :].rearrange("p (j o) -> p j o", j=H).broadcast_to([128, H, 16])
            p.op("vector", "tensor_tensor",
                 out=wu_s[:, c, :, 0:16],
                 in0=sm_b[:].rearrange("p (j e) -> p j e", j=H), in1=rsb,
                 op=ALU.mult, inc=[(s_ve, 1)])
            wu_ve = s_ve.n
            for jj in range(2):
                guard_wait("tensor", "pw2")
                p.wait("tensor", s_ve, wu_ve)
                for j4 in range(4):
                    j = jj * 4 + j4
                    p.op("tensor", "transpose", pw[2][0:16, j4 * 128:(j4 + 1) * 128],
                         wu_s[:, c, j, 0:16], identf_s[:],
                         inc=[(s_pe, 1)] if j4 == 3 else ())
                p.wait("vector", s_pe)
                p.op("vector", "tensor_copy",
                     out=wuT_s[:, jj * 4:(jj + 1) * 4, :],
                     in_=pw[2][0:16, 0:512].rearrange("p (j n) -> p j n", j=4),
                     inc=[(s_ve, 1)])
                guard_set("pw2", s_ve)
            wuT_ve = s_ve.n
            for jj in range(2):
                guard_wait("tensor", "pw2")
                p.wait("tensor", s_ve, wuT_ve)
                for j4 in range(4):
                    j = jj * 4 + j4
                    p.op("tensor", "matmul", pw[2][:, j4 * 128:(j4 + 1) * 128],
                         wuT_s[:, j, :], b2_s[:], start=True, stop=True,
                         inc=[(s_pe, 1)] if j4 == 3 else ())
                p.wait("vector", s_pe)
                p.op("vector", "tensor_copy",
                     out=acc[:, c, jj * 512:(jj + 1) * 512],
                     in_=pw[2][:, 0:512], inc=[(s_ve, 1)])
                guard_set("pw2", s_ve)

        def head_gen(c, h):
            if True:
                ch = c * NH + h
                cs = slice(c * 128, (c + 1) * 128)
                vsl = v_s[:, c, h * dv:(h + 1) * dv]
                bk = gbank[0] % 3
                gbank[0] += 1
                bank = pw[bk]
                guard_wait("tensor", f"pw{bk}")
                p.wait("tensor", s_ve, gate_ve[c])
                p.wait("tensor", s_se, sbf_se[h])
                p.op("tensor", "matmul", bank[:, 256:384], kT[:, h, cs],
                     qT[:, h, cs], start=True, stop=True)
                p.op("tensor", "matmul", bank[:, 0:dv], qT[:, h, cs],
                     S_bf[:, h, :], start=True, stop=False, inc=[(s_pe, 1)])
                p.wait("vector", s_pe)
                p.op("vector", "tensor_mul", out=Pb4[:, h, :], in0=bank[:, 256:384],
                     in1=trimask_s[:], inc=[(s_ve, 1)])
                mask_ve = s_ve.n
                yield
                p.wait("tensor", s_ve, mask_ve)
                p.op("tensor", "matmul", bank[:, 0:dv], Pb4[:, h, :], vsl,
                     start=False, stop=True, inc=[(s_pe, 1)])
                o_pe = s_pe.n
                p.wait("scalar", s_pe, o_pe)
                p.op("scalar", "activation",
                     out=o_rms[:, c, h * dv:(h + 1) * dv], in_=bank[:, 0:dv],
                     func=ACT.Square, accum_out=stats[:, 16 + ch:17 + ch],
                     inc=[(s_se, 1)])
                yield
                p.wait("vector", s_se)
                p.op("vector", "tensor_copy", out=o_rms[:, c, h * dv:(h + 1) * dv],
                     in_=bank[:, 0:dv], inc=[(s_ve, 1)])
                oev_ve = s_ve.n
                yield
                p.wait("tensor", s_ve, oev_ve)
                p.op("tensor", "matmul", bank[:, 256:512], khT_s[:, c, h, :], vsl,
                     start=True, stop=True, inc=[(s_pe, 1)])
                smm_pe = s_pe.n
                p.wait("vector", s_pe, smm_pe)
                p.wait("vector", s_se, gate_se[c])
                p.op("vector", "scalar_tensor_tensor", out=S_s[:, h, :],
                     in0=S_s[:, h, :], scalar=stats[:, 12 + h:13 + h],
                     in1=bank[:, 256:512], op0=ALU.mult, op1=ALU.add,
                     inc=[(s_ve, 1)])
                guard_set(f"pw{bk}", s_ve)
                p.wait("scalar", s_ve)
                p.op("scalar", "activation", out=S_bf[:, h, :], in_=S_s[:, h, :],
                     func=ACT.Copy, inc=[(s_se, 1)])
                sbf_se[h] = s_se.n
                yield

        for c in range(NCH):
            for h in range(NH):
                for _ in head_gen(c, h):
                    pass

        w12load(0)
        w12load(1)
        p.wait("scalar", s_se)
        p.op("scalar", "activation", out=rstd_s[:], in_=stats[:, 16:32],
             func=ACT.Ln, scale=1.0 / dv, bias=consts_s[:, 0:1], inc=[(s_se, 1)])
        p.wait("scalar", s_se)
        p.op("scalar", "activation", out=stats[:, 16:32], in_=rstd_s[:],
             func=ACT.Exp, scale=-0.5, inc=[(s_se, 1)])
        exp_se = s_se.n
        for c in range(NCH):
            p.op("scalar", "activation", out=g_s[:, c, :], in_=g_s[:, c, :],
                 func=ACT.Silu, inc=[(s_se, 1)])
        silu_se = s_se.n
        p.wait("vector", s_se, exp_se)
        for c in range(NCH):
            for h in range(NH):
                ch = c * NH + h
                p.op("vector", "scalar_tensor_tensor",
                     out=o_rms[:, c, h * dv:(h + 1) * dv],
                     in0=o_rms[:, c, h * dv:(h + 1) * dv],
                     scalar=stats[:, 16 + ch:17 + ch], in1=gwb_s[:],
                     op0=ALU.mult, op1=ALU.mult, inc=[(s_ve, 1)])
        p.wait("vector", s_ve)
        p.wait("vector", s_se, silu_se)
        for c in range(NCH):
            p.op("vector", "tensor_mul", out=o_rms[:, c, :], in0=o_rms[:, c, :],
                 in1=g_s[:, c, :], inc=[(s_ve, 1)])
        p.wait("tensor", s_ve)
        for c in range(NCH):
            transpose_slab(o_rms[:, c, :], xnT, c)
        oT_ve = s_ve.n
        p.wait("tensor", wo_ready[0], wo_ready[1])
        p.wait("tensor", s_ve, oT_ve)
        for c in range(NCH):
            for half in range(2):
                bk = (c * 2 + half) % 2
                bank = pw[bk]
                guard_wait("tensor", f"pw{bk}")
                for kk in range(KC):
                    p.op("tensor", "matmul", bank[:],
                         xnT[:, kk, c * 128:(c + 1) * 128],
                         wbigB[:, kk, half * 512:(half + 1) * 512],
                         start=(kk == 0), stop=(kk == KC - 1),
                         inc=[(s_pe, 1)] if kk == KC - 1 else ())
                p.wait("vector", s_pe)
                p.op("vector", "tensor_add", out=e1_s[:], in0=bank[:],
                     in1=x_s[:, c, half * 512:(half + 1) * 512], inc=[(s_ve, 1)])
                guard_set(f"pw{bk}", s_ve)
                p.wait("vector", s_ve)
                p.op("vector", "tensor_add",
                     out=resid[:, c, half * 512:(half + 1) * 512],
                     in0=e1_s[:], in1=bob_s[:, half * 512:(half + 1) * 512])
            guard_set(f"x_s{c}", s_ve)
        guards["wbigB"] = (s_pe, s_pe.n)
        wo2_ready = wload(wbigB, wo2_d, "wbigB")
        if t + 1 < NTILE:
            xval = xload(t + 1)

        for e in range(E):
            b = e % 2
            p.wait("tensor", s_w12[b])
            mm1_done = {}
            for h2 in range(2):
                for jh in range(2):
                    g = guards.get(f"hmidS{jh}")
                    for i in range(4):
                        pb = (h2 * 2 + jh + i) % 2
                        bank = ph[pb]
                        guard_wait("tensor", f"ph{pb}")
                        for j4 in range(4):
                            j = jh * 4 + j4
                            p.op("tensor", "matmul", bank[:, j4 * 256:(j4 + 1) * 256],
                                 w1b[:, b, i * 128:(i + 1) * 128],
                                 hT[:, j, h2 * 256:(h2 + 1) * 256],
                                 start=True, stop=True,
                                 inc=[(s_pe, 1)] if j4 == 3 else ())
                        if g is not None:
                            p.wait("scalar", g[0], g[1])
                            g = None
                        p.wait("scalar", s_pe)
                        p.op("scalar", "activation",
                             out=hmidS[:, i, jh * 4:(jh + 1) * 4, :],
                             in_=bank[:], func=ACT.Gelu,
                             bias=b1l_s[:, e, i:i + 1], inc=[(s_se, 1)])
                        guard_set(f"ph{pb}", s_se)
                    mm1_done[jh] = s_se.n
                for jh in range(2):
                    p.wait("tensor", s_se, mm1_done[jh])
                    for j4 in range(4):
                        j = jh * 4 + j4
                        bk = j % 2
                        bank = pw[bk]
                        guard_wait("tensor", f"pw{bk}")
                        for c2 in range(2):
                            for i in range(4):
                                p.op("tensor", "matmul",
                                     bank[:, c2 * 128:(c2 + 1) * 128],
                                     hmidS[:, i, j, c2 * 128:(c2 + 1) * 128],
                                     w2b[:, b, i, :], start=(i == 0), stop=(i == 3),
                                     inc=[(s_pe, 1)] if i == 3 else ())
                        p.wait("vector", s_pe)
                        for c2 in range(2):
                            cc = h2 * 2 + c2
                            p.op("vector", "scalar_tensor_tensor",
                                 out=acc[:, cc, j * 128:(j + 1) * 128],
                                 in0=bank[:, c2 * 128:(c2 + 1) * 128],
                                 scalar=wu_s[:, cc, j, e:e + 1],
                                 in1=acc[:, cc, j * 128:(j + 1) * 128],
                                 op0=ALU.mult, op1=ALU.add,
                                 inc=[(s_ve, 1)] if c2 == 1 else ())
                        guard_set(f"pw{bk}", s_ve)
                    guard_set(f"hmidS{jh}", s_ve)
            guard_set(f"w1b{b}", s_pe)
            if e + 2 < E:
                w12load(e + 2)

        p.wait("vector", s_ve)
        for c in range(NCH):
            p.op("vector", "tensor_copy", out=comb_bf[:, c, :], in_=acc[:, c, :],
                 inc=[(s_ve, 1)])
        p.wait("tensor", s_ve)
        for c in range(NCH):
            transpose_slab(comb_bf[:, c, :], xnT, c)
        combT_ve = s_ve.n
        p.wait("tensor", wo2_ready[0], wo2_ready[1])
        p.wait("tensor", s_ve, combT_ve)
        for c in range(NCH):
            for half in range(2):
                bk = (c * 2 + half) % 2
                bank = pw[bk]
                guard_wait("tensor", f"pw{bk}")
                for kk in range(KC):
                    p.op("tensor", "matmul", bank[:],
                         xnT[:, kk, c * 128:(c + 1) * 128],
                         wbigB[:, kk, half * 512:(half + 1) * 512],
                         start=(kk == 0), stop=(kk == KC - 1),
                         inc=[(s_pe, 1)] if kk == KC - 1 else ())
                ost = (c * 2 + half) % 2
                s_dox = [s_doA, s_doB][ost]
                guard_wait("vector", f"outst{ost}")
                p.wait("vector", s_pe)
                p.op("vector", "tensor_add", out=outst[:, ost, :], in0=bank[:],
                     in1=resid[:, c, half * 512:(half + 1) * 512], inc=[(s_ve, 1)])
                guard_set(f"pw{bk}", s_ve)
                p.wait("sync", s_ve)
                p.op("sync", "dma_start",
                     out=out_d[t * TT + c * 128: t * TT + (c + 1) * 128,
                               half * 512:(half + 1) * 512],
                     in_=outst[:, ost, :], inc=[(s_dox, 16)])
                guard_set(f"outst{ost}", s_dox)
            if t + 1 < NTILE:
                ln_chunk(c, xval)
        guards["wbigB"] = (s_pe, s_pe.n)
        if t + 1 < NTILE:
            wg_ready = wload(wbigB, wg_d, "wbigB")

        if t == 0 and tap_d:
            srcs = {"q": qT[:], "k": kT[:], "gksp": gksp[:], "v": v_s[:],
                    "g": g_s[:], "h": hT[:], "wu": wu_s[:], "rsum": rsum_s[:],
                    "S": S_s[:], "orms": o_rms[:], "resid": resid[:],
                    "acc": acc[:], "xnt": xnT[:], "sums": sums_s[:],
                    "smc": sm_c[:], "xs": x_s[:],
                    "smc": sm_c[:]}
            p.wait("sync", s_ve)
            p.wait("sync", s_se)
            for name, dst in tap_d.items():
                p.op("sync", "dma_start", out=dst, in_=srcs[name],
                     inc=[(s_dt, 16)])
                p.wait("sync", s_dt)

    p.wait("sync", s_doA, s_doA.n)
    p.wait("sync", s_doB, s_doB.n)
    p.wait("sync", s_dt, s_dt.n)
    p.wait("gpsimd", s_wA, s_wA.n)
    p.wait("gpsimd", s_wB, s_wB.n)
    p.wait("gpsimd", s_w12[0], s_w12[0].n)
    p.wait("gpsimd", s_w12[1], s_w12[1].n)

    block = es.enter_context(nc.Block())
    p.run(block)
    es.close()
    return nc


_NC_CACHE = {}


def _get_nc(taps=()):
    key = tuple(sorted(taps))
    if key not in _NC_CACHE:
        _NC_CACHE[key] = build_nc(taps=key)
    return _NC_CACHE[key]


def make_core_inputs(inputs):
    f = np.ascontiguousarray
    shared = {
        "wq": f(np.asarray(inputs["L0_attn_Wq"], np.float32)),
        "wk": f(np.asarray(inputs["L0_attn_Wk"], np.float32)),
        "wv": f(np.asarray(inputs["L0_attn_Wv"], np.float32)),
        "wg": f(np.asarray(inputs["L0_attn_Wg"], np.float32)),
        "wo": f(np.asarray(inputs["L0_attn_Wo"], np.float32)),
        "wm": f(np.asarray(inputs["L0_ffn_Wm"], np.float32)),
        "wo2": f(np.asarray(inputs["L0_ffn_Wo"], np.float32)),
        "wgk1": f(np.asarray(inputs["L0_attn_Wgk1"], np.float32)),
        "wgk2": f(np.asarray(inputs["L0_attn_Wgk2"], np.float32)),
        "w1": f(np.asarray(inputs["L0_ffn_W1"], np.float32)),
        "w2": f(np.asarray(inputs["L0_ffn_W2"], np.float32)),
        "emb": f(np.asarray(inputs["L0_ffn_expert_emb"], np.float32)),
        "b2": f(np.asarray(inputs["L0_ffn_b2"], np.float32)),
        "bgk2l": f(np.asarray(inputs["L0_attn_bgk2"], np.float32).reshape(4, 128).T),
        "bml": f(np.asarray(inputs["L0_ffn_bm"], np.float32).reshape(8, 128).T),
        "b1l": f(np.asarray(inputs["L0_ffn_b1"], np.float32).reshape(E, 4, 128).transpose(2, 0, 1)),
        "bob": f(np.broadcast_to(np.asarray(inputs["L0_ffn_bo"], np.float32), (128, D))),
        "gwb": f(np.broadcast_to(np.asarray(inputs["L0_attn_gw"], np.float32), (128, dv))),
        "ident": np.eye(128, dtype=ml_dtypes.bfloat16),
        "identf": np.eye(128, dtype=np.float32),
        "trimask": f(np.triu(np.ones((128, 128))).astype(ml_dtypes.bfloat16)),
    }
    x = np.asarray(inputs["L0_attn_x"], np.float32)
    in_maps = []
    for core in range(8):
        m = dict(shared)
        m["x"] = f(x[core])
        in_maps.append(m)
    return in_maps


def kernel(**inputs):
    nc = _get_nc()
    in_maps = make_core_inputs(inputs)
    res = run_bass_kernel_spmd(nc, in_maps, core_ids=list(range(8)))
    return np.stack([r["out"] for r in res.results])


# version 11
# speedup vs baseline: 1.0696x; 1.0123x over previous
import numpy as np
import ml_dtypes
import contextlib
import concourse.bass as bass
import concourse.mybir as mybir
from concourse.bass_utils import run_bass_kernel_spmd

F32 = mybir.dt.float32
BF16 = mybir.dt.bfloat16
ACT = mybir.ActivationFunctionType
ALU = mybir.AluOpType
AX = mybir.AxisListType

T, D = 2048, 1024
NTILE, TT, C, NCH = 4, 512, 128, 4
NH, dk, dv, KC = 4, 128, 256, 8
H, HD, E, I = 8, 128, 16, 512
EPS = 1e-5
LNSCALE = float(np.log(128.0 ** -0.5))
ENGS = ("tensor", "vector", "scalar", "gpsimd", "sync")


class Sem:
    __slots__ = ("h", "n")

    def __init__(self, h):
        self.h = h
        self.n = 0


import os
SERIAL = bool(int(os.environ.get("KERNEL_SERIAL", "0")))


class Prog:
    def __init__(self, nc, es):
        self.nc = nc
        self.es = es
        self.ops = {e: [] for e in ENGS}
        self.esem = {}

    def sem(self, name):
        return Sem(self.es.enter_context(self.nc.semaphore(name)))

    def wait(self, eng, sem, val=None):
        v = sem.n if val is None else val
        assert v <= sem.n, f"wait on future increment ({v} > {sem.n})"
        if v <= 0 or SERIAL:
            return
        self.ops[eng].append(lambda e: e.wait_ge(sem.h, v))

    def op(self, eng, meth, *a, inc=(), **kw):
        inc = list(inc)
        if SERIAL:
            for s, k in inc:
                s.n += k
            for oe, os_ in self.esem.items():
                if os_.n > 0:
                    v = os_.n
                    self.ops[eng].append(
                        (lambda sem, vv: lambda e: e.wait_ge(sem.h, vv))(os_, v))
            own = self.esem[eng]
            inc = [(own, 16 if meth == "dma_start" else 1)]
        for s, k in inc:
            s.n += k

        def f(e):
            inst = getattr(e, meth)(*a, **kw)
            for s, k in inc:
                inst.then_inc(s.h, k)
        self.ops[eng].append(f)

    def run(self, block):
        for eng in ENGS:
            oplist = self.ops[eng]

            def make(lst):
                def body(e):
                    for f in lst:
                        f(e)
                return body
            getattr(block, eng)(make(oplist))


def build_nc(taps=()):
    nc = bass.Bass("TRN2", target_bir_lowering=False)
    dr = lambda n, s, d=F32, k="ExternalInput": nc.dram_tensor(n, s, d, kind=k).ap()
    x_d = dr("x", [T, D])
    wq_d = dr("wq", [D, 512])
    wk_d = dr("wk", [D, 512])
    wv_d = dr("wv", [D, D])
    wg_d = dr("wg", [D, D])
    wo_d = dr("wo", [D, D])
    wm_d = dr("wm", [D, D])
    wo2_d = dr("wo2", [D, D])
    wgk1_d = dr("wgk1", [D, 16])
    wgk2_d = dr("wgk2", [16, 512])
    w1_d = dr("w1", [E, HD, I])
    w2_d = dr("w2", [E, I, HD])
    emb_d = dr("emb", [HD, E])
    b2_d = dr("b2", [E, HD])
    bgk2l_d = dr("bgk2l", [128, 4])
    bml_d = dr("bml", [128, 8])
    b1l_d = dr("b1l", [128, E, 4])
    bob_d = dr("bob", [128, D])
    gwb_d = dr("gwb", [128, dv])
    ident_d = dr("ident", [128, 128], BF16)
    identf_d = dr("identf", [128, 128])
    trimask_d = dr("trimask", [128, 128], BF16)
    out_d = dr("out", [T, D], F32, "ExternalOutput")
    tap_shapes = {"q": ([128, 4, TT], BF16), "k": ([128, 4, TT], BF16),
                  "gksp": ([128, 4, TT], F32), "v": ([128, NCH, D], BF16),
                  "g": ([128, NCH, D], BF16), "h": ([128, H, TT], BF16),
                  "wu": ([128, NCH, H, 32], F32), "rsum": ([128, NCH, H], F32),
                  "S": ([128, NH, dv], F32), "orms": ([128, NCH, D], BF16),
                  "resid": ([128, NCH, D], BF16), "acc": ([128, NCH, D], F32),
                  "xnt": ([128, KC, TT], BF16), "sums": ([128, NCH, H], F32),
                  "lgd": ([128, NCH, 128], F32), "smc": ([128, 128], F32),
                  "xs": ([128, NCH, D], F32), "lnst": ([128, NCH, 8], F32)}
    tap_d = {tp: dr("tap_" + tp, *tap_shapes[tp], k="ExternalOutput") for tp in taps}

    es = contextlib.ExitStack()
    sb = lambda n, s, d=F32: es.enter_context(nc.sbuf_tensor(n, s, d))

    wq_s = sb("wq_s", [128, KC, 512], BF16)
    wk_s = sb("wk_s", [128, KC, 512], BF16)
    wgk1_s = sb("wgk1_s", [128, KC, 16], BF16)
    wgk2_s = sb("wgk2_s", [16, 4, 128], BF16)
    emb_s = sb("emb_s", [128, E], BF16)
    b2_s = sb("b2_s", [16, 128], BF16)
    b1l_s = sb("b1l_s", [128, E, 4])
    bml_s = sb("bml_s", [128, 8])
    bgk2r_s = sb("bgk2r_s", [128, 4])
    bgk2n_s = sb("bgk2n_s", [128, 4])
    bob_s = sb("bob_s", [128, D])
    gwb_s = sb("gwb_s", [128, dv])
    ident_s = sb("ident_s", [128, 128], BF16)
    identf_s = sb("identf_s", [128, 128])
    trimask_s = sb("trimask_s", [128, 128], BF16)
    ones_s = sb("ones_s", [128, 128])
    ones_s2 = sb("ones_s2", [128, 512])
    consts_s = sb("consts_s", [128, 3])
    wbigA = sb("wbigA", [128, KC, D], BF16)
    wbigB = sb("wbigB", [128, KC, D], BF16)
    w1b = sb("w1b", [128, 2, I], BF16)
    w2b = sb("w2b", [128, 2, 4, HD], BF16)
    x_s = sb("x_s", [128, NCH, D])
    xn_tm = sb("xn_tm", [128, D], BF16)
    xnT = sb("xnT", [128, KC, TT], BF16)
    qT = sb("qT", [128, NH, TT], BF16)
    kT = sb("kT", [128, NH, TT], BF16)
    gksp = sb("gksp", [128, NCH, NH, 128])
    rstd_s = sb("rstd_s", [128, 16])
    v_s = sb("v_s", [128, NCH, D], BF16)
    g_s = sb("g_s", [128, NCH, D], BF16)
    hT = sb("hT", [128, H, TT], BF16)
    gk1b = sb("gk1b", [16, TT], BF16)
    e1_s = sb("e1_s", [128, 512])
    sm_a = sb("sm_a", [128, 128])
    sm_b = sb("sm_b", [128, 128])
    sm_c = sb("sm_c", [128, 128])
    wu_s = sb("wu_s", [128, NCH, H, 32])
    wuT_s = sb("wuT_s", [16, H, 128], BF16)
    rsum_s = sb("rsum_s", [128, NCH, H])
    sums_s = sb("sums_s", [128, NCH, H])
    stats = sb("stats", [128, 32])
    S_s = sb("S_s", [128, NH, dv])
    S_bf = sb("S_bf", [128, NH, dv], BF16)
    A4 = sb("A4", [128, NH, 128])
    lam4 = sb("lam4", [128, NH, 128])
    Pb4 = sb("Pb4", [128, NH, 128], BF16)
    khT_s = sb("khT_s", [128, NCH, NH, 128], BF16)
    o_rms = sb("o_rms", [128, NCH, D], BF16)
    resid = sb("resid", [128, NCH, D], BF16)
    acc = sb("acc", [128, NCH, D])
    comb_bf = v_s
    hmidS = sb("hmidS", [128, 4, H, 256], BF16)
    kh4 = hmidS.rearrange("p a h n -> p (a h n)")[:, 0:NCH * NH * 128] \
        .rearrange("p (c h n) -> p c h n", c=NCH, h=NH)
    outst = sb("outst", [128, 2, 512])

    ph = [es.enter_context(nc.psum_tensor(f"ph{i}", [128, 1024], F32)) for i in range(2)]
    pw = [es.enter_context(nc.psum_tensor(f"pw{i}", [128, 512], F32)) for i in range(3)]
    ptp = es.enter_context(nc.psum_tensor("ptp", [128, 1024], BF16))

    p = Prog(nc, es)
    s_dx = p.sem("s_dx")
    s_di = p.sem("s_di")
    s_winit = p.sem("s_winit")
    s_wA = p.sem("s_wA")
    s_wB = p.sem("s_wB")
    s_w12 = [p.sem("s_w12a"), p.sem("s_w12b")]
    s_doA = p.sem("s_doA")
    s_doB = p.sem("s_doB")
    s_dt = p.sem("s_dt")
    s_pe = p.sem("s_pe")
    s_ve = p.sem("s_ve")
    s_se = p.sem("s_se")
    if SERIAL:
        p.esem = {e: p.sem("ser_" + e) for e in ENGS}

    guards = {}

    def guard_wait(eng, key):
        g = guards.get(key)
        if g is not None:
            p.wait(eng, g[0], g[1])

    def guard_set(key, sem):
        guards[key] = (sem, sem.n)

    for dst, src in [(b1l_s, b1l_d), (bml_s, bml_d), (bgk2r_s, bgk2l_d),
                     (bob_s, bob_d), (gwb_s, gwb_d), (ident_s, ident_d),
                     (identf_s, identf_d), (trimask_s, trimask_d)]:
        p.op("sync", "dma_start", out=dst[:], in_=src, inc=[(s_di, 16)])
    p.op("gpsimd", "dma_start", out=wq_s[:],
         in_=wq_d.rearrange("(kk p) n -> p kk n", p=128), inc=[(s_winit, 16)])
    p.op("gpsimd", "dma_start", out=wk_s[:],
         in_=wk_d.rearrange("(kk p) n -> p kk n", p=128), inc=[(s_winit, 16)])
    p.op("gpsimd", "dma_start", out=wgk1_s[:],
         in_=wgk1_d.rearrange("(kk p) n -> p kk n", p=128), inc=[(s_winit, 16)])
    p.op("gpsimd", "dma_start", out=wgk2_s[:],
         in_=wgk2_d.rearrange("k (m n) -> k m n", m=4), inc=[(s_winit, 16)])
    p.op("gpsimd", "dma_start", out=emb_s[:], in_=emb_d, inc=[(s_winit, 16)])
    p.op("gpsimd", "dma_start", out=b2_s[:], in_=b2_d, inc=[(s_winit, 16)])
    p.wait("vector", s_di)
    p.wait("tensor", s_di)
    p.wait("tensor", s_winit)
    p.wait("scalar", s_di)
    p.op("vector", "memset", wu_s[:], 0.0)
    p.op("vector", "memset", S_s[:], 0.0)
    p.op("vector", "memset", S_bf[:], 0.0)
    p.op("vector", "memset", ones_s[:], 1.0)
    p.op("vector", "memset", ones_s2[:], 1.0)
    p.op("vector", "memset", consts_s[:, 0:1], EPS)
    p.op("vector", "memset", consts_s[:, 1:2], 1.0)
    p.op("vector", "memset", consts_s[:, 2:3], LNSCALE)
    p.op("vector", "memset", stats[:, 8:9], 0.0)
    p.op("vector", "tensor_scalar_mul", out=bgk2n_s[:], in0=bgk2r_s[:],
         scalar1=-1.0, inc=[(s_ve, 1)])

    wsem = {"wbigA": s_wA, "wbigB": s_wB}

    def wload(dst_s, src_d, key):
        guard_wait("gpsimd", key)
        sem = wsem[key]
        p.op("gpsimd", "dma_start", out=dst_s[:],
             in_=src_d.rearrange("(kk p) n -> p kk n", p=128), inc=[(sem, 16)])
        return (sem, sem.n)

    def w12load(e):
        b = e % 2
        guard_wait("gpsimd", f"w1b{b}")
        p.op("gpsimd", "dma_start", out=w1b[:, b, :], in_=w1_d[e], inc=[(s_w12[b], 16)])
        p.op("gpsimd", "dma_start", out=w2b[:, b, :, :],
             in_=w2_d[e].rearrange("(i p) n -> p i n", p=128), inc=[(s_w12[b], 16)])

    def transpose_slab(src2d, dstT, c, nblk=8):
        guard_wait("tensor", "ptp")
        for i in range(nblk):
            p.op("tensor", "transpose", ptp[:, i * 128:(i + 1) * 128],
                 src2d[:, i * 128:(i + 1) * 128], ident_s[:],
                 inc=[(s_pe, 1)] if i == nblk - 1 else ())
        p.wait("vector", s_pe)
        p.op("vector", "tensor_copy",
             out=dstT[:, 0:nblk, c * 128:(c + 1) * 128],
             in_=ptp.rearrange("p (i n) -> p i n", i=8)[:, 0:nblk, :],
             inc=[(s_ve, 1)])
        guard_set("ptp", s_ve)

    sbf_se = {h: 0 for h in range(NH)}
    gate_ve = {}
    gate_se = {}
    kh_ve_c = {}
    gbank = [0]

    def xload(t):
        for c in range(NCH):
            guard_wait("sync", f"x_s{c}")
            p.op("sync", "dma_start", out=x_s[:, c, :],
                 in_=x_d[t * TT + c * 128: t * TT + (c + 1) * 128, :],
                 inc=[(s_dx, 16)])
        return s_dx.n

    xval = xload(0)
    wv_ready = wload(wbigA, wv_d, "wbigA")
    wg_ready = wload(wbigB, wg_d, "wbigB")
    ln_ve = {}

    def ln_batched(xv):
        p.wait("scalar", s_dx, xv)
        for c in range(NCH):
            p.op("scalar", "activation", out=xn_tm[:], in_=x_s[:, c, :],
                 func=ACT.Square, accum_out=stats[:, c:c + 1], inc=[(s_se, 1)])
        sq_se = s_se.n
        p.wait("vector", s_dx, xv)
        p.op("vector", "reduce_sum", out=stats[:, 4:8], in_=x_s[:], axis=AX.X,
             inc=[(s_ve, 1)])
        p.wait("vector", s_ve)
        p.op("vector", "tensor_scalar_mul", out=stats[:, 4:8], in0=stats[:, 4:8],
             scalar1=1.0 / D, inc=[(s_ve, 1)])
        p.wait("vector", s_ve)
        p.op("vector", "tensor_mul", out=stats[:, 12:16], in0=stats[:, 4:8],
             in1=stats[:, 4:8], inc=[(s_ve, 1)])
        p.wait("vector", s_se, sq_se)
        p.op("vector", "tensor_scalar_mul", out=stats[:, 0:4], in0=stats[:, 0:4],
             scalar1=1.0 / D, inc=[(s_ve, 1)])
        p.wait("vector", s_ve)
        p.op("vector", "tensor_tensor", out=stats[:, 0:4], in0=stats[:, 0:4],
             in1=stats[:, 12:16], op=ALU.subtract, inc=[(s_ve, 1)])
        p.wait("scalar", s_ve)
        p.op("scalar", "activation", out=stats[:, 12:16], in_=stats[:, 0:4],
             func=ACT.Ln, bias=consts_s[:, 0:1], inc=[(s_se, 1)])
        p.wait("scalar", s_se)
        p.op("scalar", "activation", out=stats[:, 0:4], in_=stats[:, 12:16],
             func=ACT.Exp, scale=-0.5, inc=[(s_se, 1)])
        return s_se.n

    def ln_xn(c, rdy_se):
        p.wait("vector", s_se, rdy_se)
        p.wait("vector", s_pe)
        p.op("vector", "tensor_scalar", out=o_rms[:, c, :], in0=x_s[:, c, :],
             scalar1=stats[:, 4 + c:5 + c], scalar2=stats[:, c:c + 1],
             op0=ALU.subtract, op1=ALU.mult, inc=[(s_ve, 1)])
        ln_ve[c] = s_ve.n

    rdy = ln_batched(xval)
    for c in range(NCH):
        ln_xn(c, rdy)

    for t in range(NTILE):
        for c in range(NCH):
            p.wait("tensor", s_ve, ln_ve[c])
            transpose_slab(o_rms[:, c, :], xnT, c)
        xn_done = s_ve.n

        def proj_fm(wsb, dstT, mrange, bias_l=None, evac_eng="vector"):
            for mi, m in enumerate(mrange):
                bk = mi % 2
                bank = pw[bk]
                guard_wait("tensor", f"pw{bk}")
                for kk in range(KC):
                    p.op("tensor", "matmul", bank[:],
                         wsb[:, kk, m * 128:(m + 1) * 128], xnT[:, kk, :],
                         start=(kk == 0), stop=(kk == KC - 1),
                         inc=[(s_pe, 1)] if kk == KC - 1 else ())
                if evac_eng == "vector":
                    p.wait("vector", s_pe)
                    p.op("vector", "tensor_copy", out=dstT[:, m, :], in_=bank[:],
                         inc=[(s_ve, 1)])
                    guard_set(f"pw{bk}", s_ve)
                else:
                    p.wait("scalar", s_pe)
                    p.op("scalar", "activation", out=dstT[:, m, :], in_=bank[:],
                         func=ACT.Identity, bias=bias_l[:, m:m + 1], inc=[(s_se, 1)])
                    guard_set(f"pw{bk}", s_se)

        p.wait("tensor", s_ve, xn_done)
        proj_fm(wq_s, qT, range(NH))
        proj_fm(wk_s, kT, range(NH))

        guard_wait("tensor", "pw2")
        for kk in range(KC):
            p.op("tensor", "matmul", pw[2][0:16, 0:TT], wgk1_s[:, kk, :],
                 xnT[:, kk, :], start=(kk == 0), stop=(kk == KC - 1),
                 inc=[(s_pe, 1)] if kk == KC - 1 else ())
        p.wait("vector", s_pe)
        p.op("vector", "tensor_copy", out=gk1b[:], in_=pw[2][0:16, 0:TT],
             inc=[(s_ve, 1)])
        guard_set("pw2", s_ve)
        gk1_ve = s_ve.n
        for m in range(4):
            bk = m % 2
            bank = pw[bk]
            guard_wait("tensor", f"pw{bk}")
            p.wait("tensor", s_ve, gk1_ve)
            p.op("tensor", "matmul", bank[:], wgk2_s[:, m, :], gk1b[:],
                 start=True, stop=True, inc=[(s_pe, 1)])
            p.wait("scalar", s_pe)
            p.op("scalar", "activation", out=e1_s[:], in_=bank[:],
                 func=ACT.Exp, scale=-1.0, bias=bgk2n_s[:, m:m + 1], inc=[(s_se, 1)])
            p.wait("scalar", s_se)
            p.op("scalar", "activation", out=gksp[:, :, m, :], in_=e1_s[:],
                 func=ACT.Ln, bias=consts_s[:, 1:2], inc=[(s_se, 1)])
            guard_set(f"pw{bk}", s_se)

        def proj_tm(wsb, dst, wready):
            p.wait("tensor", wready[0], wready[1])
            for ci in range(NCH):
                for half in range(2):
                    bk = (ci * 2 + half) % 2
                    bank = pw[bk]
                    guard_wait("tensor", f"pw{bk}")
                    for kk in range(KC):
                        p.op("tensor", "matmul", bank[:],
                             xnT[:, kk, ci * 128:(ci + 1) * 128],
                             wsb[:, kk, half * 512:(half + 1) * 512],
                             start=(kk == 0), stop=(kk == KC - 1),
                             inc=[(s_pe, 1)] if kk == KC - 1 else ())
                    p.wait("scalar", s_pe)
                    p.op("scalar", "copy",
                         out=dst[:, ci, half * 512:(half + 1) * 512],
                         in_=bank[:], inc=[(s_se, 1)])
                    guard_set(f"pw{bk}", s_se)

        gk_done_se = s_se.n
        for c in range(NCH):
            cs = slice(c * 128, (c + 1) * 128)
            A2 = A4.rearrange("p h n -> p (h n)")
            p.wait("vector", s_se, gk_done_se)
            p.op("vector", "tensor_tensor_scan", out=A2, data0=ones_s2[:],
                 data1=gksp[:, c].rearrange("p h n -> p (h n)"),
                 initial=0.0, op0=ALU.mult, op1=ALU.add, inc=[(s_ve, 1)])
            p.wait("vector", s_ve)
            p.op("vector", "tensor_copy", out=stats[:, 9:12],
                 in_=A4[:, 0:3, 127], inc=[(s_ve, 1)])
            p.wait("vector", s_ve)
            pcb = stats[:, 8:12].rearrange("p (h o) -> p h o", h=NH).broadcast_to([128, NH, 128])
            p.op("vector", "tensor_tensor", out=A4[:], in0=A4[:], in1=pcb,
                 op=ALU.subtract, inc=[(s_ve, 1)])
            A_ve = s_ve.n
            p.wait("scalar", s_ve, A_ve)
            p.op("scalar", "activation", out=rstd_s[:, c * NH:(c + 1) * NH],
                 in_=A4[:, :, 127],
                 func=ACT.Exp, scale=-1.0 / 16.0, inc=[(s_se, 1)])
            p.op("scalar", "activation", out=lam4.rearrange("p h n -> p (h n)"),
                 in_=A2, func=ACT.Exp, scale=1.0 / 16.0, inc=[(s_se, 1)])
            p.wait("vector", s_se)
            p.op("vector", "tensor_mul", out=kT[:, :, cs], in0=kT[:, :, cs],
                 in1=lam4[:], inc=[(s_ve, 1)])
            kt_ve = s_ve.n
            p.wait("vector", s_ve)
            dsb = rstd_s[:, c * NH:(c + 1) * NH] \
                .rearrange("p (h o) -> p h o", h=NH).broadcast_to([128, NH, 128])
            if c == 0:
                guard_wait("vector", "hmidS0")
                guard_wait("vector", "hmidS1")
            p.op("vector", "tensor_tensor", out=kh4[:, c], in0=kT[:, :, cs], in1=dsb,
                 op=ALU.mult, inc=[(s_ve, 1)])
            kh_ve_c[c] = s_ve.n
            p.wait("scalar", s_ve, kt_ve)
            p.op("scalar", "activation", out=lam4.rearrange("p h n -> p (h n)"),
                 in_=A2, func=ACT.Exp, scale=-1.0 / 16.0, bias=consts_s[:, 2:3],
                 inc=[(s_se, 1)])
            p.wait("vector", s_se)
            p.op("vector", "tensor_mul", out=qT[:, :, cs], in0=qT[:, :, cs],
                 in1=lam4[:], inc=[(s_ve, 1)])
            gate_ve[c] = s_ve.n
            gate_se[c] = s_se.n

        proj_tm(wbigA, v_s, wv_ready)
        guards["wbigA"] = (s_pe, s_pe.n)
        wm_ready = wload(wbigA, wm_d, "wbigA")
        for c in range(NCH):
            guard_wait("tensor", "ptp")
            p.wait("tensor", s_ve, kh_ve_c[c])
            for h in range(NH):
                p.op("tensor", "transpose", ptp[:, h * 128:(h + 1) * 128],
                     kh4[:, c, h, :], ident_s[:],
                     inc=[(s_pe, 1)] if h == NH - 1 else ())
            p.wait("vector", s_pe)
            p.op("vector", "tensor_copy", out=khT_s[:, c, :, :],
                 in_=ptp.rearrange("p (i n) -> p i n", i=8)[:, 0:NH, :],
                 inc=[(s_ve, 1)])
            guard_set("ptp", s_ve)
        guards["hmidS0"] = guards["hmidS1"] = (s_ve, s_ve.n)
        proj_tm(wbigB, g_s, wg_ready)
        guards["wbigB"] = (s_pe, s_pe.n)
        wo_ready = wload(wbigB, wo_d, "wbigB")
        p.wait("tensor", wm_ready[0], wm_ready[1])
        proj_fm(wbigA, hT, range(H), bias_l=bml_s, evac_eng="scalar")
        guards["wbigA"] = (s_pe, s_pe.n)
        if t + 1 < NTILE:
            wv_ready = wload(wbigA, wv_d, "wbigA")

        h_done_se = s_se.n
        for c in range(NCH):
            guard_wait("tensor", "pw2")
            p.wait("tensor", s_se, h_done_se)
            for j in range(H):
                p.op("tensor", "matmul", pw[2][:, j * 16:(j + 1) * 16],
                     hT[:, j, c * 128:(c + 1) * 128], emb_s[:],
                     start=True, stop=True,
                     inc=[(s_pe, 1)] if j == H - 1 else ())
            lg = pw[2][:, 0:128].rearrange("p (j e) -> p j e", j=H)
            p.wait("vector", s_pe)
            p.op("vector", "tensor_reduce", out=sm_a[:, 0:8], in_=lg, axis=AX.X,
                 op=ALU.max, negate=True, inc=[(s_ve, 1)])
            p.wait("vector", s_ve)
            nmx = sm_a[:, 0:8].rearrange("p (j o) -> p j o", j=H).broadcast_to([128, H, 16])
            p.wait("vector", s_se)
            p.op("vector", "tensor_tensor",
                 out=sm_b[:].rearrange("p (j e) -> p j e", j=H), in0=lg, in1=nmx,
                 op=ALU.add, inc=[(s_ve, 1)])
            guard_set("pw2", s_ve)
            p.wait("scalar", s_ve)
            p.op("scalar", "activation", out=sm_c[:], in_=sm_b[:], func=ACT.Exp,
                 inc=[(s_se, 1)])
            p.wait("vector", s_se)
            w3 = sm_c[:].rearrange("p (j e) -> p j e", j=H)
            p.op("vector", "tensor_reduce", out=sums_s[:, c, :], in_=w3, axis=AX.X,
                 op=ALU.add, inc=[(s_ve, 1)])
            p.wait("vector", s_ve)
            p.op("vector", "reciprocal", out=rsum_s[:, c, :], in_=sums_s[:, c, :])
            p.op("vector", "tensor_reduce", out=sm_b[:, 0:8], in_=w3, axis=AX.X,
                 op=ALU.max, inc=[(s_ve, 1)])
            p.wait("vector", s_ve)
            m1b = sm_b[:, 0:8].rearrange("p (j o) -> p j o", j=H).broadcast_to([128, H, 16])
            p.op("vector", "tensor_tensor",
                 out=sm_a[:].rearrange("p (j e) -> p j e", j=H), in0=w3, in1=m1b,
                 op=ALU.is_lt, inc=[(s_ve, 1)])
            p.wait("vector", s_ve)
            p.op("vector", "tensor_tensor",
                 out=sm_a[:].rearrange("p (j e) -> p j e", j=H), in0=w3,
                 in1=sm_a[:].rearrange("p (j e) -> p j e", j=H), op=ALU.mult,
                 inc=[(s_ve, 1)])
            p.wait("vector", s_ve)
            p.op("vector", "tensor_reduce", out=sm_b[:, 0:8],
                 in_=sm_a[:].rearrange("p (j e) -> p j e", j=H), axis=AX.X,
                 op=ALU.max, inc=[(s_ve, 1)])
            p.wait("vector", s_ve)
            m2b = sm_b[:, 0:8].rearrange("p (j o) -> p j o", j=H).broadcast_to([128, H, 16])
            p.op("vector", "tensor_tensor",
                 out=sm_a[:].rearrange("p (j e) -> p j e", j=H),
                 in0=w3, in1=m2b, op=ALU.is_ge, inc=[(s_ve, 1)])
            p.wait("vector", s_ve)
            p.op("vector", "tensor_tensor",
                 out=sm_b[:].rearrange("p (j e) -> p j e", j=H), in0=w3,
                 in1=sm_a[:].rearrange("p (j e) -> p j e", j=H),
                 op=ALU.mult, inc=[(s_ve, 1)])
            p.wait("vector", s_ve)
            rsb = rsum_s[:, c, :].rearrange("p (j o) -> p j o", j=H).broadcast_to([128, H, 16])
            p.op("vector", "tensor_tensor",
                 out=wu_s[:, c, :, 0:16],
                 in0=sm_b[:].rearrange("p (j e) -> p j e", j=H), in1=rsb,
                 op=ALU.mult, inc=[(s_ve, 1)])
            wu_ve = s_ve.n
            for jj in range(2):
                guard_wait("tensor", "pw2")
                p.wait("tensor", s_ve, wu_ve)
                for j4 in range(4):
                    j = jj * 4 + j4
                    p.op("tensor", "transpose", pw[2][0:16, j4 * 128:(j4 + 1) * 128],
                         wu_s[:, c, j, 0:16], identf_s[:],
                         inc=[(s_pe, 1)] if j4 == 3 else ())
                p.wait("vector", s_pe)
                p.op("vector", "tensor_copy",
                     out=wuT_s[:, jj * 4:(jj + 1) * 4, :],
                     in_=pw[2][0:16, 0:512].rearrange("p (j n) -> p j n", j=4),
                     inc=[(s_ve, 1)])
                guard_set("pw2", s_ve)
            wuT_ve = s_ve.n
            for jj in range(2):
                guard_wait("tensor", "pw2")
                p.wait("tensor", s_ve, wuT_ve)
                for j4 in range(4):
                    j = jj * 4 + j4
                    p.op("tensor", "matmul", pw[2][:, j4 * 128:(j4 + 1) * 128],
                         wuT_s[:, j, :], b2_s[:], start=True, stop=True,
                         inc=[(s_pe, 1)] if j4 == 3 else ())
                p.wait("vector", s_pe)
                p.op("vector", "tensor_copy",
                     out=acc[:, c, jj * 512:(jj + 1) * 512],
                     in_=pw[2][:, 0:512], inc=[(s_ve, 1)])
                guard_set("pw2", s_ve)

        def head_gen(c, h):
            if True:
                ch = c * NH + h
                cs = slice(c * 128, (c + 1) * 128)
                vsl = v_s[:, c, h * dv:(h + 1) * dv]
                bk = gbank[0] % 3
                gbank[0] += 1
                bank = pw[bk]
                guard_wait("tensor", f"pw{bk}")
                p.wait("tensor", s_ve, gate_ve[c])
                p.wait("tensor", s_se, sbf_se[h])
                p.op("tensor", "matmul", bank[:, 256:384], kT[:, h, cs],
                     qT[:, h, cs], start=True, stop=True)
                p.op("tensor", "matmul", bank[:, 0:dv], qT[:, h, cs],
                     S_bf[:, h, :], start=True, stop=False, inc=[(s_pe, 1)])
                p.wait("vector", s_pe)
                p.op("vector", "tensor_mul", out=Pb4[:, h, :], in0=bank[:, 256:384],
                     in1=trimask_s[:], inc=[(s_ve, 1)])
                mask_ve = s_ve.n
                yield
                p.wait("tensor", s_ve, mask_ve)
                p.op("tensor", "matmul", bank[:, 0:dv], Pb4[:, h, :], vsl,
                     start=False, stop=True, inc=[(s_pe, 1)])
                o_pe = s_pe.n
                p.wait("scalar", s_pe, o_pe)
                p.op("scalar", "activation",
                     out=o_rms[:, c, h * dv:(h + 1) * dv], in_=bank[:, 0:dv],
                     func=ACT.Square, accum_out=stats[:, 16 + ch:17 + ch],
                     inc=[(s_se, 1)])
                yield
                p.wait("vector", s_se)
                p.op("vector", "tensor_copy", out=o_rms[:, c, h * dv:(h + 1) * dv],
                     in_=bank[:, 0:dv], inc=[(s_ve, 1)])
                oev_ve = s_ve.n
                yield
                p.wait("tensor", s_ve, oev_ve)
                p.op("tensor", "matmul", bank[:, 256:512], khT_s[:, c, h, :], vsl,
                     start=True, stop=True, inc=[(s_pe, 1)])
                smm_pe = s_pe.n
                p.wait("vector", s_pe, smm_pe)
                p.wait("vector", s_se, gate_se[c])
                p.op("vector", "scalar_tensor_tensor", out=S_s[:, h, :],
                     in0=S_s[:, h, :], scalar=rstd_s[:, c * NH + h:c * NH + h + 1],
                     in1=bank[:, 256:512], op0=ALU.mult, op1=ALU.add,
                     inc=[(s_ve, 1)])
                guard_set(f"pw{bk}", s_ve)
                p.wait("scalar", s_ve)
                p.op("scalar", "activation", out=S_bf[:, h, :], in_=S_s[:, h, :],
                     func=ACT.Copy, inc=[(s_se, 1)])
                sbf_se[h] = s_se.n
                yield

        for c in range(NCH):
            for h in range(NH):
                for _ in head_gen(c, h):
                    pass

        w12load(0)
        w12load(1)
        p.wait("scalar", s_se)
        p.op("scalar", "activation", out=rstd_s[:], in_=stats[:, 16:32],
             func=ACT.Ln, scale=1.0 / dv, bias=consts_s[:, 0:1], inc=[(s_se, 1)])
        p.wait("scalar", s_se)
        p.op("scalar", "activation", out=stats[:, 16:32], in_=rstd_s[:],
             func=ACT.Exp, scale=-0.5, inc=[(s_se, 1)])
        exp_se = s_se.n
        for c in range(NCH):
            p.op("scalar", "activation", out=g_s[:, c, :], in_=g_s[:, c, :],
                 func=ACT.Silu, inc=[(s_se, 1)])
        silu_se = s_se.n
        p.wait("vector", s_se, exp_se)
        for c in range(NCH):
            for h in range(NH):
                ch = c * NH + h
                p.op("vector", "scalar_tensor_tensor",
                     out=o_rms[:, c, h * dv:(h + 1) * dv],
                     in0=o_rms[:, c, h * dv:(h + 1) * dv],
                     scalar=stats[:, 16 + ch:17 + ch], in1=gwb_s[:],
                     op0=ALU.mult, op1=ALU.mult, inc=[(s_ve, 1)])
        p.wait("vector", s_ve)
        p.wait("vector", s_se, silu_se)
        for c in range(NCH):
            p.op("vector", "tensor_mul", out=o_rms[:, c, :], in0=o_rms[:, c, :],
                 in1=g_s[:, c, :], inc=[(s_ve, 1)])
        p.wait("tensor", s_ve)
        for c in range(NCH):
            transpose_slab(o_rms[:, c, :], xnT, c)
        oT_ve = s_ve.n
        p.wait("tensor", wo_ready[0], wo_ready[1])
        p.wait("tensor", s_ve, oT_ve)
        for c in range(NCH):
            for half in range(2):
                bk = (c * 2 + half) % 2
                bank = pw[bk]
                guard_wait("tensor", f"pw{bk}")
                for kk in range(KC):
                    p.op("tensor", "matmul", bank[:],
                         xnT[:, kk, c * 128:(c + 1) * 128],
                         wbigB[:, kk, half * 512:(half + 1) * 512],
                         start=(kk == 0), stop=(kk == KC - 1),
                         inc=[(s_pe, 1)] if kk == KC - 1 else ())
                p.wait("vector", s_pe)
                p.op("vector", "tensor_add", out=e1_s[:], in0=bank[:],
                     in1=x_s[:, c, half * 512:(half + 1) * 512], inc=[(s_ve, 1)])
                guard_set(f"pw{bk}", s_ve)
                p.wait("vector", s_ve)
                p.op("vector", "tensor_add",
                     out=resid[:, c, half * 512:(half + 1) * 512],
                     in0=e1_s[:], in1=bob_s[:, half * 512:(half + 1) * 512])
            guard_set(f"x_s{c}", s_ve)
        guards["wbigB"] = (s_pe, s_pe.n)
        wo2_ready = wload(wbigB, wo2_d, "wbigB")
        if t + 1 < NTILE:
            xval = xload(t + 1)

        for e in range(E):
            b = e % 2
            p.wait("tensor", s_w12[b])
            mm1_done = {}
            for h2 in range(2):
                for jh in range(2):
                    g = guards.get(f"hmidS{jh}")
                    for i in range(4):
                        pb = (h2 * 2 + jh + i) % 2
                        bank = ph[pb]
                        guard_wait("tensor", f"ph{pb}")
                        for j4 in range(4):
                            j = jh * 4 + j4
                            p.op("tensor", "matmul", bank[:, j4 * 256:(j4 + 1) * 256],
                                 w1b[:, b, i * 128:(i + 1) * 128],
                                 hT[:, j, h2 * 256:(h2 + 1) * 256],
                                 start=True, stop=True,
                                 inc=[(s_pe, 1)] if j4 == 3 else ())
                        if g is not None:
                            p.wait("scalar", g[0], g[1])
                            g = None
                        p.wait("scalar", s_pe)
                        p.op("scalar", "activation",
                             out=hmidS[:, i, jh * 4:(jh + 1) * 4, :],
                             in_=bank[:], func=ACT.Gelu,
                             bias=b1l_s[:, e, i:i + 1], inc=[(s_se, 1)])
                        guard_set(f"ph{pb}", s_se)
                    mm1_done[jh] = s_se.n
                for jh in range(2):
                    p.wait("tensor", s_se, mm1_done[jh])
                    for j4 in range(4):
                        j = jh * 4 + j4
                        bk = j % 2
                        bank = pw[bk]
                        guard_wait("tensor", f"pw{bk}")
                        for c2 in range(2):
                            for i in range(4):
                                p.op("tensor", "matmul",
                                     bank[:, c2 * 128:(c2 + 1) * 128],
                                     hmidS[:, i, j, c2 * 128:(c2 + 1) * 128],
                                     w2b[:, b, i, :], start=(i == 0), stop=(i == 3),
                                     inc=[(s_pe, 1)] if i == 3 else ())
                        p.wait("vector", s_pe)
                        for c2 in range(2):
                            cc = h2 * 2 + c2
                            p.op("vector", "scalar_tensor_tensor",
                                 out=acc[:, cc, j * 128:(j + 1) * 128],
                                 in0=bank[:, c2 * 128:(c2 + 1) * 128],
                                 scalar=wu_s[:, cc, j, e:e + 1],
                                 in1=acc[:, cc, j * 128:(j + 1) * 128],
                                 op0=ALU.mult, op1=ALU.add,
                                 inc=[(s_ve, 1)] if c2 == 1 else ())
                        guard_set(f"pw{bk}", s_ve)
                    guard_set(f"hmidS{jh}", s_ve)
            guard_set(f"w1b{b}", s_pe)
            if e + 2 < E:
                w12load(e + 2)

        p.wait("vector", s_ve)
        for c in range(NCH):
            p.op("vector", "tensor_copy", out=comb_bf[:, c, :], in_=acc[:, c, :],
                 inc=[(s_ve, 1)])
        p.wait("tensor", s_ve)
        for c in range(NCH):
            transpose_slab(comb_bf[:, c, :], xnT, c)
        combT_ve = s_ve.n
        if t + 1 < NTILE:
            rdy = ln_batched(xval)
        p.wait("tensor", wo2_ready[0], wo2_ready[1])
        p.wait("tensor", s_ve, combT_ve)
        for c in range(NCH):
            for half in range(2):
                bk = (c * 2 + half) % 2
                bank = pw[bk]
                guard_wait("tensor", f"pw{bk}")
                for kk in range(KC):
                    p.op("tensor", "matmul", bank[:],
                         xnT[:, kk, c * 128:(c + 1) * 128],
                         wbigB[:, kk, half * 512:(half + 1) * 512],
                         start=(kk == 0), stop=(kk == KC - 1),
                         inc=[(s_pe, 1)] if kk == KC - 1 else ())
                ost = (c * 2 + half) % 2
                s_dox = [s_doA, s_doB][ost]
                guard_wait("vector", f"outst{ost}")
                p.wait("vector", s_pe)
                p.op("vector", "tensor_add", out=outst[:, ost, :], in0=bank[:],
                     in1=resid[:, c, half * 512:(half + 1) * 512], inc=[(s_ve, 1)])
                guard_set(f"pw{bk}", s_ve)
                p.wait("sync", s_ve)
                p.op("sync", "dma_start",
                     out=out_d[t * TT + c * 128: t * TT + (c + 1) * 128,
                               half * 512:(half + 1) * 512],
                     in_=outst[:, ost, :], inc=[(s_dox, 16)])
                guard_set(f"outst{ost}", s_dox)
            if t + 1 < NTILE:
                ln_xn(c, rdy)
        guards["wbigB"] = (s_pe, s_pe.n)
        if t + 1 < NTILE:
            wg_ready = wload(wbigB, wg_d, "wbigB")

        if t == 0 and tap_d:
            srcs = {"q": qT[:], "k": kT[:], "gksp": gksp[:], "v": v_s[:],
                    "g": g_s[:], "h": hT[:], "wu": wu_s[:], "rsum": rsum_s[:],
                    "S": S_s[:], "orms": o_rms[:], "resid": resid[:],
                    "acc": acc[:], "xnt": xnT[:], "sums": sums_s[:],
                    "smc": sm_c[:], "xs": x_s[:],
                    "smc": sm_c[:]}
            p.wait("sync", s_ve)
            p.wait("sync", s_se)
            for name, dst in tap_d.items():
                p.op("sync", "dma_start", out=dst, in_=srcs[name],
                     inc=[(s_dt, 16)])
                p.wait("sync", s_dt)

    p.wait("sync", s_doA, s_doA.n)
    p.wait("sync", s_doB, s_doB.n)
    p.wait("sync", s_dt, s_dt.n)
    p.wait("gpsimd", s_wA, s_wA.n)
    p.wait("gpsimd", s_wB, s_wB.n)
    p.wait("gpsimd", s_w12[0], s_w12[0].n)
    p.wait("gpsimd", s_w12[1], s_w12[1].n)

    block = es.enter_context(nc.Block())
    p.run(block)
    es.close()
    return nc


_NC_CACHE = {}


def _get_nc(taps=()):
    key = tuple(sorted(taps))
    if key not in _NC_CACHE:
        _NC_CACHE[key] = build_nc(taps=key)
    return _NC_CACHE[key]


def make_core_inputs(inputs):
    f = np.ascontiguousarray
    shared = {
        "wq": f(np.asarray(inputs["L0_attn_Wq"], np.float32)),
        "wk": f(np.asarray(inputs["L0_attn_Wk"], np.float32)),
        "wv": f(np.asarray(inputs["L0_attn_Wv"], np.float32)),
        "wg": f(np.asarray(inputs["L0_attn_Wg"], np.float32)),
        "wo": f(np.asarray(inputs["L0_attn_Wo"], np.float32)),
        "wm": f(np.asarray(inputs["L0_ffn_Wm"], np.float32)),
        "wo2": f(np.asarray(inputs["L0_ffn_Wo"], np.float32)),
        "wgk1": f(np.asarray(inputs["L0_attn_Wgk1"], np.float32)),
        "wgk2": f(np.asarray(inputs["L0_attn_Wgk2"], np.float32)),
        "w1": f(np.asarray(inputs["L0_ffn_W1"], np.float32)),
        "w2": f(np.asarray(inputs["L0_ffn_W2"], np.float32)),
        "emb": f(np.asarray(inputs["L0_ffn_expert_emb"], np.float32)),
        "b2": f(np.asarray(inputs["L0_ffn_b2"], np.float32)),
        "bgk2l": f(np.asarray(inputs["L0_attn_bgk2"], np.float32).reshape(4, 128).T),
        "bml": f(np.asarray(inputs["L0_ffn_bm"], np.float32).reshape(8, 128).T),
        "b1l": f(np.asarray(inputs["L0_ffn_b1"], np.float32).reshape(E, 4, 128).transpose(2, 0, 1)),
        "bob": f(np.broadcast_to(np.asarray(inputs["L0_ffn_bo"], np.float32), (128, D))),
        "gwb": f(np.broadcast_to(np.asarray(inputs["L0_attn_gw"], np.float32), (128, dv))),
        "ident": np.eye(128, dtype=ml_dtypes.bfloat16),
        "identf": np.eye(128, dtype=np.float32),
        "trimask": f(np.triu(np.ones((128, 128))).astype(ml_dtypes.bfloat16)),
    }
    x = np.asarray(inputs["L0_attn_x"], np.float32)
    in_maps = []
    for core in range(8):
        m = dict(shared)
        m["x"] = f(x[core])
        in_maps.append(m)
    return in_maps


def kernel(**inputs):
    nc = _get_nc()
    in_maps = make_core_inputs(inputs)
    res = run_bass_kernel_spmd(nc, in_maps, core_ids=list(range(8)))
    return np.stack([r["out"] for r in res.results])


# version 12
# speedup vs baseline: 1.0893x; 1.0184x over previous
import numpy as np
import ml_dtypes
import contextlib
import concourse.bass as bass
import concourse.mybir as mybir
from concourse.bass_utils import run_bass_kernel_spmd

F32 = mybir.dt.float32
BF16 = mybir.dt.bfloat16
ACT = mybir.ActivationFunctionType
ALU = mybir.AluOpType
AX = mybir.AxisListType

T, D = 2048, 1024
NTILE, TT, C, NCH = 4, 512, 128, 4
NH, dk, dv, KC = 4, 128, 256, 8
H, HD, E, I = 8, 128, 16, 512
EPS = 1e-5
LNSCALE = float(np.log(128.0 ** -0.5))
ENGS = ("tensor", "vector", "scalar", "gpsimd", "sync")


class Sem:
    __slots__ = ("h", "n")

    def __init__(self, h):
        self.h = h
        self.n = 0


import os
SERIAL = bool(int(os.environ.get("KERNEL_SERIAL", "0")))


class Prog:
    def __init__(self, nc, es):
        self.nc = nc
        self.es = es
        self.ops = {e: [] for e in ENGS}
        self.esem = {}

    def sem(self, name):
        return Sem(self.es.enter_context(self.nc.semaphore(name)))

    def wait(self, eng, sem, val=None):
        v = sem.n if val is None else val
        assert v <= sem.n, f"wait on future increment ({v} > {sem.n})"
        if v <= 0 or SERIAL:
            return
        self.ops[eng].append(lambda e: e.wait_ge(sem.h, v))

    def op(self, eng, meth, *a, inc=(), **kw):
        inc = list(inc)
        if SERIAL:
            for s, k in inc:
                s.n += k
            for oe, os_ in self.esem.items():
                if os_.n > 0:
                    v = os_.n
                    self.ops[eng].append(
                        (lambda sem, vv: lambda e: e.wait_ge(sem.h, vv))(os_, v))
            own = self.esem[eng]
            inc = [(own, 16 if meth == "dma_start" else 1)]
        for s, k in inc:
            s.n += k

        def f(e):
            inst = getattr(e, meth)(*a, **kw)
            for s, k in inc:
                inst.then_inc(s.h, k)
        self.ops[eng].append(f)

    def run(self, block):
        for eng in ENGS:
            oplist = self.ops[eng]

            def make(lst):
                def body(e):
                    for f in lst:
                        f(e)
                return body
            getattr(block, eng)(make(oplist))


def build_nc(taps=()):
    nc = bass.Bass("TRN2", target_bir_lowering=False)
    dr = lambda n, s, d=F32, k="ExternalInput": nc.dram_tensor(n, s, d, kind=k).ap()
    x_d = dr("x", [T, D])
    wq_d = dr("wq", [D, 512])
    wk_d = dr("wk", [D, 512])
    wv_d = dr("wv", [D, D])
    wg_d = dr("wg", [D, D])
    wo_d = dr("wo", [D, D])
    wm_d = dr("wm", [D, D])
    wo2_d = dr("wo2", [D, D])
    wgk1_d = dr("wgk1", [D, 16])
    wgk2_d = dr("wgk2", [16, 512])
    w1_d = dr("w1", [E, HD, I])
    w2_d = dr("w2", [E, I, HD])
    emb_d = dr("emb", [HD, E])
    b2_d = dr("b2", [E, HD])
    bgk2l_d = dr("bgk2l", [128, 4])
    bml_d = dr("bml", [128, 8])
    b1l_d = dr("b1l", [128, E, 4])
    bob_d = dr("bob", [128, D])
    gwb_d = dr("gwb", [128, dv])
    ident_d = dr("ident", [128, 128], BF16)
    identf_d = dr("identf", [128, 128])
    trimask_d = dr("trimask", [128, 128], BF16)
    out_d = dr("out", [T, D], F32, "ExternalOutput")
    tap_shapes = {"q": ([128, 4, TT], BF16), "k": ([128, 4, TT], BF16),
                  "gksp": ([128, 4, TT], F32), "v": ([128, NCH, D], BF16),
                  "g": ([128, NCH, D], BF16), "h": ([128, H, TT], BF16),
                  "wu": ([128, NCH, H, 32], F32), "rsum": ([128, NCH, H], F32),
                  "S": ([128, NH, dv], F32), "orms": ([128, NCH, D], BF16),
                  "resid": ([128, NCH, D], BF16), "acc": ([128, NCH, D], F32),
                  "xnt": ([128, KC, TT], BF16), "sums": ([128, NCH, H], F32),
                  "lgd": ([128, NCH, 128], F32), "smc": ([128, 128], F32),
                  "xs": ([128, NCH, D], F32), "lnst": ([128, NCH, 8], F32)}
    tap_d = {tp: dr("tap_" + tp, *tap_shapes[tp], k="ExternalOutput") for tp in taps}

    es = contextlib.ExitStack()
    sb = lambda n, s, d=F32: es.enter_context(nc.sbuf_tensor(n, s, d))

    wq_s = sb("wq_s", [128, KC, 512], BF16)
    wk_s = sb("wk_s", [128, KC, 512], BF16)
    wgk1_s = sb("wgk1_s", [128, KC, 16], BF16)
    wgk2_s = sb("wgk2_s", [16, 4, 128], BF16)
    emb_s = sb("emb_s", [128, E], BF16)
    b2_s = sb("b2_s", [16, 128], BF16)
    b1l_s = sb("b1l_s", [128, E, 4])
    bml_s = sb("bml_s", [128, 8])
    bgk2r_s = sb("bgk2r_s", [128, 4])
    bgk2n_s = sb("bgk2n_s", [128, 4])
    bob_s = sb("bob_s", [128, D])
    gwb_s = sb("gwb_s", [128, dv])
    ident_s = sb("ident_s", [128, 128], BF16)
    identf_s = sb("identf_s", [128, 128])
    trimask_s = sb("trimask_s", [128, 128], BF16)
    ones_s = sb("ones_s", [128, 128])
    ones_s2 = sb("ones_s2", [128, 512])
    consts_s = sb("consts_s", [128, 3])
    wbigA = sb("wbigA", [128, KC, D], BF16)
    wbigB = sb("wbigB", [128, KC, D], BF16)
    w1b = sb("w1b", [128, 2, I], BF16)
    w2b = sb("w2b", [128, 2, 4, HD], BF16)
    x_s = sb("x_s", [128, NCH, D])
    xn_tm = sb("xn_tm", [128, D], BF16)
    xnT = sb("xnT", [128, KC, TT], BF16)
    qT = sb("qT", [128, NH, TT], BF16)
    kT = sb("kT", [128, NH, TT], BF16)
    gksp = sb("gksp", [128, NCH, NH, 128])
    rstd_s = sb("rstd_s", [128, 16])
    v_s = sb("v_s", [128, NCH, D], BF16)
    g_s = sb("g_s", [128, NCH, D], BF16)
    hT = sb("hT", [128, H, TT], BF16)
    gk1b = sb("gk1b", [16, TT], BF16)
    e1_s = sb("e1_s", [128, 512])
    sm_a = sb("sm_a", [128, 128])
    sm_b = sb("sm_b", [128, 128])
    sm_c = sb("sm_c", [128, 128])
    wu_s = sb("wu_s", [128, NCH, H, 32])
    wuT_s = sb("wuT_s", [16, H, 128], BF16)
    rsum_s = sb("rsum_s", [128, NCH, H])
    sums_s = sb("sums_s", [128, NCH, H])
    stats = sb("stats", [128, 32])
    S_s = sb("S_s", [128, NH, dv])
    S_bf = sb("S_bf", [128, NH, dv], BF16)
    A4 = sb("A4", [128, NH, 128])
    lam4 = sb("lam4", [128, NH, 128])
    Pb4 = sb("Pb4", [128, NH, 128], BF16)
    khT_s = sb("khT_s", [128, NCH, NH, 128], BF16)
    o_rms = sb("o_rms", [128, NCH, D], BF16)
    resid = sb("resid", [128, NCH, D], BF16)
    acc = sb("acc", [128, NCH, D])
    comb_bf = v_s
    hmidS = sb("hmidS", [128, 4, H, 256], BF16)
    kh4 = hmidS.rearrange("p a h n -> p (a h n)")[:, 0:NCH * NH * 128] \
        .rearrange("p (c h n) -> p c h n", c=NCH, h=NH)
    outst = sb("outst", [128, 2, 512])

    ph = [es.enter_context(nc.psum_tensor(f"ph{i}", [128, 1024], F32)) for i in range(2)]
    pw = [es.enter_context(nc.psum_tensor(f"pw{i}", [128, 512], F32)) for i in range(3)]
    ptp = es.enter_context(nc.psum_tensor("ptp", [128, 1024], BF16))

    p = Prog(nc, es)
    s_dx = p.sem("s_dx")
    s_di = p.sem("s_di")
    s_winit = p.sem("s_winit")
    s_wA = p.sem("s_wA")
    s_wB = p.sem("s_wB")
    s_w12 = [p.sem("s_w12a"), p.sem("s_w12b")]
    s_doA = p.sem("s_doA")
    s_doB = p.sem("s_doB")
    s_dt = p.sem("s_dt")
    s_pe = p.sem("s_pe")
    s_ve = p.sem("s_ve")
    s_se = p.sem("s_se")
    if SERIAL:
        p.esem = {e: p.sem("ser_" + e) for e in ENGS}

    guards = {}

    def guard_wait(eng, key):
        g = guards.get(key)
        if g is not None:
            p.wait(eng, g[0], g[1])

    def guard_set(key, sem):
        guards[key] = (sem, sem.n)

    for dst, src in [(b1l_s, b1l_d), (bml_s, bml_d), (bgk2r_s, bgk2l_d),
                     (bob_s, bob_d), (gwb_s, gwb_d), (ident_s, ident_d),
                     (identf_s, identf_d), (trimask_s, trimask_d)]:
        p.op("sync", "dma_start", out=dst[:], in_=src, inc=[(s_di, 16)])
    p.op("gpsimd", "dma_start", out=wq_s[:],
         in_=wq_d.rearrange("(kk p) n -> p kk n", p=128), inc=[(s_winit, 16)])
    p.op("gpsimd", "dma_start", out=wk_s[:],
         in_=wk_d.rearrange("(kk p) n -> p kk n", p=128), inc=[(s_winit, 16)])
    p.op("gpsimd", "dma_start", out=wgk1_s[:],
         in_=wgk1_d.rearrange("(kk p) n -> p kk n", p=128), inc=[(s_winit, 16)])
    p.op("gpsimd", "dma_start", out=wgk2_s[:],
         in_=wgk2_d.rearrange("k (m n) -> k m n", m=4), inc=[(s_winit, 16)])
    p.op("gpsimd", "dma_start", out=emb_s[:], in_=emb_d, inc=[(s_winit, 16)])
    p.op("gpsimd", "dma_start", out=b2_s[:], in_=b2_d, inc=[(s_winit, 16)])
    p.wait("vector", s_di)
    p.wait("tensor", s_di)
    p.wait("tensor", s_winit)
    p.wait("scalar", s_di)
    p.op("vector", "memset", wu_s[:], 0.0)
    p.op("vector", "memset", S_s[:], 0.0)
    p.op("vector", "memset", S_bf[:], 0.0)
    p.op("vector", "memset", ones_s[:], 1.0)
    p.op("vector", "memset", ones_s2[:], 1.0)
    p.op("vector", "memset", consts_s[:, 0:1], EPS)
    p.op("vector", "memset", consts_s[:, 1:2], 1.0)
    p.op("vector", "memset", consts_s[:, 2:3], LNSCALE)
    p.op("vector", "memset", stats[:, 8:9], 0.0)
    p.op("vector", "tensor_scalar_mul", out=bgk2n_s[:], in0=bgk2r_s[:],
         scalar1=-1.0, inc=[(s_ve, 1)])

    wsem = {"wbigA": s_wA, "wbigB": s_wB}

    def wload(dst_s, src_d, key):
        guard_wait("gpsimd", key)
        sem = wsem[key]
        p.op("gpsimd", "dma_start", out=dst_s[:],
             in_=src_d.rearrange("(kk p) n -> p kk n", p=128), inc=[(sem, 16)])
        return (sem, sem.n)

    def w12load(e):
        b = e % 2
        guard_wait("gpsimd", f"w1b{b}")
        p.op("gpsimd", "dma_start", out=w1b[:, b, :], in_=w1_d[e], inc=[(s_w12[b], 16)])
        p.op("gpsimd", "dma_start", out=w2b[:, b, :, :],
             in_=w2_d[e].rearrange("(i p) n -> p i n", p=128), inc=[(s_w12[b], 16)])

    def transpose_slab(src2d, dstT, c, nblk=8):
        guard_wait("tensor", "ptp")
        for i in range(nblk):
            p.op("tensor", "transpose", ptp[:, i * 128:(i + 1) * 128],
                 src2d[:, i * 128:(i + 1) * 128], ident_s[:],
                 inc=[(s_pe, 1)] if i == nblk - 1 else ())
        p.wait("vector", s_pe)
        p.op("vector", "tensor_copy",
             out=dstT[:, 0:nblk, c * 128:(c + 1) * 128],
             in_=ptp.rearrange("p (i n) -> p i n", i=8)[:, 0:nblk, :],
             inc=[(s_ve, 1)])
        guard_set("ptp", s_ve)

    sbf_se = {h: 0 for h in range(NH)}
    gate_ve = {}
    gate_se = {}
    kh_ve_c = {}
    gbank = [0]

    def xload(t):
        for c in range(NCH):
            guard_wait("sync", f"x_s{c}")
            p.op("sync", "dma_start", out=x_s[:, c, :],
                 in_=x_d[t * TT + c * 128: t * TT + (c + 1) * 128, :],
                 inc=[(s_dx, 16)])
        return s_dx.n

    xval = xload(0)
    wv_ready = wload(wbigA, wv_d, "wbigA")
    wg_ready = wload(wbigB, wg_d, "wbigB")
    ln_ve = {}

    def ln_batched(xv):
        p.wait("scalar", s_dx, xv)
        for c in range(NCH):
            p.op("scalar", "activation", out=xn_tm[:], in_=x_s[:, c, :],
                 func=ACT.Square, accum_out=stats[:, c:c + 1], inc=[(s_se, 1)])
        sq_se = s_se.n
        p.wait("vector", s_dx, xv)
        p.op("vector", "reduce_sum", out=stats[:, 4:8], in_=x_s[:], axis=AX.X,
             inc=[(s_ve, 1)])
        p.wait("vector", s_ve)
        p.op("vector", "tensor_scalar_mul", out=stats[:, 4:8], in0=stats[:, 4:8],
             scalar1=1.0 / D, inc=[(s_ve, 1)])
        p.wait("vector", s_ve)
        p.op("vector", "tensor_mul", out=stats[:, 12:16], in0=stats[:, 4:8],
             in1=stats[:, 4:8], inc=[(s_ve, 1)])
        p.wait("vector", s_se, sq_se)
        p.op("vector", "tensor_scalar_mul", out=stats[:, 0:4], in0=stats[:, 0:4],
             scalar1=1.0 / D, inc=[(s_ve, 1)])
        p.wait("vector", s_ve)
        p.op("vector", "tensor_tensor", out=stats[:, 0:4], in0=stats[:, 0:4],
             in1=stats[:, 12:16], op=ALU.subtract, inc=[(s_ve, 1)])
        p.wait("scalar", s_ve)
        p.op("scalar", "activation", out=stats[:, 12:16], in_=stats[:, 0:4],
             func=ACT.Ln, bias=consts_s[:, 0:1], inc=[(s_se, 1)])
        p.wait("scalar", s_se)
        p.op("scalar", "activation", out=stats[:, 0:4], in_=stats[:, 12:16],
             func=ACT.Exp, scale=-0.5, inc=[(s_se, 1)])
        return s_se.n

    def ln_xn(c, rdy_se):
        p.wait("vector", s_se, rdy_se)
        p.wait("vector", s_pe)
        p.op("vector", "tensor_scalar", out=o_rms[:, c, :], in0=x_s[:, c, :],
             scalar1=stats[:, 4 + c:5 + c], scalar2=stats[:, c:c + 1],
             op0=ALU.subtract, op1=ALU.mult, inc=[(s_ve, 1)])
        ln_ve[c] = s_ve.n

    rdy = ln_batched(xval)
    for c in range(NCH):
        ln_xn(c, rdy)

    for t in range(NTILE):
        for c in range(NCH):
            p.wait("tensor", s_ve, ln_ve[c])
            transpose_slab(o_rms[:, c, :], xnT, c)
        xn_done = s_ve.n

        def proj_fm(wsb, dstT, mrange, bias_l=None, evac_eng="vector"):
            for mi, m in enumerate(mrange):
                bk = mi % 2
                bank = pw[bk]
                guard_wait("tensor", f"pw{bk}")
                for kk in range(KC):
                    p.op("tensor", "matmul", bank[:],
                         wsb[:, kk, m * 128:(m + 1) * 128], xnT[:, kk, :],
                         start=(kk == 0), stop=(kk == KC - 1),
                         inc=[(s_pe, 1)] if kk == KC - 1 else ())
                if evac_eng == "vector":
                    p.wait("vector", s_pe)
                    p.op("vector", "tensor_copy", out=dstT[:, m, :], in_=bank[:],
                         inc=[(s_ve, 1)])
                    guard_set(f"pw{bk}", s_ve)
                else:
                    p.wait("scalar", s_pe)
                    p.op("scalar", "activation", out=dstT[:, m, :], in_=bank[:],
                         func=ACT.Identity, bias=bias_l[:, m:m + 1], inc=[(s_se, 1)])
                    guard_set(f"pw{bk}", s_se)

        p.wait("tensor", s_ve, xn_done)
        proj_fm(wq_s, qT, range(NH))
        proj_fm(wk_s, kT, range(NH))

        guard_wait("tensor", "pw2")
        for kk in range(KC):
            p.op("tensor", "matmul", pw[2][0:16, 0:TT], wgk1_s[:, kk, :],
                 xnT[:, kk, :], start=(kk == 0), stop=(kk == KC - 1),
                 inc=[(s_pe, 1)] if kk == KC - 1 else ())
        p.wait("vector", s_pe)
        p.op("vector", "tensor_copy", out=gk1b[:], in_=pw[2][0:16, 0:TT],
             inc=[(s_ve, 1)])
        guard_set("pw2", s_ve)
        gk1_ve = s_ve.n
        for m in range(4):
            bk = m % 2
            bank = pw[bk]
            guard_wait("tensor", f"pw{bk}")
            p.wait("tensor", s_ve, gk1_ve)
            p.op("tensor", "matmul", bank[:], wgk2_s[:, m, :], gk1b[:],
                 start=True, stop=True, inc=[(s_pe, 1)])
            p.wait("scalar", s_pe)
            p.op("scalar", "activation", out=e1_s[:], in_=bank[:],
                 func=ACT.Exp, scale=-1.0, bias=bgk2n_s[:, m:m + 1], inc=[(s_se, 1)])
            p.wait("scalar", s_se)
            p.op("scalar", "activation", out=gksp[:, :, m, :], in_=e1_s[:],
                 func=ACT.Ln, bias=consts_s[:, 1:2], inc=[(s_se, 1)])
            guard_set(f"pw{bk}", s_se)

        def tm_chunk(wsb, dst, wready, ci):
            p.wait("tensor", wready[0], wready[1])
            for half in range(2):
                bk = (ci * 2 + half) % 2
                bank = pw[bk]
                guard_wait("tensor", f"pw{bk}")
                for kk in range(KC):
                    p.op("tensor", "matmul", bank[:],
                         xnT[:, kk, ci * 128:(ci + 1) * 128],
                         wsb[:, kk, half * 512:(half + 1) * 512],
                         start=(kk == 0), stop=(kk == KC - 1),
                         inc=[(s_pe, 1)] if kk == KC - 1 else ())
                p.wait("scalar", s_pe)
                p.op("scalar", "copy",
                     out=dst[:, ci, half * 512:(half + 1) * 512],
                     in_=bank[:], inc=[(s_se, 1)])
                guard_set(f"pw{bk}", s_se)

        gk_done_se = s_se.n

        def gates_chunk(c):
            cs = slice(c * 128, (c + 1) * 128)
            A2 = A4.rearrange("p h n -> p (h n)")
            p.wait("vector", s_se, gk_done_se)
            p.op("vector", "tensor_tensor_scan", out=A2, data0=ones_s2[:],
                 data1=gksp[:, c].rearrange("p h n -> p (h n)"),
                 initial=0.0, op0=ALU.mult, op1=ALU.add, inc=[(s_ve, 1)])
            p.wait("vector", s_ve)
            p.op("vector", "tensor_copy", out=stats[:, 9:12],
                 in_=A4[:, 0:3, 127], inc=[(s_ve, 1)])
            p.wait("vector", s_ve)
            pcb = stats[:, 8:12].rearrange("p (h o) -> p h o", h=NH).broadcast_to([128, NH, 128])
            p.op("vector", "tensor_tensor", out=A4[:], in0=A4[:], in1=pcb,
                 op=ALU.subtract, inc=[(s_ve, 1)])
            A_ve = s_ve.n
            p.wait("scalar", s_ve, A_ve)
            p.op("scalar", "activation", out=rstd_s[:, c * NH:(c + 1) * NH],
                 in_=A4[:, :, 127],
                 func=ACT.Exp, scale=-1.0 / 16.0, inc=[(s_se, 1)])
            p.op("scalar", "activation", out=lam4.rearrange("p h n -> p (h n)"),
                 in_=A2, func=ACT.Exp, scale=1.0 / 16.0, inc=[(s_se, 1)])
            p.wait("vector", s_se)
            p.op("vector", "tensor_mul", out=kT[:, :, cs], in0=kT[:, :, cs],
                 in1=lam4[:], inc=[(s_ve, 1)])
            kt_ve = s_ve.n
            p.wait("vector", s_ve)
            dsb = rstd_s[:, c * NH:(c + 1) * NH] \
                .rearrange("p (h o) -> p h o", h=NH).broadcast_to([128, NH, 128])
            if c == 0:
                guard_wait("vector", "hmidS0")
                guard_wait("vector", "hmidS1")
            p.op("vector", "tensor_tensor", out=kh4[:, c], in0=kT[:, :, cs], in1=dsb,
                 op=ALU.mult, inc=[(s_ve, 1)])
            kh_ve_c[c] = s_ve.n
            p.wait("scalar", s_ve, kt_ve)
            p.op("scalar", "activation", out=lam4.rearrange("p h n -> p (h n)"),
                 in_=A2, func=ACT.Exp, scale=-1.0 / 16.0, bias=consts_s[:, 2:3],
                 inc=[(s_se, 1)])
            p.wait("vector", s_se)
            p.op("vector", "tensor_mul", out=qT[:, :, cs], in0=qT[:, :, cs],
                 in1=lam4[:], inc=[(s_ve, 1)])
            gate_ve[c] = s_ve.n
            gate_se[c] = s_se.n

        def khT_slab(c):
            guard_wait("tensor", "ptp")
            p.wait("tensor", s_ve, kh_ve_c[c])
            for h in range(NH):
                p.op("tensor", "transpose", ptp[:, h * 128:(h + 1) * 128],
                     kh4[:, c, h, :], ident_s[:],
                     inc=[(s_pe, 1)] if h == NH - 1 else ())
            p.wait("vector", s_pe)
            p.op("vector", "tensor_copy", out=khT_s[:, c, :, :],
                 in_=ptp.rearrange("p (i n) -> p i n", i=8)[:, 0:NH, :],
                 inc=[(s_ve, 1)])
            guard_set("ptp", s_ve)

        for c in range(NCH):
            gates_chunk(c)
            tm_chunk(wbigA, v_s, wv_ready, c)
            if c >= 1:
                khT_slab(c - 1)
        guards["wbigA"] = (s_pe, s_pe.n)
        wm_ready = wload(wbigA, wm_d, "wbigA")
        for c in range(NCH):
            tm_chunk(wbigB, g_s, wg_ready, c)
            if c == 0:
                khT_slab(NCH - 1)
                guards["hmidS0"] = guards["hmidS1"] = (s_ve, s_ve.n)
        guards["wbigB"] = (s_pe, s_pe.n)
        wo_ready = wload(wbigB, wo_d, "wbigB")
        p.wait("tensor", wm_ready[0], wm_ready[1])
        proj_fm(wbigA, hT, range(H), bias_l=bml_s, evac_eng="scalar")
        guards["wbigA"] = (s_pe, s_pe.n)
        if t + 1 < NTILE:
            wv_ready = wload(wbigA, wv_d, "wbigA")

        h_done_se = s_se.n
        for c in range(NCH):
            guard_wait("tensor", "pw2")
            p.wait("tensor", s_se, h_done_se)
            for j in range(H):
                p.op("tensor", "matmul", pw[2][:, j * 16:(j + 1) * 16],
                     hT[:, j, c * 128:(c + 1) * 128], emb_s[:],
                     start=True, stop=True,
                     inc=[(s_pe, 1)] if j == H - 1 else ())
            lg = pw[2][:, 0:128].rearrange("p (j e) -> p j e", j=H)
            p.wait("vector", s_pe)
            p.op("vector", "tensor_reduce", out=sm_a[:, 0:8], in_=lg, axis=AX.X,
                 op=ALU.max, negate=True, inc=[(s_ve, 1)])
            p.wait("vector", s_ve)
            nmx = sm_a[:, 0:8].rearrange("p (j o) -> p j o", j=H).broadcast_to([128, H, 16])
            p.wait("vector", s_se)
            p.op("vector", "tensor_tensor",
                 out=sm_b[:].rearrange("p (j e) -> p j e", j=H), in0=lg, in1=nmx,
                 op=ALU.add, inc=[(s_ve, 1)])
            guard_set("pw2", s_ve)
            p.wait("scalar", s_ve)
            p.op("scalar", "activation", out=sm_c[:], in_=sm_b[:], func=ACT.Exp,
                 inc=[(s_se, 1)])
            p.wait("vector", s_se)
            w3 = sm_c[:].rearrange("p (j e) -> p j e", j=H)
            p.op("vector", "tensor_reduce", out=sums_s[:, c, :], in_=w3, axis=AX.X,
                 op=ALU.add, inc=[(s_ve, 1)])
            p.wait("vector", s_ve)
            p.op("vector", "reciprocal", out=rsum_s[:, c, :], in_=sums_s[:, c, :])
            p.op("vector", "tensor_reduce", out=sm_b[:, 0:8], in_=w3, axis=AX.X,
                 op=ALU.max, inc=[(s_ve, 1)])
            p.wait("vector", s_ve)
            m1b = sm_b[:, 0:8].rearrange("p (j o) -> p j o", j=H).broadcast_to([128, H, 16])
            p.op("vector", "tensor_tensor",
                 out=sm_a[:].rearrange("p (j e) -> p j e", j=H), in0=w3, in1=m1b,
                 op=ALU.is_lt, inc=[(s_ve, 1)])
            p.wait("vector", s_ve)
            p.op("vector", "tensor_tensor",
                 out=sm_a[:].rearrange("p (j e) -> p j e", j=H), in0=w3,
                 in1=sm_a[:].rearrange("p (j e) -> p j e", j=H), op=ALU.mult,
                 inc=[(s_ve, 1)])
            p.wait("vector", s_ve)
            p.op("vector", "tensor_reduce", out=sm_b[:, 0:8],
                 in_=sm_a[:].rearrange("p (j e) -> p j e", j=H), axis=AX.X,
                 op=ALU.max, inc=[(s_ve, 1)])
            p.wait("vector", s_ve)
            m2b = sm_b[:, 0:8].rearrange("p (j o) -> p j o", j=H).broadcast_to([128, H, 16])
            p.op("vector", "tensor_tensor",
                 out=sm_a[:].rearrange("p (j e) -> p j e", j=H),
                 in0=w3, in1=m2b, op=ALU.is_ge, inc=[(s_ve, 1)])
            p.wait("vector", s_ve)
            p.op("vector", "tensor_tensor",
                 out=sm_b[:].rearrange("p (j e) -> p j e", j=H), in0=w3,
                 in1=sm_a[:].rearrange("p (j e) -> p j e", j=H),
                 op=ALU.mult, inc=[(s_ve, 1)])
            p.wait("vector", s_ve)
            rsb = rsum_s[:, c, :].rearrange("p (j o) -> p j o", j=H).broadcast_to([128, H, 16])
            p.op("vector", "tensor_tensor",
                 out=wu_s[:, c, :, 0:16],
                 in0=sm_b[:].rearrange("p (j e) -> p j e", j=H), in1=rsb,
                 op=ALU.mult, inc=[(s_ve, 1)])
            wu_ve = s_ve.n
            for jj in range(2):
                guard_wait("tensor", "pw2")
                p.wait("tensor", s_ve, wu_ve)
                for j4 in range(4):
                    j = jj * 4 + j4
                    p.op("tensor", "transpose", pw[2][0:16, j4 * 128:(j4 + 1) * 128],
                         wu_s[:, c, j, 0:16], identf_s[:],
                         inc=[(s_pe, 1)] if j4 == 3 else ())
                p.wait("vector", s_pe)
                p.op("vector", "tensor_copy",
                     out=wuT_s[:, jj * 4:(jj + 1) * 4, :],
                     in_=pw[2][0:16, 0:512].rearrange("p (j n) -> p j n", j=4),
                     inc=[(s_ve, 1)])
                guard_set("pw2", s_ve)
            wuT_ve = s_ve.n
            for jj in range(2):
                guard_wait("tensor", "pw2")
                p.wait("tensor", s_ve, wuT_ve)
                for j4 in range(4):
                    j = jj * 4 + j4
                    p.op("tensor", "matmul", pw[2][:, j4 * 128:(j4 + 1) * 128],
                         wuT_s[:, j, :], b2_s[:], start=True, stop=True,
                         inc=[(s_pe, 1)] if j4 == 3 else ())
                p.wait("vector", s_pe)
                p.op("vector", "tensor_copy",
                     out=acc[:, c, jj * 512:(jj + 1) * 512],
                     in_=pw[2][:, 0:512], inc=[(s_ve, 1)])
                guard_set("pw2", s_ve)

        def head_gen(c, h):
            if True:
                ch = c * NH + h
                cs = slice(c * 128, (c + 1) * 128)
                vsl = v_s[:, c, h * dv:(h + 1) * dv]
                bk = gbank[0] % 3
                gbank[0] += 1
                bank = pw[bk]
                guard_wait("tensor", f"pw{bk}")
                p.wait("tensor", s_ve, gate_ve[c])
                p.wait("tensor", s_se, sbf_se[h])
                p.op("tensor", "matmul", bank[:, 256:384], kT[:, h, cs],
                     qT[:, h, cs], start=True, stop=True)
                p.op("tensor", "matmul", bank[:, 0:dv], qT[:, h, cs],
                     S_bf[:, h, :], start=True, stop=False, inc=[(s_pe, 1)])
                p.wait("vector", s_pe)
                p.op("vector", "tensor_mul", out=Pb4[:, h, :], in0=bank[:, 256:384],
                     in1=trimask_s[:], inc=[(s_ve, 1)])
                mask_ve = s_ve.n
                yield
                p.wait("tensor", s_ve, mask_ve)
                p.op("tensor", "matmul", bank[:, 0:dv], Pb4[:, h, :], vsl,
                     start=False, stop=True, inc=[(s_pe, 1)])
                o_pe = s_pe.n
                p.wait("scalar", s_pe, o_pe)
                p.op("scalar", "activation",
                     out=o_rms[:, c, h * dv:(h + 1) * dv], in_=bank[:, 0:dv],
                     func=ACT.Square, accum_out=stats[:, 16 + ch:17 + ch],
                     inc=[(s_se, 1)])
                yield
                p.wait("vector", s_se)
                p.op("vector", "tensor_copy", out=o_rms[:, c, h * dv:(h + 1) * dv],
                     in_=bank[:, 0:dv], inc=[(s_ve, 1)])
                oev_ve = s_ve.n
                yield
                p.wait("tensor", s_ve, oev_ve)
                p.op("tensor", "matmul", bank[:, 256:512], khT_s[:, c, h, :], vsl,
                     start=True, stop=True, inc=[(s_pe, 1)])
                smm_pe = s_pe.n
                p.wait("vector", s_pe, smm_pe)
                p.wait("vector", s_se, gate_se[c])
                p.op("vector", "scalar_tensor_tensor", out=S_s[:, h, :],
                     in0=S_s[:, h, :], scalar=rstd_s[:, c * NH + h:c * NH + h + 1],
                     in1=bank[:, 256:512], op0=ALU.mult, op1=ALU.add,
                     inc=[(s_ve, 1)])
                guard_set(f"pw{bk}", s_ve)
                p.wait("scalar", s_ve)
                p.op("scalar", "activation", out=S_bf[:, h, :], in_=S_s[:, h, :],
                     func=ACT.Copy, inc=[(s_se, 1)])
                sbf_se[h] = s_se.n
                yield

        for c in range(NCH):
            for h in range(NH):
                for _ in head_gen(c, h):
                    pass

        w12load(0)
        w12load(1)
        p.wait("scalar", s_se)
        p.op("scalar", "activation", out=rstd_s[:], in_=stats[:, 16:32],
             func=ACT.Ln, scale=1.0 / dv, bias=consts_s[:, 0:1], inc=[(s_se, 1)])
        p.wait("scalar", s_se)
        p.op("scalar", "activation", out=stats[:, 16:32], in_=rstd_s[:],
             func=ACT.Exp, scale=-0.5, inc=[(s_se, 1)])
        exp_se = s_se.n
        for c in range(NCH):
            p.op("scalar", "activation", out=g_s[:, c, :], in_=g_s[:, c, :],
                 func=ACT.Silu, inc=[(s_se, 1)])
        silu_se = s_se.n
        p.wait("vector", s_se, exp_se)
        for c in range(NCH):
            for h in range(NH):
                ch = c * NH + h
                p.op("vector", "scalar_tensor_tensor",
                     out=o_rms[:, c, h * dv:(h + 1) * dv],
                     in0=o_rms[:, c, h * dv:(h + 1) * dv],
                     scalar=stats[:, 16 + ch:17 + ch], in1=gwb_s[:],
                     op0=ALU.mult, op1=ALU.mult, inc=[(s_ve, 1)])
        p.wait("vector", s_ve)
        p.wait("vector", s_se, silu_se)
        for c in range(NCH):
            p.op("vector", "tensor_mul", out=o_rms[:, c, :], in0=o_rms[:, c, :],
                 in1=g_s[:, c, :], inc=[(s_ve, 1)])
        p.wait("tensor", s_ve)
        for c in range(NCH):
            transpose_slab(o_rms[:, c, :], xnT, c)
        oT_ve = s_ve.n
        p.wait("tensor", wo_ready[0], wo_ready[1])
        p.wait("tensor", s_ve, oT_ve)
        for c in range(NCH):
            for half in range(2):
                bk = (c * 2 + half) % 2
                bank = pw[bk]
                guard_wait("tensor", f"pw{bk}")
                for kk in range(KC):
                    p.op("tensor", "matmul", bank[:],
                         xnT[:, kk, c * 128:(c + 1) * 128],
                         wbigB[:, kk, half * 512:(half + 1) * 512],
                         start=(kk == 0), stop=(kk == KC - 1),
                         inc=[(s_pe, 1)] if kk == KC - 1 else ())
                p.wait("vector", s_pe)
                p.op("vector", "tensor_add", out=e1_s[:], in0=bank[:],
                     in1=x_s[:, c, half * 512:(half + 1) * 512], inc=[(s_ve, 1)])
                guard_set(f"pw{bk}", s_ve)
                p.wait("vector", s_ve)
                p.op("vector", "tensor_add",
                     out=resid[:, c, half * 512:(half + 1) * 512],
                     in0=e1_s[:], in1=bob_s[:, half * 512:(half + 1) * 512])
            guard_set(f"x_s{c}", s_ve)
        guards["wbigB"] = (s_pe, s_pe.n)
        wo2_ready = wload(wbigB, wo2_d, "wbigB")
        if t + 1 < NTILE:
            xval = xload(t + 1)

        for e in range(E):
            b = e % 2
            p.wait("tensor", s_w12[b])
            mm1_done = {}
            for h2 in range(2):
                for jh in range(2):
                    g = guards.get(f"hmidS{jh}")
                    for i in range(4):
                        pb = (h2 * 2 + jh + i) % 2
                        bank = ph[pb]
                        guard_wait("tensor", f"ph{pb}")
                        for j4 in range(4):
                            j = jh * 4 + j4
                            p.op("tensor", "matmul", bank[:, j4 * 256:(j4 + 1) * 256],
                                 w1b[:, b, i * 128:(i + 1) * 128],
                                 hT[:, j, h2 * 256:(h2 + 1) * 256],
                                 start=True, stop=True,
                                 inc=[(s_pe, 1)] if j4 == 3 else ())
                        if g is not None:
                            p.wait("scalar", g[0], g[1])
                            g = None
                        p.wait("scalar", s_pe)
                        p.op("scalar", "activation",
                             out=hmidS[:, i, jh * 4:(jh + 1) * 4, :],
                             in_=bank[:], func=ACT.Gelu,
                             bias=b1l_s[:, e, i:i + 1], inc=[(s_se, 1)])
                        guard_set(f"ph{pb}", s_se)
                    mm1_done[jh] = s_se.n
                for jh in range(2):
                    p.wait("tensor", s_se, mm1_done[jh])
                    for j4 in range(4):
                        j = jh * 4 + j4
                        bk = j % 2
                        bank = pw[bk]
                        guard_wait("tensor", f"pw{bk}")
                        for c2 in range(2):
                            for i in range(4):
                                p.op("tensor", "matmul",
                                     bank[:, c2 * 128:(c2 + 1) * 128],
                                     hmidS[:, i, j, c2 * 128:(c2 + 1) * 128],
                                     w2b[:, b, i, :], start=(i == 0), stop=(i == 3),
                                     inc=[(s_pe, 1)] if i == 3 else ())
                        p.wait("vector", s_pe)
                        for c2 in range(2):
                            cc = h2 * 2 + c2
                            p.op("vector", "scalar_tensor_tensor",
                                 out=acc[:, cc, j * 128:(j + 1) * 128],
                                 in0=bank[:, c2 * 128:(c2 + 1) * 128],
                                 scalar=wu_s[:, cc, j, e:e + 1],
                                 in1=acc[:, cc, j * 128:(j + 1) * 128],
                                 op0=ALU.mult, op1=ALU.add,
                                 inc=[(s_ve, 1)] if c2 == 1 else ())
                        guard_set(f"pw{bk}", s_ve)
                    guard_set(f"hmidS{jh}", s_ve)
            guard_set(f"w1b{b}", s_pe)
            if e + 2 < E:
                w12load(e + 2)

        p.wait("vector", s_ve)
        for c in range(NCH):
            p.op("vector", "tensor_copy", out=comb_bf[:, c, :], in_=acc[:, c, :],
                 inc=[(s_ve, 1)])
        p.wait("tensor", s_ve)
        for c in range(NCH):
            transpose_slab(comb_bf[:, c, :], xnT, c)
        combT_ve = s_ve.n
        if t + 1 < NTILE:
            rdy = ln_batched(xval)
        p.wait("tensor", wo2_ready[0], wo2_ready[1])
        p.wait("tensor", s_ve, combT_ve)
        for c in range(NCH):
            for half in range(2):
                bk = (c * 2 + half) % 2
                bank = pw[bk]
                guard_wait("tensor", f"pw{bk}")
                for kk in range(KC):
                    p.op("tensor", "matmul", bank[:],
                         xnT[:, kk, c * 128:(c + 1) * 128],
                         wbigB[:, kk, half * 512:(half + 1) * 512],
                         start=(kk == 0), stop=(kk == KC - 1),
                         inc=[(s_pe, 1)] if kk == KC - 1 else ())
                ost = (c * 2 + half) % 2
                s_dox = [s_doA, s_doB][ost]
                guard_wait("vector", f"outst{ost}")
                p.wait("vector", s_pe)
                p.op("vector", "tensor_add", out=outst[:, ost, :], in0=bank[:],
                     in1=resid[:, c, half * 512:(half + 1) * 512], inc=[(s_ve, 1)])
                guard_set(f"pw{bk}", s_ve)
                p.wait("sync", s_ve)
                p.op("sync", "dma_start",
                     out=out_d[t * TT + c * 128: t * TT + (c + 1) * 128,
                               half * 512:(half + 1) * 512],
                     in_=outst[:, ost, :], inc=[(s_dox, 16)])
                guard_set(f"outst{ost}", s_dox)
            if t + 1 < NTILE:
                ln_xn(c, rdy)
        guards["wbigB"] = (s_pe, s_pe.n)
        if t + 1 < NTILE:
            wg_ready = wload(wbigB, wg_d, "wbigB")

        if t == 0 and tap_d:
            srcs = {"q": qT[:], "k": kT[:], "gksp": gksp[:], "v": v_s[:],
                    "g": g_s[:], "h": hT[:], "wu": wu_s[:], "rsum": rsum_s[:],
                    "S": S_s[:], "orms": o_rms[:], "resid": resid[:],
                    "acc": acc[:], "xnt": xnT[:], "sums": sums_s[:],
                    "smc": sm_c[:], "xs": x_s[:],
                    "smc": sm_c[:]}
            p.wait("sync", s_ve)
            p.wait("sync", s_se)
            for name, dst in tap_d.items():
                p.op("sync", "dma_start", out=dst, in_=srcs[name],
                     inc=[(s_dt, 16)])
                p.wait("sync", s_dt)

    p.wait("sync", s_doA, s_doA.n)
    p.wait("sync", s_doB, s_doB.n)
    p.wait("sync", s_dt, s_dt.n)
    p.wait("gpsimd", s_wA, s_wA.n)
    p.wait("gpsimd", s_wB, s_wB.n)
    p.wait("gpsimd", s_w12[0], s_w12[0].n)
    p.wait("gpsimd", s_w12[1], s_w12[1].n)

    block = es.enter_context(nc.Block())
    p.run(block)
    es.close()
    return nc


_NC_CACHE = {}


def _get_nc(taps=()):
    key = tuple(sorted(taps))
    if key not in _NC_CACHE:
        _NC_CACHE[key] = build_nc(taps=key)
    return _NC_CACHE[key]


def make_core_inputs(inputs):
    f = np.ascontiguousarray
    shared = {
        "wq": f(np.asarray(inputs["L0_attn_Wq"], np.float32)),
        "wk": f(np.asarray(inputs["L0_attn_Wk"], np.float32)),
        "wv": f(np.asarray(inputs["L0_attn_Wv"], np.float32)),
        "wg": f(np.asarray(inputs["L0_attn_Wg"], np.float32)),
        "wo": f(np.asarray(inputs["L0_attn_Wo"], np.float32)),
        "wm": f(np.asarray(inputs["L0_ffn_Wm"], np.float32)),
        "wo2": f(np.asarray(inputs["L0_ffn_Wo"], np.float32)),
        "wgk1": f(np.asarray(inputs["L0_attn_Wgk1"], np.float32)),
        "wgk2": f(np.asarray(inputs["L0_attn_Wgk2"], np.float32)),
        "w1": f(np.asarray(inputs["L0_ffn_W1"], np.float32)),
        "w2": f(np.asarray(inputs["L0_ffn_W2"], np.float32)),
        "emb": f(np.asarray(inputs["L0_ffn_expert_emb"], np.float32)),
        "b2": f(np.asarray(inputs["L0_ffn_b2"], np.float32)),
        "bgk2l": f(np.asarray(inputs["L0_attn_bgk2"], np.float32).reshape(4, 128).T),
        "bml": f(np.asarray(inputs["L0_ffn_bm"], np.float32).reshape(8, 128).T),
        "b1l": f(np.asarray(inputs["L0_ffn_b1"], np.float32).reshape(E, 4, 128).transpose(2, 0, 1)),
        "bob": f(np.broadcast_to(np.asarray(inputs["L0_ffn_bo"], np.float32), (128, D))),
        "gwb": f(np.broadcast_to(np.asarray(inputs["L0_attn_gw"], np.float32), (128, dv))),
        "ident": np.eye(128, dtype=ml_dtypes.bfloat16),
        "identf": np.eye(128, dtype=np.float32),
        "trimask": f(np.triu(np.ones((128, 128))).astype(ml_dtypes.bfloat16)),
    }
    x = np.asarray(inputs["L0_attn_x"], np.float32)
    in_maps = []
    for core in range(8):
        m = dict(shared)
        m["x"] = f(x[core])
        in_maps.append(m)
    return in_maps


def kernel(**inputs):
    nc = _get_nc()
    in_maps = make_core_inputs(inputs)
    res = run_bass_kernel_spmd(nc, in_maps, core_ids=list(range(8)))
    return np.stack([r["out"] for r in res.results])
